# version 20
# baseline (speedup 1.0000x reference)
import math
import numpy as np
import ml_dtypes
import concourse.bass as bass
import concourse.mybir as mybir
from concourse.bass_utils import run_bass_kernel_spmd

F32 = mybir.dt.float32
BF16 = mybir.dt.bfloat16
AF = mybir.ActivationFunctionType
ALU = mybir.AluOpType
AX = mybir.AxisListType
BF16NP = ml_dtypes.bfloat16

D = 4096
TPC = 1024
NFC = 32
NH = 32
DH = 128
BS = 256
NB = 4
NG = 8
GH = 4
HEADS, LAYER_IDX, N_LAYERS = 32, 1, 32
EPS = 1e-5
NEG_BIG = -1e30

PHASE_LIMIT = 99
TAPS = ()


def _decay_tables():
    base = np.float32(1.0 / 2.0 ** (8.0 / HEADS))
    expo = np.arange(HEADS, dtype=np.float32) + np.float32(1.0)
    factor = np.float32(1.0 - LAYER_IDX / (N_LAYERS - 1 + 1e-05) + 1e-05)
    slope = (np.power(base, expo) * factor).astype(np.float32)
    r = np.arange(BS, dtype=np.float32) + 1.0
    qd = np.exp(-slope[:, None] * r[None, :]).astype(np.float32)
    kd = np.exp(-slope[:, None] * (np.float32(BS) - r)[None, :]).astype(np.float32)
    diff = (r[:, None] - r[None, :]).astype(np.float32)
    expo2 = -slope[:, None, None] * np.maximum(diff, 0.0)[None]
    dd = np.where(diff[None] >= 0, np.exp(expo2), np.float32(0.0)).astype(np.float32)
    ddT = np.ascontiguousarray(np.transpose(dd, (0, 2, 1)))
    bd = np.exp(-slope * np.float32(BS)).astype(np.float32)
    return slope, qd, kd, ddT, bd


class Sem:
    def __init__(self, nc, name):
        self._cm = nc.semaphore(name)
        self.h = self._cm.__enter__()
        self.n = 0

    def inc(self, k=1):
        self.n += k
        return self.n


class Ring:
    def __init__(self, n):
        self.n = n
        self.free = [None] * n
        self.i = 0

    def acquire(self):
        idx = self.i % self.n
        self.i += 1
        cond = self.free[idx]
        self.free[idx] = None
        return idx, cond

    def release(self, idx, sem, target):
        self.free[idx] = [(sem, target)]

    def release_multi(self, idx, conds):
        self.free[idx] = list(conds)


def build_nc(phase_limit=99, taps=()):
    nc = bass.Bass("TRN2", target_bir_lowering=False, debug=False)
    slope, qd_t, kd_t, ddT_t, bd_t = _decay_tables()
    bd = [float(x) for x in bd_t]

    xT = nc.dram_tensor("xT", [NFC, 128, TPC], F32, kind="ExternalInput").ap()
    wqk = nc.dram_tensor("wqk", [NH, 2, 128, 4096], F32, kind="ExternalInput").ap()
    wv = nc.dram_tensor("wv", [NG, 8, 128, 2048], F32, kind="ExternalInput").ap()
    wg = nc.dram_tensor("wg", [NFC, 128, 4096], F32, kind="ExternalInput").ap()
    wo = nc.dram_tensor("wo", [NFC, 128, 4096], F32, kind="ExternalInput").ap()
    wr = nc.dram_tensor("wr", [128, 2048], F32, kind="ExternalInput").ap()
    ebx = nc.dram_tensor("ebx", [64, 4096], F32, kind="ExternalInput").ap()
    ebxTd = nc.dram_tensor("ebxT", [128, 2048], F32, kind="ExternalInput").ap()
    nwc = nc.dram_tensor("nwc", [128, 32], F32, kind="ExternalInput").ap()
    rbc = nc.dram_tensor("rbc", [64, 1], F32, kind="ExternalInput").ap()
    qdb = nc.dram_tensor("qdb", [NH, 128, 256], BF16, kind="ExternalInput").ap()
    ddtb = nc.dram_tensor("ddtb", [NH, 2, 128, 256], BF16, kind="ExternalInput").ap()
    kdc = nc.dram_tensor("kdc", [128, 64], F32, kind="ExternalInput").ap()
    rmask = nc.dram_tensor("rmask", [128, 1], F32, kind="ExternalInput").ap()
    yT = nc.dram_tensor("yT", [NFC, 128, TPC], F32, kind="ExternalOutput").ap()

    def park(name):
        kind = "ExternalOutput" if name in taps else "Internal"
        return nc.dram_tensor(name, [NFC, 128, TPC], F32, kind=kind).ap()

    ao_park = park("tap_attn")
    ao2_park = park("tap_attnout")
    x1_park = park("tap_x1")
    m1_park = nc.dram_tensor("m1_park", [1, 1024], F32, kind="Internal")
    cci = [nc.dram_tensor(f"cci{g}", [128, 512], F32, kind="Internal") for g in range(NG)]
    cco = [nc.dram_tensor(f"cco{g}", [2, 128, 512], F32, kind="Internal") for g in range(NG)]
    tap_misc = nc.dram_tensor("tap_misc", [128, 4096], F32,
                              kind="ExternalOutput" if "tap_misc" in taps else "Internal").ap()

    def tapb(name, shape):
        if name not in taps:
            return None
        return nc.dram_tensor(name, shape, BF16, kind="ExternalOutput").ap()

    tap_h1 = tapb("tap_h1", [128, 32768])
    tap_gated = tapb("tap_gated", [128, 32768])
    tap_h2 = tapb("tap_h2", [128, 32768])
    tap_kt = tapb("tap_kt", [128, 8192])
    tap_ffn = nc.dram_tensor("tap_ffn", [2, 128, 16384], F32,
                             kind="ExternalOutput").ap() if "tap_ffn" in taps else None

    ENGS = ("sync", "tensor", "vector", "scalar", "gpsimd")
    prog = {e: [] for e in ENGS}

    import contextlib
    stack = contextlib.ExitStack()

    def sb(name, shape, dtype):
        return stack.enter_context(nc.sbuf_tensor(name, shape, dtype))

    def ps_(name, shape, dtype=F32):
        return stack.enter_context(nc.psum_tensor(name, shape, dtype))

    ar1 = sb("ar1", [128, 32768], BF16)
    ar2 = sb("ar2", [128, 32768], BF16)
    land = sb("land", [128, 4096], F32)
    wb = sb("wb", [128, 8192], BF16)
    mbc = sb("mbc", [128, 1024], F32)
    sbc = sb("sbc", [128, 1024], F32)
    aost = sb("aost", [128, 2048], F32)
    sqs = sb("sqs", [128, 1024], BF16)
    wsumcol = sb("wsumcol", [64, 1], F32)
    kvrun = sb("kvrun", [128, 512], F32)
    rows = sb("rows", [1, 5120], F32)
    rowb = sb("rowb", [1, 2048], BF16)
    rowb2 = sb("rowb2", [1, 1024], BF16)
    onc = sb("onc", [128, 1], BF16)
    onr = sb("onr", [1, 128], BF16)
    idb = sb("idb", [128, 128], BF16)
    id32 = sb("id32", [128, 128], F32)
    nw = sb("nw", [128, 32], F32)
    rb64 = sb("rb64", [64, 1], F32)
    kd = sb("kd", [128, 64], F32)
    msk = sb("msk", [128, 1], F32)
    ebb = wb[0:64, 0:4096]
    kv0 = ar2[:, 24960:25984].bitcast(F32)
    sqs2 = ar1[:, 26624:27648]
    sqs2b = land[:].bitcast(BF16)[0:128, 4096:5120]
    wdT = kvrun[:].bitcast(BF16)[0:64, 0:1024]
    sb8 = ar2[:].bitcast(F32)[0:8, 2048:2176]
    ebxT_b = ar2[:, 8192:10240]
    gb = wb[0:64, 4096:4160]
    zb = land[:].bitcast(BF16)[0:64, 4096:5120]

    KT0 = 0
    VS0 = 8192
    KL0 = 16384
    QC0 = 19456
    QS0 = 20480
    SC0 = 21504
    KP0 = 22528
    DD0 = 23040
    QD0 = 24064
    KV2 = 24576
    ZKV = 24832
    QC1 = 25984
    QS1 = 27008

    def kt_ap(slab, hh):
        o = KT0 + slab * 4096 + hh * 1024
        return ar2[:, o:o + 1024]

    def vs_ap(slab):
        return ar2[:, VS0 + slab * 4096: VS0 + (slab + 1) * 4096]

    def kvloc_ap(slab, hh, b):
        o = KL0 + slab * 1536 + hh * 384 + b * 128
        return ar2[:, o:o + 128]

    def gated_ap(fc):
        return ar2[:, fc * 1024:(fc + 1) * 1024]

    def h1_ap(fc):
        return ar1[:, fc * 1024:(fc + 1) * 1024]

    h2_ap = h1_ap
    ar1_f32 = ar1[:].bitcast(F32)
    ar2_f32 = ar2[:].bitcast(F32)

    def ffn_ap(fc):
        if fc < 16:
            return ar1_f32[:, fc * 1024:(fc + 1) * 1024]
        return ar2_f32[:, (fc - 16) * 1024:(fc - 15) * 1024]

    lgsb = ar1_f32[0:64, 0:1024]
    lgT = ar1_f32[:, 1024:1536]
    esc = ar1_f32[:, 1536:2048]
    wrk = ar1_f32[:, 2048:2560]
    wdv = ar1_f32[:, 2560:3072]
    smx = ar1_f32[:, 3072:3104]
    x0r_ap = [ar2_f32[:, 0:1024], ar2_f32[:, 1024:2048]]
    x1s_ap = [ar2_f32[:, 2048:3072], ar2_f32[:, 3072:4096]]

    _ROWSLOT = {"RA": 0, "RB": 1, "RC": 2, "RD": 3, "RT1": 4, "RT2": 3}
    _ROWALIAS = {"m1": "RA", "s1": "RB", "m4": "RA", "s4": "RB", "m2": "RA", "s2": "RB",
                 "m3": "RA", "s3": "RB", "rsms": "RB", "ssq_attn": "RC", "wsum": "RC",
                 "A": "RD", "negB": "RT1", "tmsq": "RT1", "tm2": "RT2", "tvar": "RT1"}

    def row(name):
        o = _ROWSLOT[_ROWALIAS.get(name, name)] * 1024
        return rows[0:1, o:o + 1024]

    def rowx(k):
        return ar1_f32[0:1, 4096 + k * 1024: 5120 + k * 1024]

    PS = [ps_(f"ps{i}", [128, 512], F32) for i in range(8)]
    ps7b = PS[7][:].bitcast(BF16)

    s_wld = Sem(nc, "s_wld")
    s_gld = Sem(nc, "s_gld")
    s_te = Sem(nc, "s_te")
    s_ve = Sem(nc, "s_ve")
    s_se = Sem(nc, "s_se")
    s_ge = Sem(nc, "s_ge")
    s_cc = Sem(nc, "s_cc")
    s_bar = Sem(nc, "s_bar")
    s_out = Sem(nc, "s_out")
    s_land = [Sem(nc, f"s_land{i}") for i in range(2)]
    s_lh = [Sem(nc, f"s_lh{i}") for i in range(4)]
    s_ao = [Sem(nc, f"s_ao{i}") for i in range(2)]
    s_x0 = [Sem(nc, f"s_x0{i}") for i in range(2)]
    s_x1s = [Sem(nc, f"s_x1s{i}") for i in range(2)]
    s_qd = [Sem(nc, f"s_qd{i}") for i in range(2)]
    s_dd = [Sem(nc, f"s_dd{i}") for i in range(2)]
    s_kv0r = Sem(nc, "s_kv0r")
    s_cci = Sem(nc, "s_cci")

    CHAIN = {"vector": s_ve, "scalar": s_se, "gpsimd": s_ge}
    DMA_SEMS = {"sync": [s_wld, *s_land, *s_lh, *s_x0],
                "scalar": [*s_ao, *s_x1s],
                "gpsimd": [s_gld, *s_qd, *s_dd, s_kv0r, s_cci]}

    def on(e, fn):
        prog[e].append(fn)

    def W(e, sem, tgt):
        if tgt is None or tgt <= 0:
            return
        on(e, lambda eng, s=sem.h, t=tgt: eng.wait_ge(s, t))

    def Wl(e, conds):
        for sem, tgt in conds:
            W(e, sem, tgt)

    def op(e, fn):
        ch = CHAIN[e]
        prev = ch.n
        tgt = ch.inc()
        def emit(eng, f=fn, s=ch.h, p=prev):
            if p > 0:
                eng.wait_ge(s, p)
            f(eng).then_inc(s, 1)
        prog[e].append(emit)
        return tgt

    def te(fn, inc=False):
        if inc:
            tgt = s_te.inc()
            prog["tensor"].append(lambda eng, f=fn: f(eng).then_inc(s_te.h, 1))
            return tgt
        prog["tensor"].append(lambda eng, f=fn: f(eng))
        return None

    def dma(e, sem, out, in_):
        on(e, lambda eng, o=out, i=in_, s=sem.h: eng.dma_start(out=o, in_=i).then_inc(s, 16))
        return sem.inc(16)

    def barrier():
        for e, sems in DMA_SEMS.items():
            for s in sems:
                if s.n:
                    W(e, s, s.n)
        W("tensor", s_te, s_te.n)
        for e, ch in CHAIN.items():
            W(e, ch, ch.n)
        for e in ENGS:
            on(e, lambda eng, s=s_bar.h: eng.nop().then_inc(s, 1))
        s_bar.n += 5
        t = s_bar.n
        for e in ENGS:
            on(e, lambda eng, s=s_bar.h, t=t: eng.wait_ge(s, t))

    land_ring = Ring(2)
    aost_ring = Ring(2)
    wb_ring = Ring(4)

    def land_ap(i):
        return land[:, i * 2048:(i + 1) * 2048]

    def wb_ap(i):
        return wb[:, i * 2048:(i + 1) * 2048]

    def aost_ap(i):
        return aost[:, i * 1024:(i + 1) * 1024]

    def load_cast_ap(src_ap, dst_ap, ncols=2048, eng_name="scalar", wb_cond=None):
        ch = CHAIN[eng_name]
        li, cond = land_ring.acquire()
        if cond is not None:
            Wl("sync", cond)
        t_ld = dma("sync", s_land[li], land_ap(li)[:, 0:ncols], src_ap)
        if wb_cond is not None:
            Wl(eng_name, wb_cond)
        W(eng_name, s_land[li], t_ld)
        t_cast = op(eng_name, lambda eng, o=dst_ap[:, 0:ncols], i=land_ap(li)[:, 0:ncols]:
                    (eng.copy(out=o, in_=i) if hasattr(eng, "copy") else eng.tensor_copy(out=o, in_=i)))
        land_ring.release(li, ch, t_cast)
        return (ch, t_cast)

    def load_cast(src_ap, wb_idx, ncols=2048, eng_name="scalar", wb_cond=None):
        return load_cast_ap(src_ap, wb_ap(wb_idx), ncols=ncols, eng_name=eng_name, wb_cond=wb_cond)

    def row_rsqrt(dst_row, var_row):
        tv = op("vector", lambda eng, o=dst_row, i=var_row: eng.reciprocal(out=o, in_=i))
        W("scalar", s_ve, tv)
        return op("scalar", lambda eng, o=dst_row, i=dst_row: eng.sqrt(out=o, in_=i))

    def split_bcast(dst_bc, src_row, t_src_se=None, psA=0, psB=1, mp=128):
        if t_src_se is not None:
            W("vector", s_se, t_src_se)
        op("vector", lambda eng, o=rowb[0:1, 0:1024], i=src_row: eng.tensor_copy(out=o, in_=i))
        op("vector", lambda eng, o=row("tmsq"), a=src_row, b=rowb[0:1, 0:1024]:
            eng.tensor_tensor(out=o, in0=a, in1=b, op=ALU.subtract))
        t_rows = op("vector", lambda eng, o=rowb[0:1, 1024:2048], i=row("tmsq"):
                    eng.tensor_copy(out=o, in_=i))
        W("tensor", s_ve, t_rows)
        for bank, p in ((0, psA), (1, psB)):
            te(lambda eng, o=PS[p][0:mp, 0:512], hi=rowb[0:1, bank * 512:(bank + 1) * 512]:
                eng.matmul(o, onr[0:1, 0:mp], hi, start=True, stop=False))
            last = bank == 1
            t_mm = te(lambda eng, o=PS[p][0:mp, 0:512],
                      lo=rowb[0:1, 1024 + bank * 512:1024 + (bank + 1) * 512]:
                      eng.matmul(o, onr[0:1, 0:mp], lo, start=False, stop=True), inc=last)
        W("vector", s_te, t_mm)
        for bank, p in ((0, psA), (1, psB)):
            t_bc = op("vector", lambda eng, o=dst_bc[0:mp, bank * 512:(bank + 1) * 512],
                      i=PS[p][0:mp, 0:512]: eng.tensor_copy(out=o, in_=i))
        return t_bc

    def finalize_ln_rows(m_row, s_row, ps_sum=(4, 5), ps_ssq=(6, 7), t_mm=None, rms_only=False,
                         ssq_row_src=None, skip_m=False):
        if t_mm is not None:
            W("vector", s_te, t_mm)
        c = 1.0 / D
        if not rms_only:
            if not skip_m:
                for bank, p in zip((0, 1), ps_sum):
                    op("vector", lambda eng, o=m_row[0:1, bank * 512:(bank + 1) * 512],
                        i=PS[p][0:1, 0:512], cc=c:
                        eng.tensor_scalar(out=o, in0=i, scalar1=cc, scalar2=None, op0=ALU.mult))
            for bank, p in zip((0, 1), ps_ssq):
                op("vector", lambda eng, o=row("tmsq")[0:1, bank * 512:(bank + 1) * 512].squeeze().unsqueeze(0) if False else rows[0:1, _ROWSLOT["RT1"] * 1024 + bank * 512: _ROWSLOT["RT1"] * 1024 + (bank + 1) * 512],
                    i=PS[p][0:1, 0:512], cc=c:
                    eng.tensor_scalar(out=o, in0=i, scalar1=cc, scalar2=None, op0=ALU.mult))
            op("vector", lambda eng, o=row("tm2"), a=m_row, b=m_row:
                eng.tensor_tensor(out=o, in0=a, in1=b, op=ALU.mult))
            t_ps = op("vector", lambda eng, o=row("tvar"), a=row("tmsq"), b=row("tm2"):
                      eng.tensor_tensor(out=o, in0=a, in1=b, op=ALU.subtract))
        else:
            t_ps = op("vector", lambda eng, o=row("tvar"), i=ssq_row_src, cc=c:
                      eng.tensor_scalar(out=o, in0=i, scalar1=cc, scalar2=None, op0=ALU.mult))
        op("vector", lambda eng, o=row("tvar"), i=row("tvar"), e=EPS:
            eng.tensor_scalar(out=o, in0=i, scalar1=e, scalar2=None, op0=ALU.add))
        return row_rsqrt(s_row, row("tvar")), t_ps

    dma("sync", s_wld, nw[:], nwc)
    dma("sync", s_wld, rb64[:], rbc)
    dma("sync", s_wld, kd[:], kdc)
    dma("sync", s_wld, msk[:], rmask)
    op("vector", lambda eng: eng.memset(onc[:], 1.0))
    op("vector", lambda eng: eng.memset(onr[:], 1.0))
    op("vector", lambda eng: eng.memset(ar2[:, ZKV:ZKV + 128], 0.0))
    op("vector", lambda eng: eng.memset(idb[:], 0.0))
    op("vector", lambda eng: eng.memset(id32[:], 0.0))
    t_ms = op("vector", lambda eng: eng.memset(row("ssq_attn"), 0.0))
    W("gpsimd", s_ve, t_ms)
    for t_ in (idb, id32):
        op("gpsimd", lambda eng, x=t_: eng.affine_select(
            out=x[:], in_=x[:], compare_op=ALU.not_equal, fill=1.0, base=0,
            pattern=[[-1, 128]], channel_multiplier=1))
    barrier()

    if phase_limit >= 0:
        half_ring = Ring(4)

        def landh_ap(i):
            return land[:, i * 1024:(i + 1) * 1024]

        t_fc = None
        for fc in range(NFC):
            hi, cond = half_ring.acquire()
            if cond is not None:
                Wl("sync", cond)
            t_ld = dma("sync", s_lh[hi], landh_ap(hi), xT[fc])
            W("scalar", s_lh[hi], t_ld)
            t_cp = op("scalar", lambda eng, o=h1_ap(fc), i=landh_ap(hi): eng.copy(out=o, in_=i))
            half_ring.release(hi, s_se, t_cp)
            if t_fc is not None:
                W("scalar", s_te, t_fc)
            t_sq = op("scalar", lambda eng, o=sqs[:], i=h1_ap(fc):
                      eng.activation(out=o, in_=i, func=AF.Square))
            st, sp = fc == 0, fc == NFC - 1
            W("tensor", s_se, t_cp)
            for bank, p in ((0, 4), (1, 5)):
                te(lambda eng, o=PS[p][0:1, 0:512], r=h1_ap(fc)[:, bank * 512:(bank + 1) * 512],
                    st=st, sp=sp: eng.matmul(o, onc[:], r, start=st, stop=sp))
            W("tensor", s_se, t_sq)
            for bank, p in ((0, 6), (1, 7)):
                t_fc = te(lambda eng, o=PS[p][0:1, 0:512], r=sqs[:, bank * 512:(bank + 1) * 512],
                          st=st, sp=sp: eng.matmul(o, onc[:], r, start=st, stop=sp), inc=bank == 1)
        t_stats = t_fc
        (t_s1, _) = finalize_ln_rows(row("m1"), row("s1"), t_mm=t_stats)
        split_bcast(mbc, row("m1"))
        t_sb = split_bcast(sbc, row("s1"), t_src_se=t_s1)
        norm_t = []
        for fc in range(NFC):
            ai, cond = aost_ring.acquire()
            if cond is not None:
                Wl("vector", cond)
            op("vector", lambda eng, o=aost_ap(ai), a=h1_ap(fc), b=mbc[:]:
                eng.tensor_tensor(out=o, in0=a, in1=b, op=ALU.subtract))
            t_n = op("vector", lambda eng, o=h1_ap(fc), a=aost_ap(ai), b=sbc[:]:
                     eng.tensor_tensor(out=o, in0=a, in1=b, op=ALU.mult))
            norm_t.append(t_n)
            aost_ring.release(ai, s_ve, t_n)
        W("sync", s_ve, t_sb)
        dma("sync", s_wld, m1_park.ap(), row("m1"))
        t_cp_last_ph0 = s_se.n
        t_sb_ph0 = t_sb
        t_stats_ph0 = t_stats
        t_norm_last = norm_t[-1]
        if tap_h1 is not None:
            W("sync", s_ve, t_norm_last)
            dma("sync", s_out, tap_h1, ar1[:])
            on("sync", lambda eng, t=s_out.n: eng.wait_ge(s_out.h, t))

    if phase_limit >= 1:
        t_ps_free = {i: None for i in range(8)}
        if phase_limit >= 0:
            for i in (0, 1, 4, 5, 6, 7):
                t_ps_free[i] = (s_ve, t_sb_ph0)
            land_ring.free[0] = [(s_se, t_cp_last_ph0)]
            land_ring.free[1] = [(s_se, t_cp_last_ph0)]

        def te_wait_ps(i):
            if t_ps_free[i] is not None:
                W("tensor", *t_ps_free[i])
                t_ps_free[i] = None

        kt_ready = {}
        vs_ready = {}
        t_rw_prev = [t_stats_ph0 if phase_limit >= 0 else None]
        qs_guard = {0: None, 1: None}
        kvrun_free = [None]
        stageq = []
        XW_APS = [ar2[:, 28032:30080], ar2[:, 30080:32128]]
        pend_extra = [None]
        extra_free = [None]

        def pop_stage():
            if stageq:
                stageq.pop(0)()
        kv0_free = [None]
        qd_free = {0: None, 1: None}
        dd_free = {0: None, 1: None}

        def proj_head(waps, t_casts, dst_evac, pp, fc_waits=None):
            for tc_ in t_casts:
                W("tensor", *tc_)
            te_wait_ps(pp)
            te_wait_ps(pp + 1)
            t_mm = None
            for fc in range(NFC):
                if fc_waits is not None:
                    W("tensor", s_ve, fc_waits[fc])
                lt = waps[fc // 16][:, (fc % 16) * 128:(fc % 16 + 1) * 128]
                st, sp = fc == 0, fc == NFC - 1
                for bank in (0, 1):
                    t_mm = te(lambda eng, o=PS[pp + bank][0:128, 0:512], l=lt,
                              r=h1_ap(fc)[:, bank * 512:(bank + 1) * 512], st=st, sp=sp:
                              eng.matmul(o, l, r, start=st, stop=sp),
                              inc=(sp and bank == 1))
                if fc % 4 == 3:
                    pop_stage()
            W("scalar", s_te, t_mm)
            for bank in (0, 1):
                t_ev = op("scalar", lambda eng, o=dst_evac[:, bank * 512:(bank + 1) * 512],
                          i=PS[pp + bank][0:128, 0:512]:
                          eng.activation(out=o, in_=i, func=AF.Silu))
            t_ps_free[pp] = (s_se, t_ev)
            t_ps_free[pp + 1] = (s_se, t_ev)
            return t_ev, t_mm

        for g in range(NG + 1):
            slab = g % 2
            if g < NG:
                def load_pair(src2):
                    slots, tcs = [], []
                    for halfw in range(2):
                        wi, cond = wb_ring.acquire()
                        if cond is not None:
                            Wl("sync", cond)
                        tcs.append(load_cast(src2.rearrange("p (a c) -> a p c", a=2)[halfw], wi,
                                             wb_cond=cond))
                        slots.append(wi)
                    return slots, tcs

                if pend_extra[0] is not None:
                    pend = pend_extra[0]
                    pend_extra[0] = None
                    extra_used = True
                else:
                    pend = load_pair(wqk[GH * g, 1])
                    extra_used = False
                for hh in range(GH):
                    h = GH * g + hh
                    slots, tcs = pend
                    if hh + 1 < GH:
                        pend = load_pair(wqk[h + 1, 1])
                    aps = [XW_APS[0], XW_APS[1]] if (hh == 0 and extra_used) else [wb_ap(wi) for wi in slots]
                    t_ev, t_kmm = proj_head(aps, tcs, kt_ap(slab, hh), pp=(0 if hh % 2 == 0 else 2),
                                            fc_waits=(norm_t if (g == 0 and hh == 0 and phase_limit >= 0) else None))
                    kt_ready[(slab, hh)] = t_ev
                    if hh == 0 and extra_used:
                        extra_free[0] = (s_te, t_kmm)
                    else:
                        for wi in slots:
                            wb_ring.release(wi, s_te, t_kmm)
                for i in range(8):
                    te_wait_ps(i)
                vslots, vtcs = {}, {}
                for chn in range(2):
                    wi, cond = wb_ring.acquire()
                    if cond is not None:
                        Wl("sync", cond)
                    vtcs[chn] = load_cast(wv[g, chn], wi, wb_cond=cond)
                    vslots[chn] = wi
                for fc in range(NFC):
                    chn = fc // 4
                    if fc % 4 == 0:
                        if chn + 2 < 8:
                            wi, cond = wb_ring.acquire()
                            if cond is not None:
                                Wl("sync", cond)
                            vtcs[chn + 2] = load_cast(wv[g, chn + 2], wi, wb_cond=cond)
                            vslots[chn + 2] = wi
                        W("tensor", *vtcs[chn])
                    sl = vslots[chn]
                    rt = wb_ap(sl)[:, (fc % 4) * 512:(fc % 4 + 1) * 512]
                    st, sp = fc == 0, fc == NFC - 1
                    for tcch in range(8):
                        o = PS[tcch][0:128, 0:512]
                        t_vmm = te(lambda eng, o=o, l=h1_ap(fc)[:, tcch * 128:(tcch + 1) * 128],
                                   r=rt, st=st, sp=sp: eng.matmul(o, l, r, start=st, stop=sp),
                                   inc=(fc % 4 == 3 and tcch == 7))
                    if fc % 4 == 3:
                        wb_ring.release(sl, s_te, t_vmm)
                W("scalar", s_te, t_vmm)
                for tcch in range(8):
                    t_vev = op("scalar", lambda eng,
                               o=vs_ap(slab)[:, tcch * 512: tcch * 512 + 512],
                               i=PS[tcch][0:128, 0:512]:
                               eng.activation(out=o, in_=i, func=AF.Silu))
                vs_ready[slab] = t_vev
                for i in range(8):
                    t_ps_free[i] = (s_se, t_vev)
                ps_trb = PS[0][:].bitcast(BF16)
                for hh in range(GH):
                    h = GH * g + hh
                    t_trfree = {i: None for i in range(4)}
                    t_cmm = {}
                    tkp = {}
                    W("tensor", s_se, kt_ready[(slab, hh)])
                    W("tensor", s_se, vs_ready[slab])
                    te_wait_ps(0)
                    for u in range(8):
                        bb, halfm = u // 2, u % 2
                        sl = u % 4
                        if t_trfree[sl] is not None:
                            W("tensor", *t_trfree[sl])
                            t_trfree[sl] = None
                        if u >= 4:
                            W("vector", s_te, t_cmm[u // 2 - 2])
                        t_tr = te(lambda eng, o=ps_trb[0:128, sl * 128:(sl + 1) * 128],
                                  i=kt_ap(slab, hh)[:, bb * 256 + halfm * 128: bb * 256 + halfm * 128 + 128]:
                                  eng.transpose(o, i, idb[:]), inc=True)
                        W("vector", s_te, t_tr)
                        tkp[u] = op("vector", lambda eng,
                                    o=ar2[:, KP0 + sl * 128: KP0 + (sl + 1) * 128],
                                    a=ps_trb[0:128, sl * 128:(sl + 1) * 128],
                                    b=kd[:, 2 * h + halfm:2 * h + halfm + 1].broadcast_to([128, 128]):
                                    eng.tensor_tensor(out=o, in0=a, in1=b, op=ALU.mult))
                        t_trfree[sl] = (s_ve, tkp[u])
                        if u % 2 == 1:
                            bb2 = u // 2
                            te_wait_ps(4 + bb2)
                            W("tensor", s_ve, tkp[u])
                            for halfm2 in range(2):
                                t_cmm[bb2] = te(lambda eng, o=PS[4 + bb2][0:128, 0:128],
                                                l=ar2[:, KP0 + ((2 * bb2 + halfm2) % 4) * 128: KP0 + ((2 * bb2 + halfm2) % 4) * 128 + 128],
                                                r=vs_ap(slab)[:, (2 * bb2 + halfm2) * 512 + hh * 128:(2 * bb2 + halfm2) * 512 + hh * 128 + 128],
                                                st=halfm2 == 0, sp=halfm2 == 1:
                                                eng.matmul(o, l, r, start=st, stop=sp), inc=halfm2 == 1)
                    if kvrun_free[0] is not None:
                        W("vector", *kvrun_free[0])
                        if hh == GH - 1:
                            kvrun_free[0] = None
                    W("vector", s_te, t_cmm[0])
                    op("vector", lambda eng, o=kvrun[:, hh * 128:(hh + 1) * 128], i=PS[4][0:128, 0:128]:
                        eng.tensor_copy(out=o, in_=i))
                    op("vector", lambda eng, o=kvloc_ap(slab, hh, 0), i=kvrun[:, hh * 128:(hh + 1) * 128]:
                        eng.tensor_copy(out=o, in_=i))
                    for bb in range(1, NB):
                        W("vector", s_te, t_cmm[bb])
                        t_up = op("vector", lambda eng, o=kvrun[:, hh * 128:(hh + 1) * 128],
                                  sc=bd[h], i1=PS[4 + bb][0:128, 0:128]:
                                  eng.scalar_tensor_tensor(out=o, in0=o, scalar=sc, in1=i1,
                                                           op0=ALU.mult, op1=ALU.add))
                        if bb < NB - 1:
                            op("vector", lambda eng, o=kvloc_ap(slab, hh, bb),
                                i=kvrun[:, hh * 128:(hh + 1) * 128]:
                                eng.tensor_copy(out=o, in_=i))
                    t_ps_free[0] = (s_ve, tkp[7])
                    for bb in range(NB):
                        t_ps_free[4 + bb] = (s_ve, t_up)
                t_states = op("vector", lambda eng: eng.engine_nop())
                W("gpsimd", s_ve, t_states)
                t_gld = dma("gpsimd", s_cci, cci[g].ap(), kvrun[:])
                kvrun_free[0] = (s_cci, t_gld)
                W("gpsimd", s_cci, t_gld)
                on("gpsimd", lambda eng, i=cci[g].ap(), o=cco[g].ap():
                    eng.collective_compute("AllGather", ALU.bypass,
                                           replica_groups=[[0, 1], [2, 3], [4, 5], [6, 7]],
                                           ins=[i], outs=[o]).then_inc(s_cc.h, 1))
                s_cc.inc()

            if g >= 1:
                gp = g - 1
                pslab = gp % 2
                while stageq:
                    stageq.pop(0)()
                W("gpsimd", s_cc, gp + 1)
                if kv0_free[0] is not None:
                    W("gpsimd", *kv0_free[0])
                    kv0_free[0] = None
                t_r = dma("gpsimd", s_kv0r, kv0[:, 0:512], cco[gp].ap()[0])
                W("vector", s_kv0r, t_r)
                t_kv0 = op("vector", lambda eng, o=kv0[:, 0:512],
                           b=msk[:].broadcast_to([128, 512]):
                           eng.tensor_tensor(out=o, in0=o, in1=b, op=ALU.mult))
                pendq = load_pair(wqk[GH * gp, 0])

                def qc_ap(qp):
                    o = QC0 if qp == 0 else QC1
                    return ar2[:, o:o + 1024]

                def qs_ap(qp):
                    o = QS0 if qp == 0 else QS1
                    return ar2[:, o:o + 1024]

                def build_stages(h, hh, hp, qp, t_qev, t_qs, pslab=None):
                    ctx = {}
                    t_sc_l, t_sct_l, t_kvb_l = {}, {}, {}

                    def emit_S(b):
                        t_kvb_l[b] = op("vector", lambda eng,
                                        o=ar2[:, KV2 + (b % 2) * 128: KV2 + (b % 2) * 128 + 128],
                                        i0=kv0[:, hh * 128: hh * 128 + 128],
                                        sc=float(bd_t[h] ** b),
                                        i1=(ar2[:, ZKV:ZKV + 128] if b == 0 else kvloc_ap(pslab, hh, b - 1)):
                                        eng.scalar_tensor_tensor(out=o, in0=i0, scalar=sc, in1=i1,
                                                                 op0=ALU.mult, op1=ALU.add))
                        scb = 4 if b % 2 == 0 else 6
                        te_wait_ps(scb)
                        if b == 0:
                            W("tensor", s_se, t_qev)
                            W("tensor", s_se, kt_ready[(pslab, hh)])
                            W("tensor", s_se, vs_ready[pslab])
                            W("tensor", s_ve, t_qs)
                        for halfm in range(2):
                            t_sc_l[b] = te(lambda eng, o=PS[scb][0:128, halfm * 256:halfm * 256 + 256],
                                           l=kt_ap(pslab, hh)[:, b * 256 + halfm * 128: b * 256 + halfm * 128 + 128],
                                           r=qc_ap(qp)[:, b * 256:(b + 1) * 256]:
                                           eng.matmul(o, l, r, start=True, stop=True), inc=halfm == 1)
                        if b == 0:
                            W("vector", s_dd[hp], ctx["t_dd"])
                        W("vector", s_te, t_sc_l[b])
                        t_sct_l[b] = op("vector", lambda eng,
                                        o=ar2[:, SC0 + (b % 2) * 512: SC0 + (b % 2) * 512 + 512],
                                        a=PS[scb][0:128, 0:512],
                                        b2=ar2[:, DD0 + hp * 512: DD0 + (hp + 1) * 512]:
                                        eng.tensor_tensor(out=o, in0=a, in1=b2, op=ALU.mult))
                        t_ps_free[scb] = (s_ve, t_sct_l[b])

                    def emit_O(b):
                        ob = 5 if b % 2 == 0 else 7
                        bp = b % 2
                        te_wait_ps(ob)
                        W("tensor", s_ve, t_sct_l[b])
                        W("tensor", s_ve, t_kvb_l[b])
                        for halfm in range(2):
                            te(lambda eng, o=PS[ob][0:128, 0:256],
                                l=vs_ap(pslab)[:, (2 * b + halfm) * 512 + hh * 128:(2 * b + halfm) * 512 + hh * 128 + 128],
                                r=ar2[:, SC0 + bp * 512 + halfm * 256: SC0 + bp * 512 + halfm * 256 + 256],
                                st=halfm == 0:
                                eng.matmul(o, l, r, start=st, stop=False))
                        t_o = te(lambda eng, o=PS[ob][0:128, 0:256],
                                 l=ar2[:, KV2 + (b % 2) * 128: KV2 + (b % 2) * 128 + 128],
                                 r=qs_ap(qp)[:, b * 256:(b + 1) * 256]:
                                 eng.matmul(o, l, r, start=False, stop=True), inc=True)
                        if b == NB - 1:
                            qs_guard[qp] = t_o
                        W("vector", s_te, t_o)
                        t_ev = op("vector", lambda eng, o=aost_ap(ctx["ai"])[:, b * 256:(b + 1) * 256],
                                  i=PS[ob][0:128, 0:256]: eng.tensor_copy(out=o, in_=i))
                        t_ps_free[ob] = (s_ve, t_ev)
                        ctx["t_ev"] = t_ev

                    def st_first():
                        ai, cond = aost_ring.acquire()
                        ctx["ai"] = ai
                        if cond is not None:
                            Wl("vector", cond)
                        emit_S(0)

                    def st_rms():
                        ai = ctx["ai"]
                        if t_rw_prev[0] is not None:
                            W("vector", s_te, t_rw_prev[0])
                            t_rw_prev[0] = None
                        t_sq = op("vector", lambda eng, o=sqs[:], a=aost_ap(ai), b=aost_ap(ai):
                                  eng.tensor_tensor(out=o, in0=a, in1=b, op=ALU.mult))
                        te_wait_ps(6)
                        te_wait_ps(7)
                        W("tensor", s_ve, t_sq)
                        for bank, p in ((0, 6), (1, 7)):
                            t_rw = te(lambda eng, o=PS[p][0:1, 0:512],
                                      r=sqs[:, bank * 512:(bank + 1) * 512]:
                                      eng.matmul(o, onc[:], r, start=True, stop=True), inc=bank == 1)
                        t_rw_prev[0] = t_rw
                        W("vector", s_te, t_rw)
                        for bank, p in ((0, 6), (1, 7)):
                            t_acc = op("vector", lambda eng,
                                       o=rows[0:1, _ROWSLOT["RC"] * 1024 + bank * 512: _ROWSLOT["RC"] * 1024 + (bank + 1) * 512],
                                       i=PS[p][0:1, 0:512]:
                                       eng.tensor_tensor(out=o, in0=o, in1=i, op=ALU.add))
                        t_ps_free[6] = (s_ve, t_acc)
                        t_ps_free[7] = (s_ve, t_acc)
                        qd_free[hp] = (s_ve, t_qs)
                        dd_free[hp] = (s_ve, t_acc)
                        kv0_free[0] = (s_ve, t_acc)
                        W("scalar", s_ve, ctx["t_ev"])
                        t_park = dma("scalar", s_ao[ai], ao_park[h], aost_ap(ai))
                        aost_ring.release(ai, s_ao[ai], t_park)

                    return ctx, [st_first,
                                 lambda: emit_S(1),
                                 lambda: (emit_O(0), emit_S(2)),
                                 lambda: (emit_O(1), emit_S(3)),
                                 lambda: emit_O(2),
                                 lambda: emit_O(3),
                                 st_rms]

                for hh in range(GH + 1):
                    if hh < GH:
                        h = GH * gp + hh
                        qslots, qtcs = pendq
                        if hh + 1 < GH:
                            pendq = load_pair(wqk[h + 1, 0])
                        hp = h % 2
                        qp = hh % 2
                        if qd_free[hp] is not None:
                            W("gpsimd", *qd_free[hp])
                            qd_free[hp] = None
                        t_qd = dma("gpsimd", s_qd[hp], ar2[:, QD0 + hp * 256: QD0 + (hp + 1) * 256], qdb[h])
                        if dd_free[hp] is not None:
                            W("gpsimd", *dd_free[hp])
                            dd_free[hp] = None
                        dma("gpsimd", s_dd[hp],
                            ar2[:, DD0 + hp * 512: DD0 + hp * 512 + 256], ddtb[h][0])
                        t_dd = dma("gpsimd", s_dd[hp],
                                   ar2[:, DD0 + hp * 512 + 256: DD0 + hp * 512 + 512], ddtb[h][1])
                        for tc_ in qtcs:
                            W("tensor", *tc_)
                        pp = 0 if hh % 2 == 0 else 2
                        te_wait_ps(pp)
                        te_wait_ps(pp + 1)
                        t_qmm = None
                        for fc in range(NFC):
                            sl = qslots[fc // 16]
                            lt = wb_ap(sl)[:, (fc % 16) * 128:(fc % 16 + 1) * 128]
                            st, sp = fc == 0, fc == NFC - 1
                            for bank in (0, 1):
                                t_qmm = te(lambda eng, o=PS[pp + bank][0:128, 0:512], l=lt,
                                           r=h1_ap(fc)[:, bank * 512:(bank + 1) * 512], st=st, sp=sp:
                                           eng.matmul(o, l, r, start=st, stop=sp),
                                           inc=(sp and bank == 1))
                            if fc % 4 == 3 and stageq:
                                stageq.pop(0)()
                        W("scalar", s_te, t_qmm)
                        for bank in (0, 1):
                            t_qev = op("scalar", lambda eng,
                                       o=qc_ap(qp)[:, bank * 512:(bank + 1) * 512],
                                       i=PS[pp + bank][0:128, 0:512]:
                                       eng.activation(out=o, in_=i, func=AF.Silu))
                        t_ps_free[pp] = (s_se, t_qev)
                        t_ps_free[pp + 1] = (s_se, t_qev)
                        for wi in qslots:
                            wb_ring.release(wi, s_te, t_qmm)
                        W("vector", s_se, t_qev)
                        W("vector", s_qd[hp], t_qd)
                        if qs_guard[qp] is not None:
                            W("vector", s_te, qs_guard[qp])
                            qs_guard[qp] = None
                        for bb in range(NB):
                            t_qs = op("vector", lambda eng,
                                      o=qs_ap(qp)[:, bb * 256:(bb + 1) * 256],
                                      a=qc_ap(qp)[:, bb * 256:(bb + 1) * 256],
                                      b2=ar2[:, QD0 + hp * 256: QD0 + (hp + 1) * 256]:
                                      eng.tensor_tensor(out=o, in0=a, in1=b2, op=ALU.mult))
                        ctx, stages = build_stages(h, hh, hp, qp, t_qev, t_qs, pslab=pslab)
                        ctx["t_dd"] = t_dd
                        stageq.extend(stages)
                        if hh == 2 and g + 1 < NG:
                            nh = GH * (g + 1)
                            xs = wqk[nh, 1].rearrange("p (a c) -> a p c", a=2)
                            cond_x = [extra_free[0]] if extra_free[0] is not None else None
                            extra_free[0] = None
                            tcs_x = []
                            for halfw in range(2):
                                tcs_x.append(load_cast_ap(xs[halfw], XW_APS[halfw], wb_cond=cond_x))
                            pend_extra[0] = (None, tcs_x)
                    else:
                        pass
        while stageq:
            stageq.pop(0)()
        barrier()
        if tap_kt is not None:
            dma("sync", s_out, tap_kt, ar2[:, KT0:KT0 + 8192])
            on("sync", lambda eng, t=s_out.n: eng.wait_ge(s_out.h, t))

    if phase_limit >= 3:
        (t_rs, _) = finalize_ln_rows(None, row("rsms"), rms_only=True, ssq_row_src=row("ssq_attn"))
        t_rsb = split_bcast(sbc, row("rsms"), t_src_se=t_rs)
        t_ps_free3 = {0: (s_ve, t_rsb), 1: (s_ve, t_rsb), 2: None, 3: None}
        t_gated_prev = [None]

        def load_pair3(src2):
            slots_, tcs_ = [], []
            for halfw in range(2):
                wi, cond = wb_ring.acquire()
                if cond is not None:
                    Wl("sync", cond)
                tcs_.append(load_cast(src2.rearrange("p (a c) -> a p c", a=2)[halfw], wi,
                                      wb_cond=cond))
                slots_.append(wi)
            return slots_, tcs_

        pend3 = load_pair3(wg[0])
        for oc in range(NFC):
            slots, tcs = pend3
            if oc + 1 < NFC:
                pend3 = load_pair3(wg[oc + 1])
            ai, cond = aost_ring.acquire()
            if cond is not None:
                Wl("sync", cond)
            t_ao = dma("sync", s_ao[ai], aost_ap(ai), ao_park[oc])
            pp = 0 if oc % 2 == 0 else 2
            for tc_ in tcs:
                W("tensor", *tc_)
            for i in (pp, pp + 1):
                if t_ps_free3[i] is not None:
                    W("tensor", *t_ps_free3[i])
                    t_ps_free3[i] = None
            t_mm = None
            for fc in range(NFC):
                sl = slots[fc // 16]
                lt = wb_ap(sl)[:, (fc % 16) * 128:(fc % 16 + 1) * 128]
                st, sp = fc == 0, fc == NFC - 1
                for bank in (0, 1):
                    t_mm = te(lambda eng, o=PS[pp + bank][0:128, 0:512], l=lt,
                              r=h1_ap(fc)[:, bank * 512:(bank + 1) * 512], st=st, sp=sp:
                              eng.matmul(o, l, r, start=st, stop=sp), inc=(sp and bank == 1))
            W("scalar", s_te, t_mm)
            if t_gated_prev[0] is not None:
                W("scalar", s_ve, t_gated_prev[0])
                t_gated_prev[0] = None
            for bank in (0, 1):
                t_sig = op("scalar", lambda eng, o=sqs[:, bank * 512:(bank + 1) * 512],
                           i=PS[pp + bank][0:128, 0:512]:
                           eng.activation(out=o, in_=i, func=AF.Sigmoid))
            for wi in slots:
                wb_ring.release(wi, s_te, t_mm)
            t_ps_free3[pp] = (s_se, t_sig)
            t_ps_free3[pp + 1] = (s_se, t_sig)
            W("vector", s_ao[ai], t_ao)
            if oc == 0:
                W("vector", s_ve, t_rsb)
            t_nrm = op("vector", lambda eng, o=mbc[:], i0=aost_ap(ai), sc=nw[:, oc:oc + 1], i1=sbc[:]:
                       eng.scalar_tensor_tensor(out=o, in0=i0, scalar=sc, in1=i1,
                                                op0=ALU.mult, op1=ALU.mult))
            aost_ring.release(ai, s_ve, t_nrm)
            W("vector", s_se, t_sig)
            t_gated_prev[0] = op("vector", lambda eng, o=gated_ap(oc), a=mbc[:], b=sqs[:]:
                eng.tensor_tensor(out=o, in0=a, in1=b, op=ALU.mult))
        barrier()
        if tap_gated is not None:
            dma("sync", s_out, tap_gated, ar2[:])
            on("sync", lambda eng, t=s_out.n: eng.wait_ge(s_out.h, t))

    if phase_limit >= 4:
        t_ps_free4 = {i: None for i in range(4)}
        t_stats_mm = [None]
        stats_pend = [None]
        flush4_queue = []

        def load_pair4(src2):
            slots_, tcs_ = [], []
            for halfw in range(2):
                wi, cond = wb_ring.acquire()
                if cond is not None:
                    Wl("sync", cond)
                tcs_.append(load_cast(src2.rearrange("p (a c) -> a p c", a=2)[halfw], wi,
                                      wb_cond=cond))
                slots_.append(wi)
            return slots_, tcs_

        pend4 = load_pair4(wo[0])
        for oc in range(NFC):
            slots, tcs = pend4
            if oc + 1 < NFC:
                pend4 = load_pair4(wo[oc + 1])
            pp = 0 if oc % 2 == 0 else 2
            for tc_ in tcs:
                W("tensor", *tc_)
            for i in (pp, pp + 1):
                if t_ps_free4[i] is not None:
                    W("tensor", *t_ps_free4[i])
                    t_ps_free4[i] = None
            t_mm = None
            for fc in range(NFC):
                sl = slots[fc // 16]
                lt = wb_ap(sl)[:, (fc % 16) * 128:(fc % 16 + 1) * 128]
                st, sp = fc == 0, fc == NFC - 1
                for bank in (0, 1):
                    t_mm = te(lambda eng, o=PS[pp + bank][0:128, 0:512], l=lt,
                              r=gated_ap(fc)[:, bank * 512:(bank + 1) * 512], st=st, sp=sp:
                              eng.matmul(o, l, r, start=st, stop=sp), inc=(sp and bank == 1))
            if flush4_queue:
                flush4_queue.pop(0)()
            for wi in slots:
                wb_ring.release(wi, s_te, t_mm)
            ai, cond = aost_ring.acquire()
            if cond is not None:
                Wl("vector", cond)
            W("vector", s_te, t_mm)
            for bank in (0, 1):
                t_ev = op("vector", lambda eng, o=aost_ap(ai)[:, bank * 512:(bank + 1) * 512],
                          i=PS[pp + bank][0:128, 0:512]: eng.tensor_copy(out=o, in_=i))
            t_ps_free4[pp] = (s_ve, t_ev)
            t_ps_free4[pp + 1] = (s_ve, t_ev)
            if t_stats_mm[0] is not None:
                W("vector", s_te, t_stats_mm[0])
            t_c4 = op("vector", lambda eng, o=sqs[:], i=aost_ap(ai): eng.tensor_copy(out=o, in_=i))
            t_q4 = op("vector", lambda eng, o=sqs2[:], a=aost_ap(ai), b=aost_ap(ai):
                      eng.tensor_tensor(out=o, in0=a, in1=b, op=ALU.mult))
            stats_pend[0] = (oc, t_c4, t_q4)
            W("scalar", s_ve, t_ev)
            t_park = dma("scalar", s_ao[ai], ao2_park[oc], aost_ap(ai))
            aost_ring.release(ai, s_ao[ai], t_park)

            def flush_stats4():
                oc_, t_c_, t_q_ = stats_pend[0]
                stats_pend[0] = None
                W("tensor", s_ve, t_q_)
                st_, sp_ = oc_ == 0, oc_ == NFC - 1
                for bank, p in ((0, 4), (1, 5)):
                    te(lambda eng, o=PS[p][0:1, 0:512], r=sqs[:, bank * 512:(bank + 1) * 512],
                        st=st_, sp=sp_: eng.matmul(o, onc[:], r, start=st_, stop=sp_))
                for bank, p in ((0, 6), (1, 7)):
                    t_stats_mm[0] = te(lambda eng, o=PS[p][0:1, 0:512],
                                       r=sqs2[:, bank * 512:(bank + 1) * 512],
                                       st=st_, sp=sp_: eng.matmul(o, onc[:], r, start=st_, stop=sp_),
                                       inc=bank == 1)
            if oc == NFC - 1:
                flush_stats4()
            else:
                flush4_queue.append(flush_stats4)
        barrier()

    if phase_limit >= 5:
        (t_s4, t_psr5) = finalize_ln_rows(row("m4"), row("s4"))
        split_bcast(mbc, row("m4"))
        t_sb4 = split_bcast(sbc, row("s4"), t_src_se=t_s4)
        wi6, cond = wb_ring.acquire()
        if cond is not None:
            Wl("sync", cond)
        t_wr = load_cast(wr, wi6, wb_cond=cond)
        (_, t_eT) = load_cast_ap(ebxTd, ebxT_b)
        W("tensor", s_se, t_eT)
        W("tensor", s_ve, t_sb4)
        for gfc in range(NFC):
            t_G = te(lambda eng, l=ebxT_b[:, gfc * 64:(gfc + 1) * 64],
                     st=gfc == 0, sp=gfc == NFC - 1:
                     eng.matmul(PS[0][0:64, 0:64], l, l, start=st, stop=sp),
                     inc=gfc == NFC - 1)
        x0_ring = Ring(2)
        x1_ring = Ring(2)
        t_m2b = None
        pend5 = {}

        def issue5(fc_):
            ai_, cond_ = aost_ring.acquire()
            if cond_ is not None:
                Wl("sync", cond_)
            t_ao_ = dma("sync", s_ao[ai_], aost_ap(ai_), ao2_park[fc_])
            xi_, cond_ = x0_ring.acquire()
            if cond_ is not None:
                Wl("sync", cond_)
            t_x0_ = dma("sync", s_x0[xi_], x0r_ap[xi_], xT[fc_])
            pend5[fc_] = (ai_, t_ao_, xi_, t_x0_)

        issue5(0)
        for fc in range(NFC):
            ai, t_ao, xi, t_x0 = pend5.pop(fc)
            if fc + 1 < NFC:
                issue5(fc + 1)
            si, cond = x1_ring.acquire()
            SPL = 832
            t_both = {}
            for en, c0, c1 in (("vector", 0, SPL), ("gpsimd", SPL, 1024)):
                if cond is not None:
                    Wl(en, cond)
                W(en, s_ao[ai], t_ao)
                W(en, s_x0[xi], t_x0)
                if fc <= 1:
                    W(en, s_ve, t_sb4)
                op(en, lambda eng, o=x1s_ap[si][:, c0:c1], a=aost_ap(ai)[:, c0:c1], b=mbc[:, c0:c1]:
                    eng.tensor_tensor(out=o, in0=a, in1=b, op=ALU.subtract))
                op(en, lambda eng, o=x1s_ap[si][:, c0:c1], a=x1s_ap[si][:, c0:c1], b=sbc[:, c0:c1]:
                    eng.tensor_tensor(out=o, in0=a, in1=b, op=ALU.mult))
                t_both[en] = op(en, lambda eng, o=x1s_ap[si][:, c0:c1], a=x1s_ap[si][:, c0:c1],
                                b=x0r_ap[xi][:, c0:c1]:
                                eng.tensor_tensor(out=o, in0=a, in1=b, op=ALU.add))
            t_x1 = t_both["vector"]
            t_x1g = t_both["gpsimd"]
            aost_ring.release_multi(ai, [(s_ve, t_x1), (s_ge, t_x1g)])
            x0_ring.release_multi(xi, [(s_ve, t_x1), (s_ge, t_x1g)])
            W("scalar", s_ve, t_x1)
            W("scalar", s_ge, t_x1g)
            if t_m2b is not None:
                W("scalar", s_te, t_m2b)
            if t_m2b is not None:
                W("scalar", s_te, t_m2b)
            t_c = op("scalar", lambda eng, o=sqs[:], i=x1s_ap[si]: eng.copy(out=o, in_=i))
            t_qb = op("scalar", lambda eng, o=sqs2[:], i=x1s_ap[si]:
                      eng.activation(out=o, in_=i, func=AF.Square))
            W("tensor", s_se, t_c)
            if fc == 0:
                W("tensor", s_ve, t_psr5)
                W("tensor", s_ve, t_sb4)
                W("tensor", *t_wr)
            st, sp = fc == 0, fc == NFC - 1
            for bank, p in ((0, 2), (1, 3)):
                te(lambda eng, o=PS[p][0:64, 0:512],
                    l=wb_ap(wi6)[:, fc * 64:(fc + 1) * 64],
                    r=sqs[:, bank * 512:(bank + 1) * 512], st=st, sp=sp:
                    eng.matmul(o, l, r, start=st, stop=sp))
            te(lambda eng, l=wb_ap(wi6)[:, fc * 64:(fc + 1) * 64], st=st, sp=sp:
                eng.matmul(PS[1][0:64, 0:1], l, onc[:], start=st, stop=sp))
            for bank, p in ((0, 4), (1, 5)):
                te(lambda eng, o=PS[p][0:64, 0:512],
                    l=ebxT_b[:, fc * 64:(fc + 1) * 64],
                    r=sqs[:, bank * 512:(bank + 1) * 512], st=st, sp=sp:
                    eng.matmul(o, l, r, start=st, stop=sp))
            W("tensor", s_se, t_qb)
            for bank, p in ((0, 6), (1, 7)):
                t_m2b = te(lambda eng, o=PS[p][0:1, 0:512], r=sqs2[:, bank * 512:(bank + 1) * 512],
                           st=st, sp=sp: eng.matmul(o, onc[:], r, start=st, stop=sp), inc=bank == 1)
            W("scalar", s_ve, t_x1)
            W("scalar", s_ge, t_x1g)
            t_park = dma("scalar", s_x1s[si], x1_park[fc], x1s_ap[si])
            x1_ring.release(si, s_x1s[si], t_park)
        wb_ring.release(wi6, s_te, t_m2b)
        barrier()

    if phase_limit >= 6:
        t_m1r = dma("sync", s_wld, row("m2"), m1_park.ap())
        W("vector", s_wld, t_m1r)
        for _ in range(2):
            li, cond = land_ring.acquire()
            if cond is not None:
                Wl("sync", cond)
        t_eb = dma("sync", s_land[0], land[0:64, 0:4096], ebx)
        W("scalar", s_land[0], t_eb)
        t_ebc = op("scalar", lambda eng, o=ebb, i=land[0:64, 0:4096]: eng.copy(out=o, in_=i))
        (t_s2, t_psr6) = finalize_ln_rows(row("m2"), row("s2"), skip_m=True)
        t_s2sq = op("scalar", lambda eng, o=rowx(4), i=row("s2"): eng.square(out=o, in_=i))
        W("vector", s_te, s_te.n)
        op("vector", lambda eng, o=wsumcol[:], i=PS[1][0:64, 0:1]: eng.tensor_copy(out=o, in_=i))
        t_wsc = op("vector", lambda eng, o=wsumcol[:], i=wsumcol[:]:
                   eng.tensor_scalar(out=o, in0=i, scalar1=-1.0, scalar2=None, op0=ALU.mult))
        for bank, p in ((0, 2), (1, 3)):
            op("vector", lambda eng, o=lgsb[:, bank * 512:(bank + 1) * 512], i=PS[p][0:64, 0:512]:
                eng.tensor_copy(out=o, in_=i))
        W("vector", s_se, t_s2)
        t_ms2 = op("vector", lambda eng, o=row("tm2"), a=row("m2"), b=row("s2"):
                   eng.tensor_tensor(out=o, in0=a, in1=b, op=ALU.mult))
        t_s2b = split_bcast(mbc, row("s2"), t_src_se=t_s2, mp=64, psA=2, psB=3)
        t_msb = split_bcast(sbc, row("tm2"), mp=64, psA=2, psB=3)
        W("scalar", s_te, t_G)
        t_gb = op("scalar", lambda eng, o=gb, i=PS[0][0:64, 0:64]: eng.copy(out=o, in_=i))
        op("vector", lambda eng, o=lgsb, a=lgsb, b=mbc[0:64, 0:1024]:
            eng.tensor_tensor(out=o, in0=a, in1=b, op=ALU.mult))
        op("vector", lambda eng, o=lgsb, i0=sbc[0:64, 0:1024], sc=wsumcol[:], i1=lgsb:
            eng.scalar_tensor_tensor(out=o, in0=i0, scalar=sc, in1=i1, op0=ALU.mult, op1=ALU.add))
        t_lg = op("vector", lambda eng, o=lgsb, i=lgsb, sc=rb64[:]:
                  eng.tensor_scalar(out=o, in0=i, scalar1=sc, scalar2=None, op0=ALU.add))
        barrier()
        if tap_h2 is not None:
            dma("sync", s_out, tap_h2, ar1[:])
            on("sync", lambda eng, t=s_out.n: eng.wait_ge(s_out.h, t))

    if phase_limit >= 7:
        evac_tok = {0: None, 1: None}
        for tt in range(8):
            pb = tt % 2
            if evac_tok[pb] is not None:
                W("tensor", s_ve, evac_tok[pb])
            t_tr = te(lambda eng, o=PS[pb][0:128, 0:64], i=lgsb[:, tt * 128:(tt + 1) * 128]:
                      eng.transpose(o, i, id32[0:64, 0:64]), inc=True)
            W("vector", s_te, t_tr)
            t_e = op("vector", lambda eng, o=lgT[:, tt * 64:(tt + 1) * 64], i=PS[pb][0:128, 0:64]:
                     eng.tensor_copy(out=o, in_=i))
            evac_tok[pb] = t_e
        W("tensor", s_ve, t_e)
        lgT3 = lgT.rearrange("p (a b) -> p a b", a=8)
        esc3 = esc.rearrange("p (a b) -> p a b", a=8)
        wrk3 = wrk.rearrange("p (a b) -> p a b", a=8)
        nmax = smx[:, 0:8]
        Zr = smx[:, 8:16]
        acc = smx[:, 16:24]
        mx = smx[:, 24:32]
        op("vector", lambda eng, o=nmax, i=lgT3: eng.reduce_max(out=o, in_=i, axis=AX.X, negate=True))
        t_sub = op("vector", lambda eng, o=esc3, a=lgT3,
                   b=nmax.unsqueeze(2).broadcast_to([128, 8, 64]):
                   eng.tensor_tensor(out=o, in0=a, in1=b, op=ALU.add))
        W("scalar", s_ve, t_sub)
        t_exp = op("scalar", lambda eng, o=esc, i=esc: eng.activation(out=o, in_=i, func=AF.Exp))
        W("vector", s_se, t_exp)
        op("vector", lambda eng, o=Zr, i=esc3: eng.reduce_sum(out=o, in_=i, axis=AX.X))
        op("vector", lambda eng, o=wrk, i=esc: eng.tensor_copy(out=o, in_=i))
        op("vector", lambda eng, o=acc: eng.memset(o, 0.0))
        sbc3 = sbc[:, 0:512].rearrange("p (a b) -> p a b", a=8)
        wdv3 = wdv.rearrange("p (a b) -> p a b", a=8)
        for rnd in range(8):
            op("vector", lambda eng, o=mx, i=wrk3: eng.reduce_max(out=o, in_=i, axis=AX.X))
            op("vector", lambda eng, o=mx, i=mx: eng.tensor_scalar(out=o, in0=i, scalar1=0.0,
                                                                   scalar2=None, op0=ALU.max))
            op("vector", lambda eng, o=acc, a=acc, b=mx: eng.tensor_tensor(out=o, in0=a, in1=b, op=ALU.add))
            if rnd < 7:
                op("vector", lambda eng, o=sbc3, a=wrk3,
                    b=mx.unsqueeze(2).broadcast_to([128, 8, 64]):
                    eng.tensor_tensor(out=o, in0=a, in1=b, op=ALU.is_ge))
                op("vector", lambda eng, o=wrk, i0=sbc[:, 0:512], sc=NEG_BIG, i1=wrk:
                    eng.scalar_tensor_tensor(out=o, in0=i0, scalar=sc, in1=i1, op0=ALU.mult, op1=ALU.add))
        op("vector", lambda eng, o=wdv3, a=esc3,
            b=mx.unsqueeze(2).broadcast_to([128, 8, 64]):
            eng.tensor_tensor(out=o, in0=a, in1=b, op=ALU.is_ge))
        op("vector", lambda eng, o=wdv, a=wdv, b=esc: eng.tensor_tensor(out=o, in0=a, in1=b, op=ALU.mult))
        op("vector", lambda eng, o=Zr, i=Zr: eng.reciprocal(out=o, in_=i))
        op("vector", lambda eng, o=wdv3, a=wdv3,
            b=Zr.unsqueeze(2).broadcast_to([128, 8, 64]):
            eng.tensor_tensor(out=o, in0=a, in1=b, op=ALU.mult))
        t_w = op("vector", lambda eng, o=acc, a=acc, b=Zr:
                 eng.tensor_tensor(out=o, in0=a, in1=b, op=ALU.mult))
        W("tensor", s_ve, t_w)
        t_tr = te(lambda eng, o=PS[1][0:8, 0:128], i=acc: eng.transpose(o, i, id32[:]), inc=True)
        W("vector", s_te, t_tr)
        t_s8 = op("vector", lambda eng, o=sb8, i=PS[1][0:8, 0:128]: eng.tensor_copy(out=o, in_=i))
        W("sync", s_ve, t_s8)
        t_wr_ = dma("sync", s_wld, row("wsum"), sb8)
        evac_tok = {0: None, 1: t_s8}
        for tt in range(8):
            pb = tt % 2
            if evac_tok[pb] is not None:
                W("tensor", s_ve, evac_tok[pb])
            t_tr = te(lambda eng, o=PS[pb][0:64, 0:128], i=wdv[:, tt * 64:(tt + 1) * 64]:
                      eng.transpose(o, i, id32[:]), inc=True)
            W("vector", s_te, t_tr)
            t_e = op("vector", lambda eng, o=wdT[:, tt * 128:(tt + 1) * 128], i=PS[pb][0:64, 0:128]:
                     eng.tensor_copy(out=o, in_=i))
            evac_tok[pb] = t_e
        W("tensor", s_ve, t_e)
        W("vector", s_wld, t_wr_)
        t_Arow = op("vector", lambda eng, o=row("A"), a=row("s2"), b=row("wsum"):
                    eng.tensor_tensor(out=o, in0=a, in1=b, op=ALU.mult))
        W("scalar", s_ve, t_Arow)
        op("scalar", lambda eng, o=rowx(5), i=row("A"): eng.square(out=o, in_=i))
        op("vector", lambda eng, o=rowx(0), a=row("m2"), b=row("A"):
            eng.tensor_tensor(out=o, in0=a, in1=b, op=ALU.mult))
        t_A = op("vector", lambda eng, o=rowx(0), i=rowx(0):
                 eng.tensor_scalar(out=o, in0=i, scalar1=-1.0, scalar2=None, op0=ALU.mult))
        if "tap_misc" in taps:
            W("sync", s_ve, t_A)
            dma("sync", s_wld, tap_misc[:, 0:512], lgT)
            dma("sync", s_wld, tap_misc[:, 512:1024], esc)
            dma("sync", s_wld, tap_misc[:, 1024:1536], wdv)
            dma("sync", s_wld, tap_misc[:, 1536:1568], smx)
            dma("sync", s_wld, tap_misc[:, 2048:3072], mbc[:])
        barrier()

    if phase_limit >= 8:
        W("tensor", s_ve, t_psr6)
        for gfc in range(NFC):
            t_ebmm = te(lambda eng, l=ebxT_b[:, gfc * 64:(gfc + 1) * 64],
                        st=gfc == 0, sp=gfc == NFC - 1:
                        eng.matmul(PS[6][0:64, 0:1], l, onc[:], start=st, stop=sp),
                        inc=gfc == NFC - 1)
        W("vector", s_te, t_ebmm)
        t_ebm = op("vector", lambda eng, o=sqs[0:64, 0:1], i=PS[6][0:64, 0:1]:
                   eng.tensor_copy(out=o, in_=i))
        W("tensor", s_ve, t_ebm)
        for bank, p in ((0, 6), (1, 7)):
            t_emx = te(lambda eng, o=PS[p][0:1, 0:512], l=sqs[0:64, 0:1],
                       r=wdT[:, bank * 512:(bank + 1) * 512]:
                       eng.matmul(o, l, r, start=True, stop=True), inc=bank == 1)
        W("vector", s_te, t_emx)
        for bank, p in ((0, 6), (1, 7)):
            t_m3r = op("vector", lambda eng, o=row("m3")[0:1, bank * 512:(bank + 1) * 512],
                       i=PS[p][0:1, 0:512], cc=1.0 / D:
                       eng.tensor_scalar(out=o, in0=i, scalar1=cc, scalar2=None, op0=ALU.mult))
        W("tensor", s_se, t_gb)
        for bank, p in ((0, 2), (1, 3)):
            t_Q = te(lambda eng, o=PS[p][0:64, 0:512], l=gb,
                     r=wdT[:, bank * 512:(bank + 1) * 512]:
                     eng.matmul(o, l, r, start=True, stop=True), inc=bank == 1)
        op("vector", lambda eng, o=rowx(2), a=row("A"), b=row("A"):
            eng.tensor_tensor(out=o, in0=a, in1=b, op=ALU.add))
        split_bcast(sbc, rowx(2), mp=64)
        W("vector", s_te, t_Q)
        for bank, p4, pq in ((0, 4, 2), (1, 5, 3)):
            op("vector", lambda eng, o=lgsb[:, bank * 512:(bank + 1) * 512],
                a=PS[p4][0:64, 0:512], b=sbc[0:64, bank * 512:(bank + 1) * 512]:
                eng.tensor_tensor(out=o, in0=a, in1=b, op=ALU.mult))
            op("vector", lambda eng, o=lgsb[:, bank * 512:(bank + 1) * 512],
                a=lgsb[:, bank * 512:(bank + 1) * 512], b=PS[pq][0:64, 0:512]:
                eng.tensor_tensor(out=o, in0=a, in1=b, op=ALU.add))
        t_z = op("vector", lambda eng, o=lgsb, a=lgsb, b=wdT:
                 eng.tensor_tensor(out=o, in0=a, in1=b, op=ALU.mult))
        W("scalar", s_ve, t_z)
        t_zb = op("scalar", lambda eng, o=zb, i=lgsb: eng.copy(out=o, in_=i))
        W("tensor", s_se, t_zb)
        for bank, p in ((0, 6), (1, 7)):
            t_ss = te(lambda eng, o=PS[p][0:1, 0:512], r=zb[:, bank * 512:(bank + 1) * 512]:
                      eng.matmul(o, onc[0:64, 0:1], r, start=True, stop=True), inc=bank == 1)
        W("scalar", s_ve, t_m3r)
        t_m3sq = op("scalar", lambda eng, o=rowx(6), i=row("m3"): eng.square(out=o, in_=i))
        op("vector", lambda eng, o=rowx(3), i=rowx(4): eng.reciprocal(out=o, in_=i))
        op("vector", lambda eng, o=rowx(3), i0=rowx(3), sc=-EPS, i1=rowx(5):
            eng.scalar_tensor_tensor(out=o, in0=i0, scalar=sc, in1=i1, op0=ALU.add, op1=ALU.mult))
        W("vector", s_te, t_ss)
        for bank, p in ((0, 6), (1, 7)):
            op("vector", lambda eng, o=rowx(3)[0:1, bank * 512:(bank + 1) * 512],
                i0=PS[p][0:1, 0:512], cc=1.0 / D,
                i1=rowx(3)[0:1, bank * 512:(bank + 1) * 512]:
                eng.scalar_tensor_tensor(out=o, in0=i0, scalar=cc, in1=i1,
                                         op0=ALU.mult, op1=ALU.add))
        op("vector", lambda eng, o=rowx(2), a=rowx(0), b=row("m3"):
            eng.tensor_tensor(out=o, in0=a, in1=b, op=ALU.mult))
        op("vector", lambda eng, o=rowx(3), i0=rowx(2), sc=2.0, i1=rowx(3):
            eng.scalar_tensor_tensor(out=o, in0=i0, scalar=sc, in1=i1, op0=ALU.mult, op1=ALU.add))
        W("vector", s_se, t_m3sq)
        op("vector", lambda eng, o=rowx(3), a=rowx(3), b=rowx(6):
            eng.tensor_tensor(out=o, in0=a, in1=b, op=ALU.subtract))
        op("vector", lambda eng, o=rowx(3), i=rowx(3), e=EPS:
            eng.tensor_scalar(out=o, in0=i, scalar1=e, scalar2=None, op0=ALU.add))
        t_s3 = row_rsqrt(row("s3"), rowx(3))
        op("vector", lambda eng, o=rowx(2), a=rowx(0), b=row("m3"):
            eng.tensor_tensor(out=o, in0=a, in1=b, op=ALU.subtract))
        W("vector", s_se, t_s3)
        op("vector", lambda eng, o=rowx(2), a=rowx(2), b=row("s3"):
            eng.tensor_tensor(out=o, in0=a, in1=b, op=ALU.mult))
        for bank in (0, 1):
            t_rb2 = op("vector", lambda eng, o=rowb2[0:1, bank * 512:(bank + 1) * 512],
                       i=rowx(2)[0:1, bank * 512:(bank + 1) * 512]:
                       eng.tensor_copy(out=o, in_=i))
        split_bcast(sbc, row("s3"), mp=64)
        t_wdm = op("vector", lambda eng, o=wdT, a=wdT, b=sbc[0:64, 0:1024]:
                   eng.tensor_tensor(out=o, in0=a, in1=b, op=ALU.mult))
        op("vector", lambda eng, o=rowx(2), a=row("A"), b=row("s3"):
            eng.tensor_tensor(out=o, in0=a, in1=b, op=ALU.mult))
        op("vector", lambda eng, o=rowx(2), i=rowx(2), e=1.0:
            eng.tensor_scalar(out=o, in0=i, scalar1=e, scalar2=None, op0=ALU.add))
        t_A2 = split_bcast(mbc, rowx(2))
        x1_ring8 = Ring(2)
        t_ps_free8 = {i: None for i in range(4)}
        pend8 = {}

        def issue8(oc_):
            ai_, cond_ = x1_ring8.acquire()
            if cond_ is not None:
                Wl("sync", cond_)
            pend8[oc_] = (ai_, dma("sync", s_ao[ai_], aost_ap(ai_), x1_park[oc_]))

        issue8(0)
        for oc in range(NFC):
            ai, t_x1 = pend8.pop(oc)
            if oc + 1 < NFC:
                issue8(oc + 1)
            pp = 0 if oc % 2 == 0 else 2
            if oc == 0:
                W("tensor", s_se, t_ebc)
                W("tensor", s_ve, t_A2)
            for i in (pp, pp + 1):
                if t_ps_free8[i] is not None:
                    W("tensor", *t_ps_free8[i])
                    t_ps_free8[i] = None
            for bank in (0, 1):
                te(lambda eng, o=PS[pp + bank][0:128, 0:512],
                    l=ebb[:, oc * 128:(oc + 1) * 128],
                    r=wdT[:, bank * 512:(bank + 1) * 512]:
                    eng.matmul(o, l, r, start=True, stop=False))
            for bank in (0, 1):
                t_mm = te(lambda eng, o=PS[pp + bank][0:128, 0:512],
                          r=rowb2[0:1, bank * 512:(bank + 1) * 512]:
                          eng.matmul(o, onr[:], r, start=False, stop=True), inc=bank == 1)
            W("vector", s_ao[ai], t_x1)
            W("gpsimd", s_ao[ai], t_x1)
            if oc == 0:
                W("gpsimd", s_ve, t_A2)
            W("vector", s_te, t_mm)
            op("vector", lambda eng, o=ffn_ap(oc)[:, 0:704], a=aost_ap(ai)[:, 0:704],
                b=mbc[:, 0:704]:
                eng.tensor_tensor(out=o, in0=a, in1=b, op=ALU.mult))
            t_mg = op("gpsimd", lambda eng, o=ffn_ap(oc)[:, 704:1024], a=aost_ap(ai)[:, 704:1024],
                      b=mbc[:, 704:1024]:
                      eng.tensor_tensor(out=o, in0=a, in1=b, op=ALU.mult))
            op("vector", lambda eng, o=ffn_ap(oc)[:, 0:512],
                a=ffn_ap(oc)[:, 0:512], b=PS[pp][0:128, 0:512]:
                eng.tensor_tensor(out=o, in0=a, in1=b, op=ALU.add))
            W("vector", s_ge, t_mg)
            t_f = op("vector", lambda eng, o=ffn_ap(oc)[:, 512:1024],
                     a=ffn_ap(oc)[:, 512:1024], b=PS[pp + 1][0:128, 0:512]:
                     eng.tensor_tensor(out=o, in0=a, in1=b, op=ALU.add))
            x1_ring8.release_multi(ai, [(s_ve, t_f), (s_ge, t_mg)])
            t_ps_free8[pp] = (s_ve, t_f)
            t_ps_free8[pp + 1] = (s_ve, t_f)
            W("sync", s_ve, t_f)
            dma("sync", s_out, yT[oc], ffn_ap(oc))
        on("sync", lambda eng: eng.wait_ge(s_out.h, s_out.n))
    else:
        on("sync", lambda eng: eng.wait_ge(s_bar.h, s_bar.n))
        dma("sync", s_out, yT[0], mbc[:])
        on("sync", lambda eng: eng.wait_ge(s_out.h, s_out.n))

    for e, sems in DMA_SEMS.items():
        for s in sems:
            if s.n:
                W(e, s, s.n)

    with stack:
        with nc.Block() as block:
            @block.sync
            def _(eng):
                for f in prog["sync"]:
                    f(eng)

            @block.tensor
            def _(eng):
                for f in prog["tensor"]:
                    f(eng)

            @block.vector
            def _(eng):
                for f in prog["vector"]:
                    f(eng)

            @block.scalar
            def _(eng):
                for f in prog["scalar"]:
                    f(eng)

            @block.gpsimd
            def _(eng):
                for f in prog["gpsimd"]:
                    f(eng)

    return nc



def _prep_shared(qkv_w, out_w, gate_w, norm_w, rw, rb, eb):
    f32 = np.float32
    A1 = np.ascontiguousarray(qkv_w.reshape(32, 3, 128, 32, 128))
    wqk_ = np.ascontiguousarray(np.transpose(A1[:, 0:2], (0, 1, 4, 3, 2))).reshape(32, 2, 128, 4096)
    B = A1[:, 2]
    C2 = B.reshape(8, 4, 128, 8, 4, 128)
    wv_ = np.ascontiguousarray(np.transpose(C2, (0, 3, 5, 4, 1, 2))).reshape(8, 8, 128, 2048)
    G = gate_w.reshape(32, 128, 32, 128)
    wg_ = np.ascontiguousarray(np.transpose(G, (0, 3, 2, 1))).reshape(32, 128, 4096)
    O = out_w.reshape(32, 128, 32, 128)
    wo_ = np.ascontiguousarray(np.transpose(O, (0, 3, 2, 1))).reshape(32, 128, 4096)
    Dm = rw.reshape(64, 32, 128)
    wr_ = np.ascontiguousarray(np.transpose(Dm, (2, 1, 0))).reshape(128, 2048)
    nwc_ = np.ascontiguousarray(norm_w.reshape(32, 128).T)
    rbc_ = np.ascontiguousarray(rb.reshape(64, 1))
    slope, qd, kdv, ddT, bdv = _decay_tables()
    qdb_ = np.ascontiguousarray(np.broadcast_to(qd[:, None, :], (32, 128, 256))).astype(BF16NP)
    ddtb_ = np.ascontiguousarray(ddT.reshape(32, 2, 128, 256)).astype(BF16NP)
    kdc_ = np.ascontiguousarray(kdv.reshape(32, 2, 128).transpose(2, 0, 1)).reshape(128, 64)
    kdc_ = np.ascontiguousarray(kdc_).astype(f32)
    ebxT_ = np.ascontiguousarray(eb.T.reshape(32, 128, 64).transpose(1, 0, 2)).reshape(128, 2048)
    return dict(wqk=wqk_.astype(f32), wv=wv_.astype(f32), wg=wg_.astype(f32), wo=wo_.astype(f32),
                wr=wr_.astype(f32), ebx=np.ascontiguousarray(eb).astype(f32),
                ebxT=ebxT_.astype(f32),
                nwc=nwc_.astype(f32), rbc=rbc_.astype(f32), qdb=qdb_, ddtb=ddtb_, kdc=kdc_)


def _prep_core_inputs(x_full, shared):
    maps = []
    for c in range(8):
        j, p = c // 2, c % 2
        xs = x_full[j, p * 1024:(p + 1) * 1024, :]
        xTt = np.ascontiguousarray(xs.T).reshape(32, 128, 1024).astype(np.float32)
        m = dict(shared)
        m["xT"] = xTt
        m["rmask"] = np.full((128, 1), 1.0 if p == 1 else 0.0, np.float32)
        maps.append(m)
    return maps


def kernel(L0_attn_hidden_states, L0_attn_qkv_w, L0_attn_out_w, L0_attn_gate_w,
           L0_attn_norm_w, L0_ffn_router_weight, L0_ffn_router_bias, L0_ffn_expert_bias):
    x = np.asarray(L0_attn_hidden_states, np.float32)
    shared = _prep_shared(np.asarray(L0_attn_qkv_w, np.float32),
                          np.asarray(L0_attn_out_w, np.float32),
                          np.asarray(L0_attn_gate_w, np.float32),
                          np.asarray(L0_attn_norm_w, np.float32),
                          np.asarray(L0_ffn_router_weight, np.float32),
                          np.asarray(L0_ffn_router_bias, np.float32),
                          np.asarray(L0_ffn_expert_bias, np.float32))
    in_maps = _prep_core_inputs(x, shared)
    nc = build_nc(PHASE_LIMIT, TAPS)
    res = run_bass_kernel_spmd(nc, in_maps, core_ids=list(range(8)))
    out = np.empty((4, 2048, 4096), np.float32)
    for c in range(8):
        j, p = c // 2, c % 2
        yt = res.results[c]["yT"].reshape(4096, 1024)
        out[j, p * 1024:(p + 1) * 1024, :] = yt.T
    return out


# version 21
# speedup vs baseline: 1.0794x; 1.0794x over previous
import math
import numpy as np
import ml_dtypes
import concourse.bass as bass
import concourse.mybir as mybir
from concourse.bass_utils import run_bass_kernel_spmd

F32 = mybir.dt.float32
BF16 = mybir.dt.bfloat16
AF = mybir.ActivationFunctionType
ALU = mybir.AluOpType
AX = mybir.AxisListType
BF16NP = ml_dtypes.bfloat16

D = 4096
TPC = 1024
NFC = 32
NH = 32
DH = 128
BS = 256
NB = 4
NG = 8
GH = 4
HEADS, LAYER_IDX, N_LAYERS = 32, 1, 32
EPS = 1e-5
NEG_BIG = -1e30

PHASE_LIMIT = 99
TAPS = ()


def _decay_tables():
    base = np.float32(1.0 / 2.0 ** (8.0 / HEADS))
    expo = np.arange(HEADS, dtype=np.float32) + np.float32(1.0)
    factor = np.float32(1.0 - LAYER_IDX / (N_LAYERS - 1 + 1e-05) + 1e-05)
    slope = (np.power(base, expo) * factor).astype(np.float32)
    r = np.arange(BS, dtype=np.float32) + 1.0
    qd = np.exp(-slope[:, None] * r[None, :]).astype(np.float32)
    kd = np.exp(-slope[:, None] * (np.float32(BS) - r)[None, :]).astype(np.float32)
    diff = (r[:, None] - r[None, :]).astype(np.float32)
    expo2 = -slope[:, None, None] * np.maximum(diff, 0.0)[None]
    dd = np.where(diff[None] >= 0, np.exp(expo2), np.float32(0.0)).astype(np.float32)
    ddT = np.ascontiguousarray(np.transpose(dd, (0, 2, 1)))
    bd = np.exp(-slope * np.float32(BS)).astype(np.float32)
    return slope, qd, kd, ddT, bd


class Sem:
    def __init__(self, nc, name):
        self._cm = nc.semaphore(name)
        self.h = self._cm.__enter__()
        self.n = 0

    def inc(self, k=1):
        self.n += k
        return self.n


class Ring:
    def __init__(self, n):
        self.n = n
        self.free = [None] * n
        self.i = 0

    def acquire(self):
        idx = self.i % self.n
        self.i += 1
        cond = self.free[idx]
        self.free[idx] = None
        return idx, cond

    def release(self, idx, sem, target):
        self.free[idx] = [(sem, target)]

    def release_multi(self, idx, conds):
        self.free[idx] = list(conds)


def build_nc(phase_limit=99, taps=()):
    nc = bass.Bass("TRN2", target_bir_lowering=False, debug=False)
    slope, qd_t, kd_t, ddT_t, bd_t = _decay_tables()
    bd = [float(x) for x in bd_t]

    xT = nc.dram_tensor("xT", [NFC, 128, TPC], F32, kind="ExternalInput").ap()
    wqk = nc.dram_tensor("wqk", [NH, 2, 128, 4096], F32, kind="ExternalInput").ap()
    wv = nc.dram_tensor("wv", [NG, 8, 128, 2048], F32, kind="ExternalInput").ap()
    wg = nc.dram_tensor("wg", [NFC, 128, 4096], F32, kind="ExternalInput").ap()
    wo = nc.dram_tensor("wo", [NFC, 128, 4096], F32, kind="ExternalInput").ap()
    wr = nc.dram_tensor("wr", [128, 2048], F32, kind="ExternalInput").ap()
    ebx = nc.dram_tensor("ebx", [64, 4096], F32, kind="ExternalInput").ap()
    ebxTd = nc.dram_tensor("ebxT", [128, 2048], F32, kind="ExternalInput").ap()
    nwc = nc.dram_tensor("nwc", [128, 32], F32, kind="ExternalInput").ap()
    rbc = nc.dram_tensor("rbc", [64, 1], F32, kind="ExternalInput").ap()
    qdb = nc.dram_tensor("qdb", [NH, 128, 256], BF16, kind="ExternalInput").ap()
    ddtb = nc.dram_tensor("ddtb", [NH, 2, 128, 256], BF16, kind="ExternalInput").ap()
    kdc = nc.dram_tensor("kdc", [128, 64], F32, kind="ExternalInput").ap()
    rmask = nc.dram_tensor("rmask", [128, 1], F32, kind="ExternalInput").ap()
    yT = nc.dram_tensor("yT", [NFC, 128, TPC], F32, kind="ExternalOutput").ap()

    def park(name):
        kind = "ExternalOutput" if name in taps else "Internal"
        return nc.dram_tensor(name, [NFC, 128, TPC], F32, kind=kind).ap()

    ao_park = park("tap_attn")
    ao2_park = park("tap_attnout")
    x1_park = park("tap_x1")
    m1_park = nc.dram_tensor("m1_park", [1, 1024], F32, kind="Internal")
    cci = [nc.dram_tensor(f"cci{g}", [128, 512], F32, kind="Internal") for g in range(NG)]
    cco = [nc.dram_tensor(f"cco{g}", [2, 128, 512], F32, kind="Internal") for g in range(NG)]
    tap_misc = nc.dram_tensor("tap_misc", [128, 4096], F32,
                              kind="ExternalOutput" if "tap_misc" in taps else "Internal").ap()

    def tapb(name, shape):
        if name not in taps:
            return None
        return nc.dram_tensor(name, shape, BF16, kind="ExternalOutput").ap()

    tap_h1 = tapb("tap_h1", [128, 32768])
    tap_gated = tapb("tap_gated", [128, 32768])
    tap_h2 = tapb("tap_h2", [128, 32768])
    tap_kt = tapb("tap_kt", [128, 8192])
    tap_ffn = nc.dram_tensor("tap_ffn", [2, 128, 16384], F32,
                             kind="ExternalOutput").ap() if "tap_ffn" in taps else None

    ENGS = ("sync", "tensor", "vector", "scalar", "gpsimd")
    prog = {e: [] for e in ENGS}

    import contextlib
    stack = contextlib.ExitStack()

    def sb(name, shape, dtype):
        return stack.enter_context(nc.sbuf_tensor(name, shape, dtype))

    def ps_(name, shape, dtype=F32):
        return stack.enter_context(nc.psum_tensor(name, shape, dtype))

    ar1 = sb("ar1", [128, 32768], BF16)
    ar2 = sb("ar2", [128, 32768], BF16)
    land = sb("land", [128, 4096], F32)
    wb = sb("wb", [128, 8192], BF16)
    mbc = sb("mbc", [128, 1024], F32)
    sbc = sb("sbc", [128, 1024], F32)
    aost = sb("aost", [128, 2048], F32)
    sqs = sb("sqs", [128, 1024], BF16)
    wsumcol = sb("wsumcol", [64, 1], F32)
    kvrun = sb("kvrun", [128, 512], F32)
    rows = sb("rows", [1, 5120], F32)
    rowb = sb("rowb", [1, 2048], BF16)
    rowb2 = sb("rowb2", [1, 1024], BF16)
    onc = sb("onc", [128, 1], BF16)
    onr = sb("onr", [1, 128], BF16)
    idb = sb("idb", [128, 128], BF16)
    id32 = sb("id32", [128, 128], F32)
    nw = sb("nw", [128, 32], F32)
    rb64 = sb("rb64", [64, 1], F32)
    kd = sb("kd", [128, 64], F32)
    msk = sb("msk", [128, 1], F32)
    ebb = wb[0:64, 0:4096]
    kv0 = ar2[:, 24960:25984].bitcast(F32)
    sqs2 = ar1[:, 26624:27648]
    sqs2b = land[:].bitcast(BF16)[0:128, 4096:5120]
    wdT = kvrun[:].bitcast(BF16)[0:64, 0:1024]
    sb8 = ar2[:].bitcast(F32)[0:8, 2048:2176]
    ebxT_b = ar2[:, 8192:10240]
    gb = wb[0:64, 4096:4160]
    zb = land[:].bitcast(BF16)[0:64, 4096:5120]

    KT0 = 0
    VS0 = 8192
    KL0 = 16384
    QC0 = 19456
    QS0 = 20480
    SC0 = 21504
    KP0 = 22528
    DD0 = 23040
    QD0 = 24064
    KV2 = 24576
    ZKV = 24832
    QC1 = 25984
    QS1 = 27008

    def kt_ap(slab, hh):
        o = KT0 + slab * 4096 + hh * 1024
        return ar2[:, o:o + 1024]

    def vs_ap(slab):
        return ar2[:, VS0 + slab * 4096: VS0 + (slab + 1) * 4096]

    def kvloc_ap(slab, hh, b):
        o = KL0 + slab * 1536 + hh * 384 + b * 128
        return ar2[:, o:o + 128]

    def gated_ap(fc):
        return ar2[:, fc * 1024:(fc + 1) * 1024]

    def h1_ap(fc):
        return ar1[:, fc * 1024:(fc + 1) * 1024]

    h2_ap = h1_ap
    ar1_f32 = ar1[:].bitcast(F32)
    ar2_f32 = ar2[:].bitcast(F32)

    def ffn_ap(fc):
        if fc < 16:
            return ar1_f32[:, fc * 1024:(fc + 1) * 1024]
        return ar2_f32[:, (fc - 16) * 1024:(fc - 15) * 1024]

    lgsb = ar1_f32[0:64, 0:1024]
    lgT = ar1_f32[:, 1024:1536]
    esc = ar1_f32[:, 1536:2048]
    wrk = ar1_f32[:, 2048:2560]
    wdv = ar1_f32[:, 2560:3072]
    smx = ar1_f32[:, 3072:3104]
    x0r_ap = [ar2_f32[:, 0:1024], ar2_f32[:, 1024:2048]]
    x1s_ap = [ar2_f32[:, 2048:3072], ar2_f32[:, 3072:4096]]

    _ROWSLOT = {"RA": 0, "RB": 1, "RC": 2, "RD": 3, "RT1": 4, "RT2": 3}
    _ROWALIAS = {"m1": "RA", "s1": "RB", "m4": "RA", "s4": "RB", "m2": "RA", "s2": "RB",
                 "m3": "RA", "s3": "RB", "rsms": "RB", "ssq_attn": "RC", "wsum": "RC",
                 "A": "RD", "negB": "RT1", "tmsq": "RT1", "tm2": "RT2", "tvar": "RT1"}

    def row(name):
        o = _ROWSLOT[_ROWALIAS.get(name, name)] * 1024
        return rows[0:1, o:o + 1024]

    def rowx(k):
        return ar1_f32[0:1, 4096 + k * 1024: 5120 + k * 1024]

    PS = [ps_(f"ps{i}", [128, 512], F32) for i in range(8)]
    ps7b = PS[7][:].bitcast(BF16)

    s_wld = Sem(nc, "s_wld")
    s_gld = Sem(nc, "s_gld")
    s_te = Sem(nc, "s_te")
    s_ve = Sem(nc, "s_ve")
    s_se = Sem(nc, "s_se")
    s_ge = Sem(nc, "s_ge")
    s_cc = Sem(nc, "s_cc")
    s_bar = Sem(nc, "s_bar")
    s_out = Sem(nc, "s_out")
    s_land = [Sem(nc, f"s_land{i}") for i in range(2)]
    s_lh = [Sem(nc, f"s_lh{i}") for i in range(4)]
    s_ao = [Sem(nc, f"s_ao{i}") for i in range(2)]
    s_x0 = [Sem(nc, f"s_x0{i}") for i in range(2)]
    s_x1s = [Sem(nc, f"s_x1s{i}") for i in range(2)]
    s_qd = [Sem(nc, f"s_qd{i}") for i in range(2)]
    s_dd = [Sem(nc, f"s_dd{i}") for i in range(2)]
    s_kv0r = Sem(nc, "s_kv0r")
    s_cci = Sem(nc, "s_cci")

    CHAIN = {"vector": s_ve, "scalar": s_se, "gpsimd": s_ge}
    DMA_SEMS = {"sync": [s_wld, *s_land, *s_lh, *s_x0],
                "scalar": [*s_ao, *s_x1s],
                "gpsimd": [s_gld, *s_qd, *s_dd, s_kv0r, s_cci]}

    def on(e, fn):
        prog[e].append(fn)

    def W(e, sem, tgt):
        if tgt is None or tgt <= 0:
            return
        on(e, lambda eng, s=sem.h, t=tgt: eng.wait_ge(s, t))

    def Wl(e, conds):
        for sem, tgt in conds:
            W(e, sem, tgt)

    def op(e, fn):
        ch = CHAIN[e]
        prev = ch.n
        tgt = ch.inc()
        def emit(eng, f=fn, s=ch.h, p=prev):
            if p > 0:
                eng.wait_ge(s, p)
            f(eng).then_inc(s, 1)
        prog[e].append(emit)
        return tgt

    def te(fn, inc=False):
        if inc:
            tgt = s_te.inc()
            prog["tensor"].append(lambda eng, f=fn: f(eng).then_inc(s_te.h, 1))
            return tgt
        prog["tensor"].append(lambda eng, f=fn: f(eng))
        return None

    def dma(e, sem, out, in_):
        on(e, lambda eng, o=out, i=in_, s=sem.h: eng.dma_start(out=o, in_=i).then_inc(s, 16))
        return sem.inc(16)

    def barrier():
        for e, sems in DMA_SEMS.items():
            for s in sems:
                if s.n:
                    W(e, s, s.n)
        W("tensor", s_te, s_te.n)
        for e, ch in CHAIN.items():
            W(e, ch, ch.n)
        for e in ENGS:
            on(e, lambda eng, s=s_bar.h: eng.nop().then_inc(s, 1))
        s_bar.n += 5
        t = s_bar.n
        for e in ENGS:
            on(e, lambda eng, s=s_bar.h, t=t: eng.wait_ge(s, t))

    land_ring = Ring(2)
    aost_ring = Ring(2)
    wb_ring = Ring(4)

    def land_ap(i):
        return land[:, i * 2048:(i + 1) * 2048]

    def wb_ap(i):
        return wb[:, i * 2048:(i + 1) * 2048]

    def aost_ap(i):
        return aost[:, i * 1024:(i + 1) * 1024]

    def load_cast_ap(src_ap, dst_ap, ncols=2048, eng_name="scalar", wb_cond=None):
        ch = CHAIN[eng_name]
        li, cond = land_ring.acquire()
        if cond is not None:
            Wl("sync", cond)
        t_ld = dma("sync", s_land[li], land_ap(li)[:, 0:ncols], src_ap)
        if wb_cond is not None:
            Wl(eng_name, wb_cond)
        W(eng_name, s_land[li], t_ld)
        t_cast = op(eng_name, lambda eng, o=dst_ap[:, 0:ncols], i=land_ap(li)[:, 0:ncols]:
                    (eng.copy(out=o, in_=i) if hasattr(eng, "copy") else eng.tensor_copy(out=o, in_=i)))
        land_ring.release(li, ch, t_cast)
        return (ch, t_cast)

    def load_cast(src_ap, wb_idx, ncols=2048, eng_name="scalar", wb_cond=None):
        return load_cast_ap(src_ap, wb_ap(wb_idx), ncols=ncols, eng_name=eng_name, wb_cond=wb_cond)

    def row_rsqrt(dst_row, var_row):
        tv = op("vector", lambda eng, o=dst_row, i=var_row: eng.reciprocal(out=o, in_=i))
        W("scalar", s_ve, tv)
        return op("scalar", lambda eng, o=dst_row, i=dst_row: eng.sqrt(out=o, in_=i))

    def split_bcast(dst_bc, src_row, t_src_se=None, psA=0, psB=1, mp=128):
        if t_src_se is not None:
            W("vector", s_se, t_src_se)
        op("vector", lambda eng, o=rowb[0:1, 0:1024], i=src_row: eng.tensor_copy(out=o, in_=i))
        op("vector", lambda eng, o=row("tmsq"), a=src_row, b=rowb[0:1, 0:1024]:
            eng.tensor_tensor(out=o, in0=a, in1=b, op=ALU.subtract))
        t_rows = op("vector", lambda eng, o=rowb[0:1, 1024:2048], i=row("tmsq"):
                    eng.tensor_copy(out=o, in_=i))
        W("tensor", s_ve, t_rows)
        for bank, p in ((0, psA), (1, psB)):
            te(lambda eng, o=PS[p][0:mp, 0:512], hi=rowb[0:1, bank * 512:(bank + 1) * 512]:
                eng.matmul(o, onr[0:1, 0:mp], hi, start=True, stop=False))
            last = bank == 1
            t_mm = te(lambda eng, o=PS[p][0:mp, 0:512],
                      lo=rowb[0:1, 1024 + bank * 512:1024 + (bank + 1) * 512]:
                      eng.matmul(o, onr[0:1, 0:mp], lo, start=False, stop=True), inc=last)
        W("vector", s_te, t_mm)
        for bank, p in ((0, psA), (1, psB)):
            t_bc = op("vector", lambda eng, o=dst_bc[0:mp, bank * 512:(bank + 1) * 512],
                      i=PS[p][0:mp, 0:512]: eng.tensor_copy(out=o, in_=i))
        return t_bc

    def finalize_ln_rows(m_row, s_row, ps_sum=(4, 5), ps_ssq=(6, 7), t_mm=None, rms_only=False,
                         ssq_row_src=None, skip_m=False):
        if t_mm is not None:
            W("vector", s_te, t_mm)
        c = 1.0 / D
        if not rms_only:
            if not skip_m:
                for bank, p in zip((0, 1), ps_sum):
                    op("vector", lambda eng, o=m_row[0:1, bank * 512:(bank + 1) * 512],
                        i=PS[p][0:1, 0:512], cc=c:
                        eng.tensor_scalar(out=o, in0=i, scalar1=cc, scalar2=None, op0=ALU.mult))
            for bank, p in zip((0, 1), ps_ssq):
                op("vector", lambda eng, o=row("tmsq")[0:1, bank * 512:(bank + 1) * 512].squeeze().unsqueeze(0) if False else rows[0:1, _ROWSLOT["RT1"] * 1024 + bank * 512: _ROWSLOT["RT1"] * 1024 + (bank + 1) * 512],
                    i=PS[p][0:1, 0:512], cc=c:
                    eng.tensor_scalar(out=o, in0=i, scalar1=cc, scalar2=None, op0=ALU.mult))
            op("vector", lambda eng, o=row("tm2"), a=m_row, b=m_row:
                eng.tensor_tensor(out=o, in0=a, in1=b, op=ALU.mult))
            t_ps = op("vector", lambda eng, o=row("tvar"), a=row("tmsq"), b=row("tm2"):
                      eng.tensor_tensor(out=o, in0=a, in1=b, op=ALU.subtract))
        else:
            t_ps = op("vector", lambda eng, o=row("tvar"), i=ssq_row_src, cc=c:
                      eng.tensor_scalar(out=o, in0=i, scalar1=cc, scalar2=None, op0=ALU.mult))
        op("vector", lambda eng, o=row("tvar"), i=row("tvar"), e=EPS:
            eng.tensor_scalar(out=o, in0=i, scalar1=e, scalar2=None, op0=ALU.add))
        return row_rsqrt(s_row, row("tvar")), t_ps

    dma("sync", s_wld, nw[:], nwc)
    dma("sync", s_wld, rb64[:], rbc)
    dma("sync", s_wld, kd[:], kdc)
    dma("sync", s_wld, msk[:], rmask)
    op("vector", lambda eng: eng.memset(onc[:], 1.0))
    op("vector", lambda eng: eng.memset(onr[:], 1.0))
    op("vector", lambda eng: eng.memset(ar2[:, ZKV:ZKV + 128], 0.0))
    op("vector", lambda eng: eng.memset(idb[:], 0.0))
    op("vector", lambda eng: eng.memset(id32[:], 0.0))
    t_ms = op("vector", lambda eng: eng.memset(row("ssq_attn"), 0.0))
    W("gpsimd", s_ve, t_ms)
    for t_ in (idb, id32):
        op("gpsimd", lambda eng, x=t_: eng.affine_select(
            out=x[:], in_=x[:], compare_op=ALU.not_equal, fill=1.0, base=0,
            pattern=[[-1, 128]], channel_multiplier=1))
    barrier()

    if phase_limit >= 0:
        half_ring = Ring(4)

        def landh_ap(i):
            return land[:, i * 1024:(i + 1) * 1024]

        t_fc = None
        for fc in range(NFC):
            q0 = "sync" if fc % 2 == 0 else "gpsimd"
            hi, cond = half_ring.acquire()
            if cond is not None:
                Wl(q0, cond)
            t_ld = dma(q0, s_lh[hi], landh_ap(hi), xT[fc])
            W("scalar", s_lh[hi], t_ld)
            t_cp = op("scalar", lambda eng, o=h1_ap(fc), i=landh_ap(hi): eng.copy(out=o, in_=i))
            half_ring.release(hi, s_se, t_cp)
            if t_fc is not None:
                W("scalar", s_te, t_fc)
            t_sq = op("scalar", lambda eng, o=sqs[:], i=h1_ap(fc):
                      eng.activation(out=o, in_=i, func=AF.Square))
            st, sp = fc == 0, fc == NFC - 1
            W("tensor", s_se, t_cp)
            for bank, p in ((0, 4), (1, 5)):
                te(lambda eng, o=PS[p][0:1, 0:512], r=h1_ap(fc)[:, bank * 512:(bank + 1) * 512],
                    st=st, sp=sp: eng.matmul(o, onc[:], r, start=st, stop=sp))
            W("tensor", s_se, t_sq)
            for bank, p in ((0, 6), (1, 7)):
                t_fc = te(lambda eng, o=PS[p][0:1, 0:512], r=sqs[:, bank * 512:(bank + 1) * 512],
                          st=st, sp=sp: eng.matmul(o, onc[:], r, start=st, stop=sp), inc=bank == 1)
        t_stats = t_fc
        (t_s1, _) = finalize_ln_rows(row("m1"), row("s1"), t_mm=t_stats)
        split_bcast(mbc, row("m1"))
        t_sb = split_bcast(sbc, row("s1"), t_src_se=t_s1)
        norm_t = []
        for fc in range(NFC):
            ai, cond = aost_ring.acquire()
            if cond is not None:
                Wl("vector", cond)
            op("vector", lambda eng, o=aost_ap(ai), a=h1_ap(fc), b=mbc[:]:
                eng.tensor_tensor(out=o, in0=a, in1=b, op=ALU.subtract))
            t_n = op("vector", lambda eng, o=h1_ap(fc), a=aost_ap(ai), b=sbc[:]:
                     eng.tensor_tensor(out=o, in0=a, in1=b, op=ALU.mult))
            norm_t.append(t_n)
            aost_ring.release(ai, s_ve, t_n)
        W("sync", s_ve, t_sb)
        dma("sync", s_wld, m1_park.ap(), row("m1"))
        t_cp_last_ph0 = s_se.n
        t_sb_ph0 = t_sb
        t_stats_ph0 = t_stats
        t_norm_last = norm_t[-1]
        if tap_h1 is not None:
            W("sync", s_ve, t_norm_last)
            dma("sync", s_out, tap_h1, ar1[:])
            on("sync", lambda eng, t=s_out.n: eng.wait_ge(s_out.h, t))

    if phase_limit >= 1:
        t_ps_free = {i: None for i in range(8)}
        if phase_limit >= 0:
            for i in (0, 1, 4, 5, 6, 7):
                t_ps_free[i] = (s_ve, t_sb_ph0)
            land_ring.free[0] = [(s_se, t_cp_last_ph0)]
            land_ring.free[1] = [(s_se, t_cp_last_ph0)]

        def te_wait_ps(i):
            if t_ps_free[i] is not None:
                W("tensor", *t_ps_free[i])
                t_ps_free[i] = None

        kt_ready = {}
        vs_ready = {}
        t_rw_prev = [t_stats_ph0 if phase_limit >= 0 else None]
        qs_guard = {0: None, 1: None}
        kvrun_free = [None]
        stageq = []
        XW_APS = [ar2[:, 28032:30080], ar2[:, 30080:32128]]
        pend_extra = [None]
        extra_free = [None]

        def pop_stage():
            if stageq:
                stageq.pop(0)()
        kv0_free = [None]
        qd_free = {0: None, 1: None}
        dd_free = {0: None, 1: None}

        def proj_head(waps, t_casts, dst_evac, pp, fc_waits=None):
            for tc_ in t_casts:
                W("tensor", *tc_)
            te_wait_ps(pp)
            te_wait_ps(pp + 1)
            t_mm = None
            for fc in range(NFC):
                if fc_waits is not None:
                    W("tensor", s_ve, fc_waits[fc])
                lt = waps[fc // 16][:, (fc % 16) * 128:(fc % 16 + 1) * 128]
                st, sp = fc == 0, fc == NFC - 1
                for bank in (0, 1):
                    t_mm = te(lambda eng, o=PS[pp + bank][0:128, 0:512], l=lt,
                              r=h1_ap(fc)[:, bank * 512:(bank + 1) * 512], st=st, sp=sp:
                              eng.matmul(o, l, r, start=st, stop=sp),
                              inc=(sp and bank == 1))
                if fc % 4 == 3:
                    pop_stage()
            W("scalar", s_te, t_mm)
            for bank in (0, 1):
                t_ev = op("scalar", lambda eng, o=dst_evac[:, bank * 512:(bank + 1) * 512],
                          i=PS[pp + bank][0:128, 0:512]:
                          eng.activation(out=o, in_=i, func=AF.Silu))
            t_ps_free[pp] = (s_se, t_ev)
            t_ps_free[pp + 1] = (s_se, t_ev)
            return t_ev, t_mm

        for g in range(NG + 1):
            slab = g % 2
            if g < NG:
                def load_pair(src2):
                    slots, tcs = [], []
                    for halfw in range(2):
                        wi, cond = wb_ring.acquire()
                        if cond is not None:
                            Wl("sync", cond)
                        tcs.append(load_cast(src2.rearrange("p (a c) -> a p c", a=2)[halfw], wi,
                                             wb_cond=cond))
                        slots.append(wi)
                    return slots, tcs

                if pend_extra[0] is not None:
                    pend = pend_extra[0]
                    pend_extra[0] = None
                    extra_used = True
                else:
                    pend = load_pair(wqk[GH * g, 1])
                    extra_used = False
                for hh in range(GH):
                    h = GH * g + hh
                    slots, tcs = pend
                    if hh + 1 < GH:
                        pend = load_pair(wqk[h + 1, 1])
                    aps = [XW_APS[0], XW_APS[1]] if (hh == 0 and extra_used) else [wb_ap(wi) for wi in slots]
                    t_ev, t_kmm = proj_head(aps, tcs, kt_ap(slab, hh), pp=(0 if hh % 2 == 0 else 2),
                                            fc_waits=(norm_t if (g == 0 and hh == 0 and phase_limit >= 0) else None))
                    kt_ready[(slab, hh)] = t_ev
                    if hh == 0 and extra_used:
                        extra_free[0] = (s_te, t_kmm)
                    else:
                        for wi in slots:
                            wb_ring.release(wi, s_te, t_kmm)
                for i in range(8):
                    te_wait_ps(i)
                vslots, vtcs = {}, {}
                for chn in range(2):
                    wi, cond = wb_ring.acquire()
                    if cond is not None:
                        Wl("sync", cond)
                    vtcs[chn] = load_cast(wv[g, chn], wi, wb_cond=cond)
                    vslots[chn] = wi
                for fc in range(NFC):
                    chn = fc // 4
                    if fc % 4 == 0:
                        if chn + 2 < 8:
                            wi, cond = wb_ring.acquire()
                            if cond is not None:
                                Wl("sync", cond)
                            vtcs[chn + 2] = load_cast(wv[g, chn + 2], wi, wb_cond=cond)
                            vslots[chn + 2] = wi
                        W("tensor", *vtcs[chn])
                    sl = vslots[chn]
                    rt = wb_ap(sl)[:, (fc % 4) * 512:(fc % 4 + 1) * 512]
                    st, sp = fc == 0, fc == NFC - 1
                    for tcch in range(8):
                        o = PS[tcch][0:128, 0:512]
                        t_vmm = te(lambda eng, o=o, l=h1_ap(fc)[:, tcch * 128:(tcch + 1) * 128],
                                   r=rt, st=st, sp=sp: eng.matmul(o, l, r, start=st, stop=sp),
                                   inc=(fc % 4 == 3 and tcch == 7))
                    if fc % 4 == 3:
                        wb_ring.release(sl, s_te, t_vmm)
                W("scalar", s_te, t_vmm)
                for tcch in range(8):
                    t_vev = op("scalar", lambda eng,
                               o=vs_ap(slab)[:, tcch * 512: tcch * 512 + 512],
                               i=PS[tcch][0:128, 0:512]:
                               eng.activation(out=o, in_=i, func=AF.Silu))
                vs_ready[slab] = t_vev
                for i in range(8):
                    t_ps_free[i] = (s_se, t_vev)
                ps_trb = PS[0][:].bitcast(BF16)
                for hh in range(GH):
                    h = GH * g + hh
                    t_trfree = {i: None for i in range(4)}
                    t_cmm = {}
                    tkp = {}
                    W("tensor", s_se, kt_ready[(slab, hh)])
                    W("tensor", s_se, vs_ready[slab])
                    te_wait_ps(0)
                    for u in range(8):
                        bb, halfm = u // 2, u % 2
                        sl = u % 4
                        if t_trfree[sl] is not None:
                            W("tensor", *t_trfree[sl])
                            t_trfree[sl] = None
                        if u >= 4:
                            W("vector", s_te, t_cmm[u // 2 - 2])
                        t_tr = te(lambda eng, o=ps_trb[0:128, sl * 128:(sl + 1) * 128],
                                  i=kt_ap(slab, hh)[:, bb * 256 + halfm * 128: bb * 256 + halfm * 128 + 128]:
                                  eng.transpose(o, i, idb[:]), inc=True)
                        W("vector", s_te, t_tr)
                        tkp[u] = op("vector", lambda eng,
                                    o=ar2[:, KP0 + sl * 128: KP0 + (sl + 1) * 128],
                                    a=ps_trb[0:128, sl * 128:(sl + 1) * 128],
                                    b=kd[:, 2 * h + halfm:2 * h + halfm + 1].broadcast_to([128, 128]):
                                    eng.tensor_tensor(out=o, in0=a, in1=b, op=ALU.mult))
                        t_trfree[sl] = (s_ve, tkp[u])
                        if u % 2 == 1:
                            bb2 = u // 2
                            te_wait_ps(4 + bb2)
                            W("tensor", s_ve, tkp[u])
                            for halfm2 in range(2):
                                t_cmm[bb2] = te(lambda eng, o=PS[4 + bb2][0:128, 0:128],
                                                l=ar2[:, KP0 + ((2 * bb2 + halfm2) % 4) * 128: KP0 + ((2 * bb2 + halfm2) % 4) * 128 + 128],
                                                r=vs_ap(slab)[:, (2 * bb2 + halfm2) * 512 + hh * 128:(2 * bb2 + halfm2) * 512 + hh * 128 + 128],
                                                st=halfm2 == 0, sp=halfm2 == 1:
                                                eng.matmul(o, l, r, start=st, stop=sp), inc=halfm2 == 1)
                    if kvrun_free[0] is not None:
                        W("vector", *kvrun_free[0])
                        if hh == GH - 1:
                            kvrun_free[0] = None
                    W("vector", s_te, t_cmm[0])
                    op("vector", lambda eng, o=kvrun[:, hh * 128:(hh + 1) * 128], i=PS[4][0:128, 0:128]:
                        eng.tensor_copy(out=o, in_=i))
                    op("vector", lambda eng, o=kvloc_ap(slab, hh, 0), i=kvrun[:, hh * 128:(hh + 1) * 128]:
                        eng.tensor_copy(out=o, in_=i))
                    for bb in range(1, NB):
                        W("vector", s_te, t_cmm[bb])
                        t_up = op("vector", lambda eng, o=kvrun[:, hh * 128:(hh + 1) * 128],
                                  sc=bd[h], i1=PS[4 + bb][0:128, 0:128]:
                                  eng.scalar_tensor_tensor(out=o, in0=o, scalar=sc, in1=i1,
                                                           op0=ALU.mult, op1=ALU.add))
                        if bb < NB - 1:
                            op("vector", lambda eng, o=kvloc_ap(slab, hh, bb),
                                i=kvrun[:, hh * 128:(hh + 1) * 128]:
                                eng.tensor_copy(out=o, in_=i))
                    t_ps_free[0] = (s_ve, tkp[7])
                    for bb in range(NB):
                        t_ps_free[4 + bb] = (s_ve, t_up)
                t_states = op("vector", lambda eng: eng.engine_nop())
                W("gpsimd", s_ve, t_states)
                t_gld = dma("gpsimd", s_cci, cci[g].ap(), kvrun[:])
                kvrun_free[0] = (s_cci, t_gld)
                W("gpsimd", s_cci, t_gld)
                on("gpsimd", lambda eng, i=cci[g].ap(), o=cco[g].ap():
                    eng.collective_compute("AllGather", ALU.bypass,
                                           replica_groups=[[0, 1], [2, 3], [4, 5], [6, 7]],
                                           ins=[i], outs=[o]).then_inc(s_cc.h, 1))
                s_cc.inc()

            if g >= 1:
                gp = g - 1
                pslab = gp % 2
                while stageq:
                    stageq.pop(0)()
                W("gpsimd", s_cc, gp + 1)
                if kv0_free[0] is not None:
                    W("gpsimd", *kv0_free[0])
                    kv0_free[0] = None
                t_r = dma("gpsimd", s_kv0r, kv0[:, 0:512], cco[gp].ap()[0])
                W("vector", s_kv0r, t_r)
                t_kv0 = op("vector", lambda eng, o=kv0[:, 0:512],
                           b=msk[:].broadcast_to([128, 512]):
                           eng.tensor_tensor(out=o, in0=o, in1=b, op=ALU.mult))
                pendq = load_pair(wqk[GH * gp, 0])

                def qc_ap(qp):
                    o = QC0 if qp == 0 else QC1
                    return ar2[:, o:o + 1024]

                def qs_ap(qp):
                    o = QS0 if qp == 0 else QS1
                    return ar2[:, o:o + 1024]

                def build_stages(h, hh, hp, qp, t_qev, t_qs, pslab=None):
                    ctx = {}
                    t_sc_l, t_sct_l, t_kvb_l = {}, {}, {}

                    def emit_S(b):
                        t_kvb_l[b] = op("vector", lambda eng,
                                        o=ar2[:, KV2 + (b % 2) * 128: KV2 + (b % 2) * 128 + 128],
                                        i0=kv0[:, hh * 128: hh * 128 + 128],
                                        sc=float(bd_t[h] ** b),
                                        i1=(ar2[:, ZKV:ZKV + 128] if b == 0 else kvloc_ap(pslab, hh, b - 1)):
                                        eng.scalar_tensor_tensor(out=o, in0=i0, scalar=sc, in1=i1,
                                                                 op0=ALU.mult, op1=ALU.add))
                        scb = 4 if b % 2 == 0 else 6
                        te_wait_ps(scb)
                        if b == 0:
                            W("tensor", s_se, t_qev)
                            W("tensor", s_se, kt_ready[(pslab, hh)])
                            W("tensor", s_se, vs_ready[pslab])
                            W("tensor", s_ve, t_qs)
                        for halfm in range(2):
                            t_sc_l[b] = te(lambda eng, o=PS[scb][0:128, halfm * 256:halfm * 256 + 256],
                                           l=kt_ap(pslab, hh)[:, b * 256 + halfm * 128: b * 256 + halfm * 128 + 128],
                                           r=qc_ap(qp)[:, b * 256:(b + 1) * 256]:
                                           eng.matmul(o, l, r, start=True, stop=True), inc=halfm == 1)
                        if b == 0:
                            W("vector", s_dd[hp], ctx["t_dd"])
                        W("vector", s_te, t_sc_l[b])
                        t_sct_l[b] = op("vector", lambda eng,
                                        o=ar2[:, SC0 + (b % 2) * 512: SC0 + (b % 2) * 512 + 512],
                                        a=PS[scb][0:128, 0:512],
                                        b2=ar2[:, DD0 + hp * 512: DD0 + (hp + 1) * 512]:
                                        eng.tensor_tensor(out=o, in0=a, in1=b2, op=ALU.mult))
                        t_ps_free[scb] = (s_ve, t_sct_l[b])

                    def emit_O(b):
                        ob = 5 if b % 2 == 0 else 7
                        bp = b % 2
                        te_wait_ps(ob)
                        W("tensor", s_ve, t_sct_l[b])
                        W("tensor", s_ve, t_kvb_l[b])
                        for halfm in range(2):
                            te(lambda eng, o=PS[ob][0:128, 0:256],
                                l=vs_ap(pslab)[:, (2 * b + halfm) * 512 + hh * 128:(2 * b + halfm) * 512 + hh * 128 + 128],
                                r=ar2[:, SC0 + bp * 512 + halfm * 256: SC0 + bp * 512 + halfm * 256 + 256],
                                st=halfm == 0:
                                eng.matmul(o, l, r, start=st, stop=False))
                        t_o = te(lambda eng, o=PS[ob][0:128, 0:256],
                                 l=ar2[:, KV2 + (b % 2) * 128: KV2 + (b % 2) * 128 + 128],
                                 r=qs_ap(qp)[:, b * 256:(b + 1) * 256]:
                                 eng.matmul(o, l, r, start=False, stop=True), inc=True)
                        if b == NB - 1:
                            qs_guard[qp] = t_o
                        W("vector", s_te, t_o)
                        t_ev = op("vector", lambda eng, o=aost_ap(ctx["ai"])[:, b * 256:(b + 1) * 256],
                                  i=PS[ob][0:128, 0:256]: eng.tensor_copy(out=o, in_=i))
                        t_ps_free[ob] = (s_ve, t_ev)
                        ctx["t_ev"] = t_ev

                    def st_first():
                        ai, cond = aost_ring.acquire()
                        ctx["ai"] = ai
                        if cond is not None:
                            Wl("vector", cond)
                        emit_S(0)

                    def st_rms():
                        ai = ctx["ai"]
                        if t_rw_prev[0] is not None:
                            W("vector", s_te, t_rw_prev[0])
                            t_rw_prev[0] = None
                        t_sq = op("vector", lambda eng, o=sqs[:], a=aost_ap(ai), b=aost_ap(ai):
                                  eng.tensor_tensor(out=o, in0=a, in1=b, op=ALU.mult))
                        te_wait_ps(6)
                        te_wait_ps(7)
                        W("tensor", s_ve, t_sq)
                        for bank, p in ((0, 6), (1, 7)):
                            t_rw = te(lambda eng, o=PS[p][0:1, 0:512],
                                      r=sqs[:, bank * 512:(bank + 1) * 512]:
                                      eng.matmul(o, onc[:], r, start=True, stop=True), inc=bank == 1)
                        t_rw_prev[0] = t_rw
                        W("vector", s_te, t_rw)
                        for bank, p in ((0, 6), (1, 7)):
                            t_acc = op("vector", lambda eng,
                                       o=rows[0:1, _ROWSLOT["RC"] * 1024 + bank * 512: _ROWSLOT["RC"] * 1024 + (bank + 1) * 512],
                                       i=PS[p][0:1, 0:512]:
                                       eng.tensor_tensor(out=o, in0=o, in1=i, op=ALU.add))
                        t_ps_free[6] = (s_ve, t_acc)
                        t_ps_free[7] = (s_ve, t_acc)
                        qd_free[hp] = (s_ve, t_qs)
                        dd_free[hp] = (s_ve, t_acc)
                        kv0_free[0] = (s_ve, t_acc)
                        W("scalar", s_ve, ctx["t_ev"])
                        t_park = dma("scalar", s_ao[ai], ao_park[h], aost_ap(ai))
                        aost_ring.release(ai, s_ao[ai], t_park)

                    return ctx, [st_first,
                                 lambda: emit_S(1),
                                 lambda: (emit_O(0), emit_S(2)),
                                 lambda: (emit_O(1), emit_S(3)),
                                 lambda: emit_O(2),
                                 lambda: emit_O(3),
                                 st_rms]

                for hh in range(GH + 1):
                    if hh < GH:
                        h = GH * gp + hh
                        qslots, qtcs = pendq
                        if hh + 1 < GH:
                            pendq = load_pair(wqk[h + 1, 0])
                        hp = h % 2
                        qp = hh % 2
                        if qd_free[hp] is not None:
                            W("gpsimd", *qd_free[hp])
                            qd_free[hp] = None
                        t_qd = dma("gpsimd", s_qd[hp], ar2[:, QD0 + hp * 256: QD0 + (hp + 1) * 256], qdb[h])
                        if dd_free[hp] is not None:
                            W("gpsimd", *dd_free[hp])
                            dd_free[hp] = None
                        dma("gpsimd", s_dd[hp],
                            ar2[:, DD0 + hp * 512: DD0 + hp * 512 + 256], ddtb[h][0])
                        t_dd = dma("gpsimd", s_dd[hp],
                                   ar2[:, DD0 + hp * 512 + 256: DD0 + hp * 512 + 512], ddtb[h][1])
                        for tc_ in qtcs:
                            W("tensor", *tc_)
                        pp = 0 if hh % 2 == 0 else 2
                        te_wait_ps(pp)
                        te_wait_ps(pp + 1)
                        t_qmm = None
                        for fc in range(NFC):
                            sl = qslots[fc // 16]
                            lt = wb_ap(sl)[:, (fc % 16) * 128:(fc % 16 + 1) * 128]
                            st, sp = fc == 0, fc == NFC - 1
                            for bank in (0, 1):
                                t_qmm = te(lambda eng, o=PS[pp + bank][0:128, 0:512], l=lt,
                                           r=h1_ap(fc)[:, bank * 512:(bank + 1) * 512], st=st, sp=sp:
                                           eng.matmul(o, l, r, start=st, stop=sp),
                                           inc=(sp and bank == 1))
                            if fc % 4 == 3 and stageq:
                                stageq.pop(0)()
                        W("scalar", s_te, t_qmm)
                        for bank in (0, 1):
                            t_qev = op("scalar", lambda eng,
                                       o=qc_ap(qp)[:, bank * 512:(bank + 1) * 512],
                                       i=PS[pp + bank][0:128, 0:512]:
                                       eng.activation(out=o, in_=i, func=AF.Silu))
                        t_ps_free[pp] = (s_se, t_qev)
                        t_ps_free[pp + 1] = (s_se, t_qev)
                        for wi in qslots:
                            wb_ring.release(wi, s_te, t_qmm)
                        W("vector", s_se, t_qev)
                        W("vector", s_qd[hp], t_qd)
                        if qs_guard[qp] is not None:
                            W("vector", s_te, qs_guard[qp])
                            qs_guard[qp] = None
                        for bb in range(NB):
                            t_qs = op("vector", lambda eng,
                                      o=qs_ap(qp)[:, bb * 256:(bb + 1) * 256],
                                      a=qc_ap(qp)[:, bb * 256:(bb + 1) * 256],
                                      b2=ar2[:, QD0 + hp * 256: QD0 + (hp + 1) * 256]:
                                      eng.tensor_tensor(out=o, in0=a, in1=b2, op=ALU.mult))
                        ctx, stages = build_stages(h, hh, hp, qp, t_qev, t_qs, pslab=pslab)
                        ctx["t_dd"] = t_dd
                        stageq.extend(stages)
                        if hh == 2 and g + 1 < NG:
                            nh = GH * (g + 1)
                            xs = wqk[nh, 1].rearrange("p (a c) -> a p c", a=2)
                            cond_x = [extra_free[0]] if extra_free[0] is not None else None
                            extra_free[0] = None
                            tcs_x = []
                            for halfw in range(2):
                                tcs_x.append(load_cast_ap(xs[halfw], XW_APS[halfw], wb_cond=cond_x))
                            pend_extra[0] = (None, tcs_x)
                    else:
                        pass
        while stageq:
            stageq.pop(0)()
        barrier()
        if tap_kt is not None:
            dma("sync", s_out, tap_kt, ar2[:, KT0:KT0 + 8192])
            on("sync", lambda eng, t=s_out.n: eng.wait_ge(s_out.h, t))

    if phase_limit >= 3:
        (t_rs, _) = finalize_ln_rows(None, row("rsms"), rms_only=True, ssq_row_src=row("ssq_attn"))
        t_rsb = split_bcast(sbc, row("rsms"), t_src_se=t_rs)
        t_ps_free3 = {0: (s_ve, t_rsb), 1: (s_ve, t_rsb), 2: None, 3: None}
        t_gated_prev = [None]

        def load_pair3(src2):
            slots_, tcs_ = [], []
            for halfw in range(2):
                wi, cond = wb_ring.acquire()
                if cond is not None:
                    Wl("sync", cond)
                tcs_.append(load_cast(src2.rearrange("p (a c) -> a p c", a=2)[halfw], wi,
                                      wb_cond=cond))
                slots_.append(wi)
            return slots_, tcs_

        pend3 = load_pair3(wg[0])
        for oc in range(NFC):
            slots, tcs = pend3
            if oc + 1 < NFC:
                pend3 = load_pair3(wg[oc + 1])
            ai, cond = aost_ring.acquire()
            if cond is not None:
                Wl("sync", cond)
            t_ao = dma("sync", s_ao[ai], aost_ap(ai), ao_park[oc])
            pp = 0 if oc % 2 == 0 else 2
            for tc_ in tcs:
                W("tensor", *tc_)
            for i in (pp, pp + 1):
                if t_ps_free3[i] is not None:
                    W("tensor", *t_ps_free3[i])
                    t_ps_free3[i] = None
            t_mm = None
            for fc in range(NFC):
                sl = slots[fc // 16]
                lt = wb_ap(sl)[:, (fc % 16) * 128:(fc % 16 + 1) * 128]
                st, sp = fc == 0, fc == NFC - 1
                for bank in (0, 1):
                    t_mm = te(lambda eng, o=PS[pp + bank][0:128, 0:512], l=lt,
                              r=h1_ap(fc)[:, bank * 512:(bank + 1) * 512], st=st, sp=sp:
                              eng.matmul(o, l, r, start=st, stop=sp), inc=(sp and bank == 1))
            W("scalar", s_te, t_mm)
            if t_gated_prev[0] is not None:
                W("scalar", s_ve, t_gated_prev[0])
                t_gated_prev[0] = None
            for bank in (0, 1):
                t_sig = op("scalar", lambda eng, o=sqs[:, bank * 512:(bank + 1) * 512],
                           i=PS[pp + bank][0:128, 0:512]:
                           eng.activation(out=o, in_=i, func=AF.Sigmoid))
            for wi in slots:
                wb_ring.release(wi, s_te, t_mm)
            t_ps_free3[pp] = (s_se, t_sig)
            t_ps_free3[pp + 1] = (s_se, t_sig)
            W("vector", s_ao[ai], t_ao)
            if oc == 0:
                W("vector", s_ve, t_rsb)
            t_nrm = op("vector", lambda eng, o=mbc[:], i0=aost_ap(ai), sc=nw[:, oc:oc + 1], i1=sbc[:]:
                       eng.scalar_tensor_tensor(out=o, in0=i0, scalar=sc, in1=i1,
                                                op0=ALU.mult, op1=ALU.mult))
            aost_ring.release(ai, s_ve, t_nrm)
            W("vector", s_se, t_sig)
            t_gated_prev[0] = op("vector", lambda eng, o=gated_ap(oc), a=mbc[:], b=sqs[:]:
                eng.tensor_tensor(out=o, in0=a, in1=b, op=ALU.mult))
        barrier()
        if tap_gated is not None:
            dma("sync", s_out, tap_gated, ar2[:])
            on("sync", lambda eng, t=s_out.n: eng.wait_ge(s_out.h, t))

    if phase_limit >= 4:
        t_ps_free4 = {i: None for i in range(4)}
        t_stats_mm = [None]
        stats_pend = [None]
        flush4_queue = []

        def load_pair4(src2):
            slots_, tcs_ = [], []
            for halfw in range(2):
                wi, cond = wb_ring.acquire()
                if cond is not None:
                    Wl("sync", cond)
                tcs_.append(load_cast(src2.rearrange("p (a c) -> a p c", a=2)[halfw], wi,
                                      wb_cond=cond))
                slots_.append(wi)
            return slots_, tcs_

        pend4 = load_pair4(wo[0])
        for oc in range(NFC):
            slots, tcs = pend4
            if oc + 1 < NFC:
                pend4 = load_pair4(wo[oc + 1])
            pp = 0 if oc % 2 == 0 else 2
            for tc_ in tcs:
                W("tensor", *tc_)
            for i in (pp, pp + 1):
                if t_ps_free4[i] is not None:
                    W("tensor", *t_ps_free4[i])
                    t_ps_free4[i] = None
            t_mm = None
            for fc in range(NFC):
                sl = slots[fc // 16]
                lt = wb_ap(sl)[:, (fc % 16) * 128:(fc % 16 + 1) * 128]
                st, sp = fc == 0, fc == NFC - 1
                for bank in (0, 1):
                    t_mm = te(lambda eng, o=PS[pp + bank][0:128, 0:512], l=lt,
                              r=gated_ap(fc)[:, bank * 512:(bank + 1) * 512], st=st, sp=sp:
                              eng.matmul(o, l, r, start=st, stop=sp), inc=(sp and bank == 1))
            if flush4_queue:
                flush4_queue.pop(0)()
            for wi in slots:
                wb_ring.release(wi, s_te, t_mm)
            ai, cond = aost_ring.acquire()
            if cond is not None:
                Wl("vector", cond)
            W("vector", s_te, t_mm)
            for bank in (0, 1):
                t_ev = op("vector", lambda eng, o=aost_ap(ai)[:, bank * 512:(bank + 1) * 512],
                          i=PS[pp + bank][0:128, 0:512]: eng.tensor_copy(out=o, in_=i))
            t_ps_free4[pp] = (s_ve, t_ev)
            t_ps_free4[pp + 1] = (s_ve, t_ev)
            if t_stats_mm[0] is not None:
                W("vector", s_te, t_stats_mm[0])
            t_c4 = op("vector", lambda eng, o=sqs[:], i=aost_ap(ai): eng.tensor_copy(out=o, in_=i))
            t_q4 = op("vector", lambda eng, o=sqs2[:], a=aost_ap(ai), b=aost_ap(ai):
                      eng.tensor_tensor(out=o, in0=a, in1=b, op=ALU.mult))
            stats_pend[0] = (oc, t_c4, t_q4)
            W("scalar", s_ve, t_ev)
            t_park = dma("scalar", s_ao[ai], ao2_park[oc], aost_ap(ai))
            aost_ring.release(ai, s_ao[ai], t_park)

            def flush_stats4():
                oc_, t_c_, t_q_ = stats_pend[0]
                stats_pend[0] = None
                W("tensor", s_ve, t_q_)
                st_, sp_ = oc_ == 0, oc_ == NFC - 1
                for bank, p in ((0, 4), (1, 5)):
                    te(lambda eng, o=PS[p][0:1, 0:512], r=sqs[:, bank * 512:(bank + 1) * 512],
                        st=st_, sp=sp_: eng.matmul(o, onc[:], r, start=st_, stop=sp_))
                for bank, p in ((0, 6), (1, 7)):
                    t_stats_mm[0] = te(lambda eng, o=PS[p][0:1, 0:512],
                                       r=sqs2[:, bank * 512:(bank + 1) * 512],
                                       st=st_, sp=sp_: eng.matmul(o, onc[:], r, start=st_, stop=sp_),
                                       inc=bank == 1)
            if oc == NFC - 1:
                flush_stats4()
            else:
                flush4_queue.append(flush_stats4)
        barrier()

    if phase_limit >= 5:
        (t_s4, t_psr5) = finalize_ln_rows(row("m4"), row("s4"))
        split_bcast(mbc, row("m4"))
        t_sb4 = split_bcast(sbc, row("s4"), t_src_se=t_s4)
        wi6, cond = wb_ring.acquire()
        if cond is not None:
            Wl("sync", cond)
        t_wr = load_cast(wr, wi6, wb_cond=cond)
        (_, t_eT) = load_cast_ap(ebxTd, ebxT_b)
        W("tensor", s_se, t_eT)
        W("tensor", s_ve, t_sb4)
        for gfc in range(NFC):
            t_G = te(lambda eng, l=ebxT_b[:, gfc * 64:(gfc + 1) * 64],
                     st=gfc == 0, sp=gfc == NFC - 1:
                     eng.matmul(PS[0][0:64, 0:64], l, l, start=st, stop=sp),
                     inc=gfc == NFC - 1)
        x0_ring = Ring(2)
        x1_ring = Ring(2)
        t_m2b = None
        pend5 = {}

        def issue5(fc_):
            ai_, cond_ = aost_ring.acquire()
            if cond_ is not None:
                Wl("sync", cond_)
            t_ao_ = dma("sync", s_ao[ai_], aost_ap(ai_), ao2_park[fc_])
            xi_, cond_ = x0_ring.acquire()
            if cond_ is not None:
                Wl("sync", cond_)
            t_x0_ = dma("sync", s_x0[xi_], x0r_ap[xi_], xT[fc_])
            pend5[fc_] = (ai_, t_ao_, xi_, t_x0_)

        issue5(0)
        for fc in range(NFC):
            ai, t_ao, xi, t_x0 = pend5.pop(fc)
            if fc + 1 < NFC:
                issue5(fc + 1)
            si, cond = x1_ring.acquire()
            SPL = 832
            t_both = {}
            for en, c0, c1 in (("vector", 0, SPL), ("gpsimd", SPL, 1024)):
                if cond is not None:
                    Wl(en, cond)
                W(en, s_ao[ai], t_ao)
                W(en, s_x0[xi], t_x0)
                if fc <= 1:
                    W(en, s_ve, t_sb4)
                op(en, lambda eng, o=x1s_ap[si][:, c0:c1], a=aost_ap(ai)[:, c0:c1], b=mbc[:, c0:c1]:
                    eng.tensor_tensor(out=o, in0=a, in1=b, op=ALU.subtract))
                op(en, lambda eng, o=x1s_ap[si][:, c0:c1], a=x1s_ap[si][:, c0:c1], b=sbc[:, c0:c1]:
                    eng.tensor_tensor(out=o, in0=a, in1=b, op=ALU.mult))
                t_both[en] = op(en, lambda eng, o=x1s_ap[si][:, c0:c1], a=x1s_ap[si][:, c0:c1],
                                b=x0r_ap[xi][:, c0:c1]:
                                eng.tensor_tensor(out=o, in0=a, in1=b, op=ALU.add))
            t_x1 = t_both["vector"]
            t_x1g = t_both["gpsimd"]
            aost_ring.release_multi(ai, [(s_ve, t_x1), (s_ge, t_x1g)])
            x0_ring.release_multi(xi, [(s_ve, t_x1), (s_ge, t_x1g)])
            W("scalar", s_ve, t_x1)
            W("scalar", s_ge, t_x1g)
            if t_m2b is not None:
                W("scalar", s_te, t_m2b)
            if t_m2b is not None:
                W("scalar", s_te, t_m2b)
            t_c = op("scalar", lambda eng, o=sqs[:], i=x1s_ap[si]: eng.copy(out=o, in_=i))
            t_qb = op("scalar", lambda eng, o=sqs2[:], i=x1s_ap[si]:
                      eng.activation(out=o, in_=i, func=AF.Square))
            W("tensor", s_se, t_c)
            if fc == 0:
                W("tensor", s_ve, t_psr5)
                W("tensor", s_ve, t_sb4)
                W("tensor", *t_wr)
            st, sp = fc == 0, fc == NFC - 1
            for bank, p in ((0, 2), (1, 3)):
                te(lambda eng, o=PS[p][0:64, 0:512],
                    l=wb_ap(wi6)[:, fc * 64:(fc + 1) * 64],
                    r=sqs[:, bank * 512:(bank + 1) * 512], st=st, sp=sp:
                    eng.matmul(o, l, r, start=st, stop=sp))
            te(lambda eng, l=wb_ap(wi6)[:, fc * 64:(fc + 1) * 64], st=st, sp=sp:
                eng.matmul(PS[1][0:64, 0:1], l, onc[:], start=st, stop=sp))
            for bank, p in ((0, 4), (1, 5)):
                te(lambda eng, o=PS[p][0:64, 0:512],
                    l=ebxT_b[:, fc * 64:(fc + 1) * 64],
                    r=sqs[:, bank * 512:(bank + 1) * 512], st=st, sp=sp:
                    eng.matmul(o, l, r, start=st, stop=sp))
            W("tensor", s_se, t_qb)
            for bank, p in ((0, 6), (1, 7)):
                t_m2b = te(lambda eng, o=PS[p][0:1, 0:512], r=sqs2[:, bank * 512:(bank + 1) * 512],
                           st=st, sp=sp: eng.matmul(o, onc[:], r, start=st, stop=sp), inc=bank == 1)
            W("scalar", s_ve, t_x1)
            W("scalar", s_ge, t_x1g)
            t_park = dma("scalar", s_x1s[si], x1_park[fc], x1s_ap[si])
            x1_ring.release(si, s_x1s[si], t_park)
        wb_ring.release(wi6, s_te, t_m2b)
        barrier()

    if phase_limit >= 6:
        t_m1r = dma("sync", s_wld, row("m2"), m1_park.ap())
        W("vector", s_wld, t_m1r)
        for _ in range(2):
            li, cond = land_ring.acquire()
            if cond is not None:
                Wl("sync", cond)
        t_eb = dma("sync", s_land[0], land[0:64, 0:4096], ebx)
        W("scalar", s_land[0], t_eb)
        t_ebc = op("scalar", lambda eng, o=ebb, i=land[0:64, 0:4096]: eng.copy(out=o, in_=i))
        (t_s2, t_psr6) = finalize_ln_rows(row("m2"), row("s2"), skip_m=True)
        t_s2sq = op("scalar", lambda eng, o=rowx(4), i=row("s2"): eng.square(out=o, in_=i))
        W("vector", s_te, s_te.n)
        op("vector", lambda eng, o=wsumcol[:], i=PS[1][0:64, 0:1]: eng.tensor_copy(out=o, in_=i))
        t_wsc = op("vector", lambda eng, o=wsumcol[:], i=wsumcol[:]:
                   eng.tensor_scalar(out=o, in0=i, scalar1=-1.0, scalar2=None, op0=ALU.mult))
        for bank, p in ((0, 2), (1, 3)):
            op("vector", lambda eng, o=lgsb[:, bank * 512:(bank + 1) * 512], i=PS[p][0:64, 0:512]:
                eng.tensor_copy(out=o, in_=i))
        W("vector", s_se, t_s2)
        t_ms2 = op("vector", lambda eng, o=row("tm2"), a=row("m2"), b=row("s2"):
                   eng.tensor_tensor(out=o, in0=a, in1=b, op=ALU.mult))
        t_s2b = split_bcast(mbc, row("s2"), t_src_se=t_s2, mp=64, psA=2, psB=3)
        t_msb = split_bcast(sbc, row("tm2"), mp=64, psA=2, psB=3)
        W("scalar", s_te, t_G)
        t_gb = op("scalar", lambda eng, o=gb, i=PS[0][0:64, 0:64]: eng.copy(out=o, in_=i))
        op("vector", lambda eng, o=lgsb, a=lgsb, b=mbc[0:64, 0:1024]:
            eng.tensor_tensor(out=o, in0=a, in1=b, op=ALU.mult))
        op("vector", lambda eng, o=lgsb, i0=sbc[0:64, 0:1024], sc=wsumcol[:], i1=lgsb:
            eng.scalar_tensor_tensor(out=o, in0=i0, scalar=sc, in1=i1, op0=ALU.mult, op1=ALU.add))
        t_lg = op("vector", lambda eng, o=lgsb, i=lgsb, sc=rb64[:]:
                  eng.tensor_scalar(out=o, in0=i, scalar1=sc, scalar2=None, op0=ALU.add))
        barrier()
        if tap_h2 is not None:
            dma("sync", s_out, tap_h2, ar1[:])
            on("sync", lambda eng, t=s_out.n: eng.wait_ge(s_out.h, t))

    if phase_limit >= 7:
        evac_tok = {0: None, 1: None}
        for tt in range(8):
            pb = tt % 2
            if evac_tok[pb] is not None:
                W("tensor", s_ve, evac_tok[pb])
            t_tr = te(lambda eng, o=PS[pb][0:128, 0:64], i=lgsb[:, tt * 128:(tt + 1) * 128]:
                      eng.transpose(o, i, id32[0:64, 0:64]), inc=True)
            W("vector", s_te, t_tr)
            t_e = op("vector", lambda eng, o=lgT[:, tt * 64:(tt + 1) * 64], i=PS[pb][0:128, 0:64]:
                     eng.tensor_copy(out=o, in_=i))
            evac_tok[pb] = t_e
        W("tensor", s_ve, t_e)
        lgT3 = lgT.rearrange("p (a b) -> p a b", a=8)
        esc3 = esc.rearrange("p (a b) -> p a b", a=8)
        wrk3 = wrk.rearrange("p (a b) -> p a b", a=8)
        nmax = smx[:, 0:8]
        Zr = smx[:, 8:16]
        acc = smx[:, 16:24]
        mx = smx[:, 24:32]
        op("vector", lambda eng, o=nmax, i=lgT3: eng.reduce_max(out=o, in_=i, axis=AX.X, negate=True))
        t_sub = op("vector", lambda eng, o=esc3, a=lgT3,
                   b=nmax.unsqueeze(2).broadcast_to([128, 8, 64]):
                   eng.tensor_tensor(out=o, in0=a, in1=b, op=ALU.add))
        W("scalar", s_ve, t_sub)
        t_exp = op("scalar", lambda eng, o=esc, i=esc: eng.activation(out=o, in_=i, func=AF.Exp))
        W("vector", s_se, t_exp)
        op("vector", lambda eng, o=Zr, i=esc3: eng.reduce_sum(out=o, in_=i, axis=AX.X))
        op("vector", lambda eng, o=wrk, i=esc: eng.tensor_copy(out=o, in_=i))
        op("vector", lambda eng, o=acc: eng.memset(o, 0.0))
        sbc3 = sbc[:, 0:512].rearrange("p (a b) -> p a b", a=8)
        wdv3 = wdv.rearrange("p (a b) -> p a b", a=8)
        for rnd in range(8):
            op("vector", lambda eng, o=mx, i=wrk3: eng.reduce_max(out=o, in_=i, axis=AX.X))
            op("vector", lambda eng, o=mx, i=mx: eng.tensor_scalar(out=o, in0=i, scalar1=0.0,
                                                                   scalar2=None, op0=ALU.max))
            op("vector", lambda eng, o=acc, a=acc, b=mx: eng.tensor_tensor(out=o, in0=a, in1=b, op=ALU.add))
            if rnd < 7:
                op("vector", lambda eng, o=sbc3, a=wrk3,
                    b=mx.unsqueeze(2).broadcast_to([128, 8, 64]):
                    eng.tensor_tensor(out=o, in0=a, in1=b, op=ALU.is_ge))
                op("vector", lambda eng, o=wrk, i0=sbc[:, 0:512], sc=NEG_BIG, i1=wrk:
                    eng.scalar_tensor_tensor(out=o, in0=i0, scalar=sc, in1=i1, op0=ALU.mult, op1=ALU.add))
        op("vector", lambda eng, o=wdv3, a=esc3,
            b=mx.unsqueeze(2).broadcast_to([128, 8, 64]):
            eng.tensor_tensor(out=o, in0=a, in1=b, op=ALU.is_ge))
        op("vector", lambda eng, o=wdv, a=wdv, b=esc: eng.tensor_tensor(out=o, in0=a, in1=b, op=ALU.mult))
        op("vector", lambda eng, o=Zr, i=Zr: eng.reciprocal(out=o, in_=i))
        op("vector", lambda eng, o=wdv3, a=wdv3,
            b=Zr.unsqueeze(2).broadcast_to([128, 8, 64]):
            eng.tensor_tensor(out=o, in0=a, in1=b, op=ALU.mult))
        t_w = op("vector", lambda eng, o=acc, a=acc, b=Zr:
                 eng.tensor_tensor(out=o, in0=a, in1=b, op=ALU.mult))
        W("tensor", s_ve, t_w)
        t_tr = te(lambda eng, o=PS[1][0:8, 0:128], i=acc: eng.transpose(o, i, id32[:]), inc=True)
        W("vector", s_te, t_tr)
        t_s8 = op("vector", lambda eng, o=sb8, i=PS[1][0:8, 0:128]: eng.tensor_copy(out=o, in_=i))
        W("sync", s_ve, t_s8)
        t_wr_ = dma("sync", s_wld, row("wsum"), sb8)
        evac_tok = {0: None, 1: t_s8}
        for tt in range(8):
            pb = tt % 2
            if evac_tok[pb] is not None:
                W("tensor", s_ve, evac_tok[pb])
            t_tr = te(lambda eng, o=PS[pb][0:64, 0:128], i=wdv[:, tt * 64:(tt + 1) * 64]:
                      eng.transpose(o, i, id32[:]), inc=True)
            W("vector", s_te, t_tr)
            t_e = op("vector", lambda eng, o=wdT[:, tt * 128:(tt + 1) * 128], i=PS[pb][0:64, 0:128]:
                     eng.tensor_copy(out=o, in_=i))
            evac_tok[pb] = t_e
        W("tensor", s_ve, t_e)
        W("vector", s_wld, t_wr_)
        t_Arow = op("vector", lambda eng, o=row("A"), a=row("s2"), b=row("wsum"):
                    eng.tensor_tensor(out=o, in0=a, in1=b, op=ALU.mult))
        W("scalar", s_ve, t_Arow)
        op("scalar", lambda eng, o=rowx(5), i=row("A"): eng.square(out=o, in_=i))
        op("vector", lambda eng, o=rowx(0), a=row("m2"), b=row("A"):
            eng.tensor_tensor(out=o, in0=a, in1=b, op=ALU.mult))
        t_A = op("vector", lambda eng, o=rowx(0), i=rowx(0):
                 eng.tensor_scalar(out=o, in0=i, scalar1=-1.0, scalar2=None, op0=ALU.mult))
        if "tap_misc" in taps:
            W("sync", s_ve, t_A)
            dma("sync", s_wld, tap_misc[:, 0:512], lgT)
            dma("sync", s_wld, tap_misc[:, 512:1024], esc)
            dma("sync", s_wld, tap_misc[:, 1024:1536], wdv)
            dma("sync", s_wld, tap_misc[:, 1536:1568], smx)
            dma("sync", s_wld, tap_misc[:, 2048:3072], mbc[:])
        barrier()

    if phase_limit >= 8:
        W("tensor", s_ve, t_psr6)
        for gfc in range(NFC):
            t_ebmm = te(lambda eng, l=ebxT_b[:, gfc * 64:(gfc + 1) * 64],
                        st=gfc == 0, sp=gfc == NFC - 1:
                        eng.matmul(PS[6][0:64, 0:1], l, onc[:], start=st, stop=sp),
                        inc=gfc == NFC - 1)
        W("vector", s_te, t_ebmm)
        t_ebm = op("vector", lambda eng, o=sqs[0:64, 0:1], i=PS[6][0:64, 0:1]:
                   eng.tensor_copy(out=o, in_=i))
        W("tensor", s_ve, t_ebm)
        for bank, p in ((0, 6), (1, 7)):
            t_emx = te(lambda eng, o=PS[p][0:1, 0:512], l=sqs[0:64, 0:1],
                       r=wdT[:, bank * 512:(bank + 1) * 512]:
                       eng.matmul(o, l, r, start=True, stop=True), inc=bank == 1)
        W("vector", s_te, t_emx)
        for bank, p in ((0, 6), (1, 7)):
            t_m3r = op("vector", lambda eng, o=row("m3")[0:1, bank * 512:(bank + 1) * 512],
                       i=PS[p][0:1, 0:512], cc=1.0 / D:
                       eng.tensor_scalar(out=o, in0=i, scalar1=cc, scalar2=None, op0=ALU.mult))
        W("tensor", s_se, t_gb)
        for bank, p in ((0, 2), (1, 3)):
            t_Q = te(lambda eng, o=PS[p][0:64, 0:512], l=gb,
                     r=wdT[:, bank * 512:(bank + 1) * 512]:
                     eng.matmul(o, l, r, start=True, stop=True), inc=bank == 1)
        op("vector", lambda eng, o=rowx(2), a=row("A"), b=row("A"):
            eng.tensor_tensor(out=o, in0=a, in1=b, op=ALU.add))
        split_bcast(sbc, rowx(2), mp=64)
        W("vector", s_te, t_Q)
        for bank, p4, pq in ((0, 4, 2), (1, 5, 3)):
            op("vector", lambda eng, o=lgsb[:, bank * 512:(bank + 1) * 512],
                a=PS[p4][0:64, 0:512], b=sbc[0:64, bank * 512:(bank + 1) * 512]:
                eng.tensor_tensor(out=o, in0=a, in1=b, op=ALU.mult))
            op("vector", lambda eng, o=lgsb[:, bank * 512:(bank + 1) * 512],
                a=lgsb[:, bank * 512:(bank + 1) * 512], b=PS[pq][0:64, 0:512]:
                eng.tensor_tensor(out=o, in0=a, in1=b, op=ALU.add))
        t_z = op("vector", lambda eng, o=lgsb, a=lgsb, b=wdT:
                 eng.tensor_tensor(out=o, in0=a, in1=b, op=ALU.mult))
        W("scalar", s_ve, t_z)
        t_zb = op("scalar", lambda eng, o=zb, i=lgsb: eng.copy(out=o, in_=i))
        W("tensor", s_se, t_zb)
        for bank, p in ((0, 6), (1, 7)):
            t_ss = te(lambda eng, o=PS[p][0:1, 0:512], r=zb[:, bank * 512:(bank + 1) * 512]:
                      eng.matmul(o, onc[0:64, 0:1], r, start=True, stop=True), inc=bank == 1)
        W("scalar", s_ve, t_m3r)
        t_m3sq = op("scalar", lambda eng, o=rowx(6), i=row("m3"): eng.square(out=o, in_=i))
        op("vector", lambda eng, o=rowx(3), i=rowx(4): eng.reciprocal(out=o, in_=i))
        op("vector", lambda eng, o=rowx(3), i0=rowx(3), sc=-EPS, i1=rowx(5):
            eng.scalar_tensor_tensor(out=o, in0=i0, scalar=sc, in1=i1, op0=ALU.add, op1=ALU.mult))
        W("vector", s_te, t_ss)
        for bank, p in ((0, 6), (1, 7)):
            op("vector", lambda eng, o=rowx(3)[0:1, bank * 512:(bank + 1) * 512],
                i0=PS[p][0:1, 0:512], cc=1.0 / D,
                i1=rowx(3)[0:1, bank * 512:(bank + 1) * 512]:
                eng.scalar_tensor_tensor(out=o, in0=i0, scalar=cc, in1=i1,
                                         op0=ALU.mult, op1=ALU.add))
        op("vector", lambda eng, o=rowx(2), a=rowx(0), b=row("m3"):
            eng.tensor_tensor(out=o, in0=a, in1=b, op=ALU.mult))
        op("vector", lambda eng, o=rowx(3), i0=rowx(2), sc=2.0, i1=rowx(3):
            eng.scalar_tensor_tensor(out=o, in0=i0, scalar=sc, in1=i1, op0=ALU.mult, op1=ALU.add))
        W("vector", s_se, t_m3sq)
        op("vector", lambda eng, o=rowx(3), a=rowx(3), b=rowx(6):
            eng.tensor_tensor(out=o, in0=a, in1=b, op=ALU.subtract))
        op("vector", lambda eng, o=rowx(3), i=rowx(3), e=EPS:
            eng.tensor_scalar(out=o, in0=i, scalar1=e, scalar2=None, op0=ALU.add))
        t_s3 = row_rsqrt(row("s3"), rowx(3))
        op("vector", lambda eng, o=rowx(2), a=rowx(0), b=row("m3"):
            eng.tensor_tensor(out=o, in0=a, in1=b, op=ALU.subtract))
        W("vector", s_se, t_s3)
        op("vector", lambda eng, o=rowx(2), a=rowx(2), b=row("s3"):
            eng.tensor_tensor(out=o, in0=a, in1=b, op=ALU.mult))
        for bank in (0, 1):
            t_rb2 = op("vector", lambda eng, o=rowb2[0:1, bank * 512:(bank + 1) * 512],
                       i=rowx(2)[0:1, bank * 512:(bank + 1) * 512]:
                       eng.tensor_copy(out=o, in_=i))
        split_bcast(sbc, row("s3"), mp=64)
        t_wdm = op("vector", lambda eng, o=wdT, a=wdT, b=sbc[0:64, 0:1024]:
                   eng.tensor_tensor(out=o, in0=a, in1=b, op=ALU.mult))
        op("vector", lambda eng, o=rowx(2), a=row("A"), b=row("s3"):
            eng.tensor_tensor(out=o, in0=a, in1=b, op=ALU.mult))
        op("vector", lambda eng, o=rowx(2), i=rowx(2), e=1.0:
            eng.tensor_scalar(out=o, in0=i, scalar1=e, scalar2=None, op0=ALU.add))
        t_A2 = split_bcast(mbc, rowx(2))
        x1_ring8 = Ring(2)
        t_ps_free8 = {i: None for i in range(4)}
        pend8 = {}

        def issue8(oc_):
            ai_, cond_ = x1_ring8.acquire()
            if cond_ is not None:
                Wl("scalar", cond_)
            pend8[oc_] = (ai_, dma("scalar", s_ao[ai_], aost_ap(ai_), x1_park[oc_]))

        issue8(0)
        for oc in range(NFC):
            ai, t_x1 = pend8.pop(oc)
            if oc + 1 < NFC:
                issue8(oc + 1)
            pp = 0 if oc % 2 == 0 else 2
            if oc == 0:
                W("tensor", s_se, t_ebc)
                W("tensor", s_ve, t_A2)
            for i in (pp, pp + 1):
                if t_ps_free8[i] is not None:
                    W("tensor", *t_ps_free8[i])
                    t_ps_free8[i] = None
            for bank in (0, 1):
                te(lambda eng, o=PS[pp + bank][0:128, 0:512],
                    l=ebb[:, oc * 128:(oc + 1) * 128],
                    r=wdT[:, bank * 512:(bank + 1) * 512]:
                    eng.matmul(o, l, r, start=True, stop=False))
            for bank in (0, 1):
                t_mm = te(lambda eng, o=PS[pp + bank][0:128, 0:512],
                          r=rowb2[0:1, bank * 512:(bank + 1) * 512]:
                          eng.matmul(o, onr[:], r, start=False, stop=True), inc=bank == 1)
            W("vector", s_ao[ai], t_x1)
            W("gpsimd", s_ao[ai], t_x1)
            if oc == 0:
                W("gpsimd", s_ve, t_A2)
            W("vector", s_te, t_mm)
            op("vector", lambda eng, o=ffn_ap(oc)[:, 0:704], a=aost_ap(ai)[:, 0:704],
                b=mbc[:, 0:704]:
                eng.tensor_tensor(out=o, in0=a, in1=b, op=ALU.mult))
            t_mg = op("gpsimd", lambda eng, o=ffn_ap(oc)[:, 704:1024], a=aost_ap(ai)[:, 704:1024],
                      b=mbc[:, 704:1024]:
                      eng.tensor_tensor(out=o, in0=a, in1=b, op=ALU.mult))
            op("vector", lambda eng, o=ffn_ap(oc)[:, 0:512],
                a=ffn_ap(oc)[:, 0:512], b=PS[pp][0:128, 0:512]:
                eng.tensor_tensor(out=o, in0=a, in1=b, op=ALU.add))
            W("vector", s_ge, t_mg)
            t_f = op("vector", lambda eng, o=ffn_ap(oc)[:, 512:1024],
                     a=ffn_ap(oc)[:, 512:1024], b=PS[pp + 1][0:128, 0:512]:
                     eng.tensor_tensor(out=o, in0=a, in1=b, op=ALU.add))
            x1_ring8.release_multi(ai, [(s_ve, t_f), (s_ge, t_mg)])
            t_ps_free8[pp] = (s_ve, t_f)
            t_ps_free8[pp + 1] = (s_ve, t_f)
            W("sync", s_ve, t_f)
            dma("sync", s_out, yT[oc], ffn_ap(oc))
        on("sync", lambda eng: eng.wait_ge(s_out.h, s_out.n))
    else:
        on("sync", lambda eng: eng.wait_ge(s_bar.h, s_bar.n))
        dma("sync", s_out, yT[0], mbc[:])
        on("sync", lambda eng: eng.wait_ge(s_out.h, s_out.n))

    for e, sems in DMA_SEMS.items():
        for s in sems:
            if s.n:
                W(e, s, s.n)

    with stack:
        with nc.Block() as block:
            @block.sync
            def _(eng):
                for f in prog["sync"]:
                    f(eng)

            @block.tensor
            def _(eng):
                for f in prog["tensor"]:
                    f(eng)

            @block.vector
            def _(eng):
                for f in prog["vector"]:
                    f(eng)

            @block.scalar
            def _(eng):
                for f in prog["scalar"]:
                    f(eng)

            @block.gpsimd
            def _(eng):
                for f in prog["gpsimd"]:
                    f(eng)

    return nc



def _prep_shared(qkv_w, out_w, gate_w, norm_w, rw, rb, eb):
    f32 = np.float32
    A1 = np.ascontiguousarray(qkv_w.reshape(32, 3, 128, 32, 128))
    wqk_ = np.ascontiguousarray(np.transpose(A1[:, 0:2], (0, 1, 4, 3, 2))).reshape(32, 2, 128, 4096)
    B = A1[:, 2]
    C2 = B.reshape(8, 4, 128, 8, 4, 128)
    wv_ = np.ascontiguousarray(np.transpose(C2, (0, 3, 5, 4, 1, 2))).reshape(8, 8, 128, 2048)
    G = gate_w.reshape(32, 128, 32, 128)
    wg_ = np.ascontiguousarray(np.transpose(G, (0, 3, 2, 1))).reshape(32, 128, 4096)
    O = out_w.reshape(32, 128, 32, 128)
    wo_ = np.ascontiguousarray(np.transpose(O, (0, 3, 2, 1))).reshape(32, 128, 4096)
    Dm = rw.reshape(64, 32, 128)
    wr_ = np.ascontiguousarray(np.transpose(Dm, (2, 1, 0))).reshape(128, 2048)
    nwc_ = np.ascontiguousarray(norm_w.reshape(32, 128).T)
    rbc_ = np.ascontiguousarray(rb.reshape(64, 1))
    slope, qd, kdv, ddT, bdv = _decay_tables()
    qdb_ = np.ascontiguousarray(np.broadcast_to(qd[:, None, :], (32, 128, 256))).astype(BF16NP)
    ddtb_ = np.ascontiguousarray(ddT.reshape(32, 2, 128, 256)).astype(BF16NP)
    kdc_ = np.ascontiguousarray(kdv.reshape(32, 2, 128).transpose(2, 0, 1)).reshape(128, 64)
    kdc_ = np.ascontiguousarray(kdc_).astype(f32)
    ebxT_ = np.ascontiguousarray(eb.T.reshape(32, 128, 64).transpose(1, 0, 2)).reshape(128, 2048)
    return dict(wqk=wqk_.astype(f32), wv=wv_.astype(f32), wg=wg_.astype(f32), wo=wo_.astype(f32),
                wr=wr_.astype(f32), ebx=np.ascontiguousarray(eb).astype(f32),
                ebxT=ebxT_.astype(f32),
                nwc=nwc_.astype(f32), rbc=rbc_.astype(f32), qdb=qdb_, ddtb=ddtb_, kdc=kdc_)


def _prep_core_inputs(x_full, shared):
    maps = []
    for c in range(8):
        j, p = c // 2, c % 2
        xs = x_full[j, p * 1024:(p + 1) * 1024, :]
        xTt = np.ascontiguousarray(xs.T).reshape(32, 128, 1024).astype(np.float32)
        m = dict(shared)
        m["xT"] = xTt
        m["rmask"] = np.full((128, 1), 1.0 if p == 1 else 0.0, np.float32)
        maps.append(m)
    return maps


def kernel(L0_attn_hidden_states, L0_attn_qkv_w, L0_attn_out_w, L0_attn_gate_w,
           L0_attn_norm_w, L0_ffn_router_weight, L0_ffn_router_bias, L0_ffn_expert_bias):
    x = np.asarray(L0_attn_hidden_states, np.float32)
    shared = _prep_shared(np.asarray(L0_attn_qkv_w, np.float32),
                          np.asarray(L0_attn_out_w, np.float32),
                          np.asarray(L0_attn_gate_w, np.float32),
                          np.asarray(L0_attn_norm_w, np.float32),
                          np.asarray(L0_ffn_router_weight, np.float32),
                          np.asarray(L0_ffn_router_bias, np.float32),
                          np.asarray(L0_ffn_expert_bias, np.float32))
    in_maps = _prep_core_inputs(x, shared)
    nc = build_nc(PHASE_LIMIT, TAPS)
    res = run_bass_kernel_spmd(nc, in_maps, core_ids=list(range(8)))
    out = np.empty((4, 2048, 4096), np.float32)
    for c in range(8):
        j, p = c // 2, c % 2
        yt = res.results[c]["yT"].reshape(4096, 1024)
        out[j, p * 1024:(p + 1) * 1024, :] = yt.T
    return out


# version 22
# speedup vs baseline: 1.0836x; 1.0038x over previous
import math
import numpy as np
import ml_dtypes
import concourse.bass as bass
import concourse.mybir as mybir
from concourse.bass_utils import run_bass_kernel_spmd

F32 = mybir.dt.float32
BF16 = mybir.dt.bfloat16
AF = mybir.ActivationFunctionType
ALU = mybir.AluOpType
AX = mybir.AxisListType
BF16NP = ml_dtypes.bfloat16

D = 4096
TPC = 1024
NFC = 32
NH = 32
DH = 128
BS = 256
NB = 4
NG = 8
GH = 4
HEADS, LAYER_IDX, N_LAYERS = 32, 1, 32
EPS = 1e-5
NEG_BIG = -1e30

PHASE_LIMIT = 99
TAPS = ()


def _decay_tables():
    base = np.float32(1.0 / 2.0 ** (8.0 / HEADS))
    expo = np.arange(HEADS, dtype=np.float32) + np.float32(1.0)
    factor = np.float32(1.0 - LAYER_IDX / (N_LAYERS - 1 + 1e-05) + 1e-05)
    slope = (np.power(base, expo) * factor).astype(np.float32)
    r = np.arange(BS, dtype=np.float32) + 1.0
    qd = np.exp(-slope[:, None] * r[None, :]).astype(np.float32)
    kd = np.exp(-slope[:, None] * (np.float32(BS) - r)[None, :]).astype(np.float32)
    diff = (r[:, None] - r[None, :]).astype(np.float32)
    expo2 = -slope[:, None, None] * np.maximum(diff, 0.0)[None]
    dd = np.where(diff[None] >= 0, np.exp(expo2), np.float32(0.0)).astype(np.float32)
    ddT = np.ascontiguousarray(np.transpose(dd, (0, 2, 1)))
    bd = np.exp(-slope * np.float32(BS)).astype(np.float32)
    return slope, qd, kd, ddT, bd


class Sem:
    def __init__(self, nc, name):
        self._cm = nc.semaphore(name)
        self.h = self._cm.__enter__()
        self.n = 0

    def inc(self, k=1):
        self.n += k
        return self.n


class Ring:
    def __init__(self, n):
        self.n = n
        self.free = [None] * n
        self.i = 0

    def acquire(self):
        idx = self.i % self.n
        self.i += 1
        cond = self.free[idx]
        self.free[idx] = None
        return idx, cond

    def release(self, idx, sem, target):
        self.free[idx] = [(sem, target)]

    def release_multi(self, idx, conds):
        self.free[idx] = list(conds)


def build_nc(phase_limit=99, taps=()):
    nc = bass.Bass("TRN2", target_bir_lowering=False, debug=False)
    slope, qd_t, kd_t, ddT_t, bd_t = _decay_tables()
    bd = [float(x) for x in bd_t]

    xT = nc.dram_tensor("xT", [NFC, 128, TPC], F32, kind="ExternalInput").ap()
    wqk = nc.dram_tensor("wqk", [NH, 2, 128, 4096], F32, kind="ExternalInput").ap()
    wv = nc.dram_tensor("wv", [NG, 8, 128, 2048], F32, kind="ExternalInput").ap()
    wg = nc.dram_tensor("wg", [NFC, 128, 4096], F32, kind="ExternalInput").ap()
    wo = nc.dram_tensor("wo", [NFC, 128, 4096], F32, kind="ExternalInput").ap()
    wr = nc.dram_tensor("wr", [128, 2048], F32, kind="ExternalInput").ap()
    ebx = nc.dram_tensor("ebx", [64, 4096], F32, kind="ExternalInput").ap()
    ebxTd = nc.dram_tensor("ebxT", [128, 2048], F32, kind="ExternalInput").ap()
    nwc = nc.dram_tensor("nwc", [128, 32], F32, kind="ExternalInput").ap()
    rbc = nc.dram_tensor("rbc", [64, 1], F32, kind="ExternalInput").ap()
    qdb = nc.dram_tensor("qdb", [NH, 128, 256], BF16, kind="ExternalInput").ap()
    ddtb = nc.dram_tensor("ddtb", [NH, 2, 128, 256], BF16, kind="ExternalInput").ap()
    kdc = nc.dram_tensor("kdc", [128, 64], F32, kind="ExternalInput").ap()
    rmask = nc.dram_tensor("rmask", [128, 1], F32, kind="ExternalInput").ap()
    yT = nc.dram_tensor("yT", [NFC, 128, TPC], F32, kind="ExternalOutput").ap()

    def park(name):
        kind = "ExternalOutput" if name in taps else "Internal"
        return nc.dram_tensor(name, [NFC, 128, TPC], F32, kind=kind).ap()

    ao_park = park("tap_attn")
    ao2_park = park("tap_attnout")
    x1_park = park("tap_x1")
    m1_park = nc.dram_tensor("m1_park", [1, 1024], F32, kind="Internal")
    cci = [nc.dram_tensor(f"cci{g}", [128, 512], F32, kind="Internal") for g in range(NG)]
    cco = [nc.dram_tensor(f"cco{g}", [2, 128, 512], F32, kind="Internal") for g in range(NG)]
    tap_misc = nc.dram_tensor("tap_misc", [128, 4096], F32,
                              kind="ExternalOutput" if "tap_misc" in taps else "Internal").ap()

    def tapb(name, shape):
        if name not in taps:
            return None
        return nc.dram_tensor(name, shape, BF16, kind="ExternalOutput").ap()

    tap_h1 = tapb("tap_h1", [128, 32768])
    tap_gated = tapb("tap_gated", [128, 32768])
    tap_h2 = tapb("tap_h2", [128, 32768])
    tap_kt = tapb("tap_kt", [128, 8192])
    tap_ffn = nc.dram_tensor("tap_ffn", [2, 128, 16384], F32,
                             kind="ExternalOutput").ap() if "tap_ffn" in taps else None

    ENGS = ("sync", "tensor", "vector", "scalar", "gpsimd")
    prog = {e: [] for e in ENGS}

    import contextlib
    stack = contextlib.ExitStack()

    def sb(name, shape, dtype):
        return stack.enter_context(nc.sbuf_tensor(name, shape, dtype))

    def ps_(name, shape, dtype=F32):
        return stack.enter_context(nc.psum_tensor(name, shape, dtype))

    ar1 = sb("ar1", [128, 32768], BF16)
    ar2 = sb("ar2", [128, 32768], BF16)
    land = sb("land", [128, 4096], F32)
    wb = sb("wb", [128, 8192], BF16)
    mbc = sb("mbc", [128, 1024], F32)
    sbc = sb("sbc", [128, 1024], F32)
    aost = sb("aost", [128, 2048], F32)
    sqs = sb("sqs", [128, 1024], BF16)
    wsumcol = sb("wsumcol", [64, 1], F32)
    kvrun = sb("kvrun", [128, 512], F32)
    rows = sb("rows", [1, 5120], F32)
    rowb = sb("rowb", [1, 2048], BF16)
    rowb2 = sb("rowb2", [1, 1024], BF16)
    onc = sb("onc", [128, 1], BF16)
    onr = sb("onr", [1, 128], BF16)
    idb = sb("idb", [128, 128], BF16)
    id32 = sb("id32", [128, 128], F32)
    nw = sb("nw", [128, 32], F32)
    rb64 = sb("rb64", [64, 1], F32)
    kd = sb("kd", [128, 64], F32)
    msk = sb("msk", [128, 1], F32)
    ebb = wb[0:64, 0:4096]
    kv0 = ar2[:, 24960:25984].bitcast(F32)
    sqs2 = ar1[:, 26624:27648]
    sqs2b = land[:].bitcast(BF16)[0:128, 4096:5120]
    wdT = kvrun[:].bitcast(BF16)[0:64, 0:1024]
    sb8 = ar2[:].bitcast(F32)[0:8, 2048:2176]
    ebxT_b = ar2[:, 8192:10240]
    gb = wb[0:64, 4096:4160]
    zb = land[:].bitcast(BF16)[0:64, 4096:5120]

    KT0 = 0
    VS0 = 8192
    KL0 = 16384
    QC0 = 19456
    QS0 = 20480
    SC0 = 21504
    KP0 = 22528
    DD0 = 23040
    QD0 = 24064
    KV2 = 24576
    ZKV = 24832
    QC1 = 25984
    QS1 = 27008

    def kt_ap(slab, hh):
        o = KT0 + slab * 4096 + hh * 1024
        return ar2[:, o:o + 1024]

    def vs_ap(slab):
        return ar2[:, VS0 + slab * 4096: VS0 + (slab + 1) * 4096]

    def kvloc_ap(slab, hh, b):
        o = KL0 + slab * 1536 + hh * 384 + b * 128
        return ar2[:, o:o + 128]

    def gated_ap(fc):
        return ar2[:, fc * 1024:(fc + 1) * 1024]

    def h1_ap(fc):
        return ar1[:, fc * 1024:(fc + 1) * 1024]

    h2_ap = h1_ap
    ar1_f32 = ar1[:].bitcast(F32)
    ar2_f32 = ar2[:].bitcast(F32)

    def ffn_ap(fc):
        if fc < 16:
            return ar1_f32[:, fc * 1024:(fc + 1) * 1024]
        return ar2_f32[:, (fc - 16) * 1024:(fc - 15) * 1024]

    lgsb = ar1_f32[0:64, 0:1024]
    lgT = ar1_f32[:, 1024:1536]
    esc = ar1_f32[:, 1536:2048]
    wrk = ar1_f32[:, 2048:2560]
    wdv = ar1_f32[:, 2560:3072]
    smx = ar1_f32[:, 3072:3104]
    x0r_ap = [ar2_f32[:, 0:1024], ar2_f32[:, 1024:2048]]
    x1s_ap = [ar2_f32[:, 2048:3072], ar2_f32[:, 3072:4096]]

    _ROWSLOT = {"RA": 0, "RB": 1, "RC": 2, "RD": 3, "RT1": 4, "RT2": 3}
    _ROWALIAS = {"m1": "RA", "s1": "RB", "m4": "RA", "s4": "RB", "m2": "RA", "s2": "RB",
                 "m3": "RA", "s3": "RB", "rsms": "RB", "ssq_attn": "RC", "wsum": "RC",
                 "A": "RD", "negB": "RT1", "tmsq": "RT1", "tm2": "RT2", "tvar": "RT1"}

    def row(name):
        o = _ROWSLOT[_ROWALIAS.get(name, name)] * 1024
        return rows[0:1, o:o + 1024]

    def rowx(k):
        return ar1_f32[0:1, 4096 + k * 1024: 5120 + k * 1024]

    PS = [ps_(f"ps{i}", [128, 512], F32) for i in range(8)]
    ps7b = PS[7][:].bitcast(BF16)

    s_wld = Sem(nc, "s_wld")
    s_gld = Sem(nc, "s_gld")
    s_te = Sem(nc, "s_te")
    s_ve = Sem(nc, "s_ve")
    s_se = Sem(nc, "s_se")
    s_ge = Sem(nc, "s_ge")
    s_cc = Sem(nc, "s_cc")
    s_bar = Sem(nc, "s_bar")
    s_out = Sem(nc, "s_out")
    s_land = [Sem(nc, f"s_land{i}") for i in range(2)]
    s_lh = [Sem(nc, f"s_lh{i}") for i in range(4)]
    s_ao = [Sem(nc, f"s_ao{i}") for i in range(2)]
    s_x0 = [Sem(nc, f"s_x0{i}") for i in range(2)]
    s_x1s = [Sem(nc, f"s_x1s{i}") for i in range(2)]
    s_qd = [Sem(nc, f"s_qd{i}") for i in range(2)]
    s_dd = [Sem(nc, f"s_dd{i}") for i in range(2)]
    s_kv0r = Sem(nc, "s_kv0r")
    s_cci = Sem(nc, "s_cci")

    CHAIN = {"vector": s_ve, "scalar": s_se, "gpsimd": s_ge}
    DMA_SEMS = {"sync": [s_wld, *s_land, *s_lh, *s_x0],
                "scalar": [*s_ao, *s_x1s],
                "gpsimd": [s_gld, *s_qd, *s_dd, s_kv0r, s_cci]}

    def on(e, fn):
        prog[e].append(fn)

    def W(e, sem, tgt):
        if tgt is None or tgt <= 0:
            return
        on(e, lambda eng, s=sem.h, t=tgt: eng.wait_ge(s, t))

    def Wl(e, conds):
        for sem, tgt in conds:
            W(e, sem, tgt)

    def op(e, fn):
        ch = CHAIN[e]
        prev = ch.n
        tgt = ch.inc()
        def emit(eng, f=fn, s=ch.h, p=prev):
            if p > 0:
                eng.wait_ge(s, p)
            f(eng).then_inc(s, 1)
        prog[e].append(emit)
        return tgt

    def te(fn, inc=False):
        if inc:
            tgt = s_te.inc()
            prog["tensor"].append(lambda eng, f=fn: f(eng).then_inc(s_te.h, 1))
            return tgt
        prog["tensor"].append(lambda eng, f=fn: f(eng))
        return None

    def dma(e, sem, out, in_):
        on(e, lambda eng, o=out, i=in_, s=sem.h: eng.dma_start(out=o, in_=i).then_inc(s, 16))
        return sem.inc(16)

    def barrier():
        for e, sems in DMA_SEMS.items():
            for s in sems:
                if s.n:
                    W(e, s, s.n)
        W("tensor", s_te, s_te.n)
        for e, ch in CHAIN.items():
            W(e, ch, ch.n)
        for e in ENGS:
            on(e, lambda eng, s=s_bar.h: eng.nop().then_inc(s, 1))
        s_bar.n += 5
        t = s_bar.n
        for e in ENGS:
            on(e, lambda eng, s=s_bar.h, t=t: eng.wait_ge(s, t))

    land_ring = Ring(2)
    aost_ring = Ring(2)
    wb_ring = Ring(4)

    def land_ap(i):
        return land[:, i * 2048:(i + 1) * 2048]

    def wb_ap(i):
        return wb[:, i * 2048:(i + 1) * 2048]

    def aost_ap(i):
        return aost[:, i * 1024:(i + 1) * 1024]

    def load_cast_ap(src_ap, dst_ap, ncols=2048, eng_name="scalar", wb_cond=None):
        ch = CHAIN[eng_name]
        li, cond = land_ring.acquire()
        if cond is not None:
            Wl("sync", cond)
        t_ld = dma("sync", s_land[li], land_ap(li)[:, 0:ncols], src_ap)
        if wb_cond is not None:
            Wl(eng_name, wb_cond)
        W(eng_name, s_land[li], t_ld)
        t_cast = op(eng_name, lambda eng, o=dst_ap[:, 0:ncols], i=land_ap(li)[:, 0:ncols]:
                    (eng.copy(out=o, in_=i) if hasattr(eng, "copy") else eng.tensor_copy(out=o, in_=i)))
        land_ring.release(li, ch, t_cast)
        return (ch, t_cast)

    def load_cast(src_ap, wb_idx, ncols=2048, eng_name="scalar", wb_cond=None):
        return load_cast_ap(src_ap, wb_ap(wb_idx), ncols=ncols, eng_name=eng_name, wb_cond=wb_cond)

    def row_rsqrt(dst_row, var_row):
        tv = op("vector", lambda eng, o=dst_row, i=var_row: eng.reciprocal(out=o, in_=i))
        W("scalar", s_ve, tv)
        return op("scalar", lambda eng, o=dst_row, i=dst_row: eng.sqrt(out=o, in_=i))

    def split_bcast(dst_bc, src_row, t_src_se=None, psA=0, psB=1, mp=128):
        if t_src_se is not None:
            W("vector", s_se, t_src_se)
        op("vector", lambda eng, o=rowb[0:1, 0:1024], i=src_row: eng.tensor_copy(out=o, in_=i))
        op("vector", lambda eng, o=row("tmsq"), a=src_row, b=rowb[0:1, 0:1024]:
            eng.tensor_tensor(out=o, in0=a, in1=b, op=ALU.subtract))
        t_rows = op("vector", lambda eng, o=rowb[0:1, 1024:2048], i=row("tmsq"):
                    eng.tensor_copy(out=o, in_=i))
        W("tensor", s_ve, t_rows)
        for bank, p in ((0, psA), (1, psB)):
            te(lambda eng, o=PS[p][0:mp, 0:512], hi=rowb[0:1, bank * 512:(bank + 1) * 512]:
                eng.matmul(o, onr[0:1, 0:mp], hi, start=True, stop=False))
            last = bank == 1
            t_mm = te(lambda eng, o=PS[p][0:mp, 0:512],
                      lo=rowb[0:1, 1024 + bank * 512:1024 + (bank + 1) * 512]:
                      eng.matmul(o, onr[0:1, 0:mp], lo, start=False, stop=True), inc=last)
        W("vector", s_te, t_mm)
        for bank, p in ((0, psA), (1, psB)):
            t_bc = op("vector", lambda eng, o=dst_bc[0:mp, bank * 512:(bank + 1) * 512],
                      i=PS[p][0:mp, 0:512]: eng.tensor_copy(out=o, in_=i))
        return t_bc

    def finalize_ln_rows(m_row, s_row, ps_sum=(4, 5), ps_ssq=(6, 7), t_mm=None, rms_only=False,
                         ssq_row_src=None, skip_m=False):
        if t_mm is not None:
            W("vector", s_te, t_mm)
        c = 1.0 / D
        if not rms_only:
            if not skip_m:
                for bank, p in zip((0, 1), ps_sum):
                    op("vector", lambda eng, o=m_row[0:1, bank * 512:(bank + 1) * 512],
                        i=PS[p][0:1, 0:512], cc=c:
                        eng.tensor_scalar(out=o, in0=i, scalar1=cc, scalar2=None, op0=ALU.mult))
            for bank, p in zip((0, 1), ps_ssq):
                op("vector", lambda eng, o=row("tmsq")[0:1, bank * 512:(bank + 1) * 512].squeeze().unsqueeze(0) if False else rows[0:1, _ROWSLOT["RT1"] * 1024 + bank * 512: _ROWSLOT["RT1"] * 1024 + (bank + 1) * 512],
                    i=PS[p][0:1, 0:512], cc=c:
                    eng.tensor_scalar(out=o, in0=i, scalar1=cc, scalar2=None, op0=ALU.mult))
            op("vector", lambda eng, o=row("tm2"), a=m_row, b=m_row:
                eng.tensor_tensor(out=o, in0=a, in1=b, op=ALU.mult))
            t_ps = op("vector", lambda eng, o=row("tvar"), a=row("tmsq"), b=row("tm2"):
                      eng.tensor_tensor(out=o, in0=a, in1=b, op=ALU.subtract))
        else:
            t_ps = op("vector", lambda eng, o=row("tvar"), i=ssq_row_src, cc=c:
                      eng.tensor_scalar(out=o, in0=i, scalar1=cc, scalar2=None, op0=ALU.mult))
        op("vector", lambda eng, o=row("tvar"), i=row("tvar"), e=EPS:
            eng.tensor_scalar(out=o, in0=i, scalar1=e, scalar2=None, op0=ALU.add))
        return row_rsqrt(s_row, row("tvar")), t_ps

    dma("sync", s_wld, nw[:], nwc)
    dma("sync", s_wld, rb64[:], rbc)
    dma("sync", s_wld, kd[:], kdc)
    dma("sync", s_wld, msk[:], rmask)
    op("vector", lambda eng: eng.memset(onc[:], 1.0))
    op("vector", lambda eng: eng.memset(onr[:], 1.0))
    op("vector", lambda eng: eng.memset(ar2[:, ZKV:ZKV + 128], 0.0))
    op("vector", lambda eng: eng.memset(idb[:], 0.0))
    op("vector", lambda eng: eng.memset(id32[:], 0.0))
    t_ms = op("vector", lambda eng: eng.memset(row("ssq_attn"), 0.0))
    W("gpsimd", s_ve, t_ms)
    for t_ in (idb, id32):
        op("gpsimd", lambda eng, x=t_: eng.affine_select(
            out=x[:], in_=x[:], compare_op=ALU.not_equal, fill=1.0, base=0,
            pattern=[[-1, 128]], channel_multiplier=1))
    barrier()

    if phase_limit >= 0:
        half_ring = Ring(4)

        def landh_ap(i):
            return land[:, i * 1024:(i + 1) * 1024]

        t_fc = None
        for fc in range(NFC):
            q0 = "sync" if fc % 2 == 0 else "gpsimd"
            hi, cond = half_ring.acquire()
            if cond is not None:
                Wl(q0, cond)
            t_ld = dma(q0, s_lh[hi], landh_ap(hi), xT[fc])
            W("scalar", s_lh[hi], t_ld)
            t_cp = op("scalar", lambda eng, o=h1_ap(fc), i=landh_ap(hi): eng.copy(out=o, in_=i))
            half_ring.release(hi, s_se, t_cp)
            if t_fc is not None:
                W("scalar", s_te, t_fc)
            t_sq = op("scalar", lambda eng, o=sqs[:], i=h1_ap(fc):
                      eng.activation(out=o, in_=i, func=AF.Square))
            st, sp = fc == 0, fc == NFC - 1
            W("tensor", s_se, t_cp)
            for bank, p in ((0, 4), (1, 5)):
                te(lambda eng, o=PS[p][0:1, 0:512], r=h1_ap(fc)[:, bank * 512:(bank + 1) * 512],
                    st=st, sp=sp: eng.matmul(o, onc[:], r, start=st, stop=sp))
            W("tensor", s_se, t_sq)
            for bank, p in ((0, 6), (1, 7)):
                t_fc = te(lambda eng, o=PS[p][0:1, 0:512], r=sqs[:, bank * 512:(bank + 1) * 512],
                          st=st, sp=sp: eng.matmul(o, onc[:], r, start=st, stop=sp), inc=bank == 1)
        t_stats = t_fc
        (t_s1, _) = finalize_ln_rows(row("m1"), row("s1"), t_mm=t_stats)
        split_bcast(mbc, row("m1"))
        t_sb = split_bcast(sbc, row("s1"), t_src_se=t_s1)
        norm_t = []
        for fc in range(NFC):
            ai, cond = aost_ring.acquire()
            if cond is not None:
                Wl("vector", cond)
            op("vector", lambda eng, o=aost_ap(ai), a=h1_ap(fc), b=mbc[:]:
                eng.tensor_tensor(out=o, in0=a, in1=b, op=ALU.subtract))
            t_n = op("vector", lambda eng, o=h1_ap(fc), a=aost_ap(ai), b=sbc[:]:
                     eng.tensor_tensor(out=o, in0=a, in1=b, op=ALU.mult))
            norm_t.append(t_n)
            aost_ring.release(ai, s_ve, t_n)
        W("sync", s_ve, t_sb)
        dma("sync", s_wld, m1_park.ap(), row("m1"))
        t_cp_last_ph0 = s_se.n
        t_sb_ph0 = t_sb
        t_stats_ph0 = t_stats
        t_norm_last = norm_t[-1]
        if tap_h1 is not None:
            W("sync", s_ve, t_norm_last)
            dma("sync", s_out, tap_h1, ar1[:])
            on("sync", lambda eng, t=s_out.n: eng.wait_ge(s_out.h, t))

    if phase_limit >= 1:
        t_ps_free = {i: None for i in range(8)}
        if phase_limit >= 0:
            for i in (0, 1, 4, 5, 6, 7):
                t_ps_free[i] = (s_ve, t_sb_ph0)
            land_ring.free[0] = [(s_se, t_cp_last_ph0)]
            land_ring.free[1] = [(s_se, t_cp_last_ph0)]

        def te_wait_ps(i):
            if t_ps_free[i] is not None:
                W("tensor", *t_ps_free[i])
                t_ps_free[i] = None

        kt_ready = {}
        vs_ready = {}
        t_rw_prev = [t_stats_ph0 if phase_limit >= 0 else None]
        qs_guard = {0: None, 1: None}
        kvrun_free = [None]
        stageq = []
        XW_APS = [ar2[:, 28032:30080], ar2[:, 30080:32128]]
        pend_extra = [None]
        extra_free = [None]

        def pop_stage():
            if stageq:
                stageq.pop(0)()
        kv0_free = [None]
        qd_free = {0: None, 1: None}
        dd_free = {0: None, 1: None}

        def proj_head(waps, t_casts, dst_evac, pp, fc_waits=None):
            for tc_ in t_casts:
                W("tensor", *tc_)
            te_wait_ps(pp)
            te_wait_ps(pp + 1)
            t_mm = None
            for fc in range(NFC):
                if fc_waits is not None:
                    W("tensor", s_ve, fc_waits[fc])
                lt = waps[fc // 16][:, (fc % 16) * 128:(fc % 16 + 1) * 128]
                st, sp = fc == 0, fc == NFC - 1
                for bank in (0, 1):
                    t_mm = te(lambda eng, o=PS[pp + bank][0:128, 0:512], l=lt,
                              r=h1_ap(fc)[:, bank * 512:(bank + 1) * 512], st=st, sp=sp:
                              eng.matmul(o, l, r, start=st, stop=sp),
                              inc=(sp and bank == 1))
                if fc % 4 == 3:
                    pop_stage()
            W("scalar", s_te, t_mm)
            for bank in (0, 1):
                t_ev = op("scalar", lambda eng, o=dst_evac[:, bank * 512:(bank + 1) * 512],
                          i=PS[pp + bank][0:128, 0:512]:
                          eng.activation(out=o, in_=i, func=AF.Silu))
            t_ps_free[pp] = (s_se, t_ev)
            t_ps_free[pp + 1] = (s_se, t_ev)
            return t_ev, t_mm

        for g in range(NG + 1):
            slab = g % 2
            if g < NG:
                def load_pair(src2):
                    slots, tcs = [], []
                    for halfw in range(2):
                        wi, cond = wb_ring.acquire()
                        if cond is not None:
                            Wl("sync", cond)
                        tcs.append(load_cast(src2.rearrange("p (a c) -> a p c", a=2)[halfw], wi,
                                             wb_cond=cond))
                        slots.append(wi)
                    return slots, tcs

                if pend_extra[0] is not None:
                    pend = pend_extra[0]
                    pend_extra[0] = None
                    extra_used = True
                else:
                    pend = load_pair(wqk[GH * g, 1])
                    extra_used = False
                for hh in range(GH):
                    h = GH * g + hh
                    slots, tcs = pend
                    if hh + 1 < GH:
                        pend = load_pair(wqk[h + 1, 1])
                    aps = [XW_APS[0], XW_APS[1]] if (hh == 0 and extra_used) else [wb_ap(wi) for wi in slots]
                    t_ev, t_kmm = proj_head(aps, tcs, kt_ap(slab, hh), pp=(0 if hh % 2 == 0 else 2),
                                            fc_waits=(norm_t if (g == 0 and hh == 0 and phase_limit >= 0) else None))
                    kt_ready[(slab, hh)] = t_ev
                    if hh == 0 and extra_used:
                        extra_free[0] = (s_te, t_kmm)
                    else:
                        for wi in slots:
                            wb_ring.release(wi, s_te, t_kmm)
                for i in range(8):
                    te_wait_ps(i)
                vslots, vtcs = {}, {}
                for chn in range(2):
                    wi, cond = wb_ring.acquire()
                    if cond is not None:
                        Wl("sync", cond)
                    vtcs[chn] = load_cast(wv[g, chn], wi, wb_cond=cond)
                    vslots[chn] = wi
                for fc in range(NFC):
                    chn = fc // 4
                    if fc % 4 == 0:
                        if chn + 2 < 8:
                            wi, cond = wb_ring.acquire()
                            if cond is not None:
                                Wl("sync", cond)
                            vtcs[chn + 2] = load_cast(wv[g, chn + 2], wi, wb_cond=cond)
                            vslots[chn + 2] = wi
                        W("tensor", *vtcs[chn])
                    sl = vslots[chn]
                    rt = wb_ap(sl)[:, (fc % 4) * 512:(fc % 4 + 1) * 512]
                    st, sp = fc == 0, fc == NFC - 1
                    for tcch in range(8):
                        o = PS[tcch][0:128, 0:512]
                        t_vmm = te(lambda eng, o=o, l=h1_ap(fc)[:, tcch * 128:(tcch + 1) * 128],
                                   r=rt, st=st, sp=sp: eng.matmul(o, l, r, start=st, stop=sp),
                                   inc=(fc % 4 == 3 and tcch == 7))
                    if fc % 4 == 3:
                        wb_ring.release(sl, s_te, t_vmm)
                W("scalar", s_te, t_vmm)
                for tcch in range(8):
                    t_vev = op("scalar", lambda eng,
                               o=vs_ap(slab)[:, tcch * 512: tcch * 512 + 512],
                               i=PS[tcch][0:128, 0:512]:
                               eng.activation(out=o, in_=i, func=AF.Silu))
                vs_ready[slab] = t_vev
                for i in range(8):
                    t_ps_free[i] = (s_se, t_vev)
                ps_trb = PS[0][:].bitcast(BF16)
                for hh in range(GH):
                    h = GH * g + hh
                    t_trfree = {i: None for i in range(4)}
                    t_cmm = {}
                    tkp = {}
                    W("tensor", s_se, kt_ready[(slab, hh)])
                    W("tensor", s_se, vs_ready[slab])
                    te_wait_ps(0)
                    for u in range(8):
                        bb, halfm = u // 2, u % 2
                        sl = u % 4
                        if t_trfree[sl] is not None:
                            W("tensor", *t_trfree[sl])
                            t_trfree[sl] = None
                        if u >= 4:
                            W("vector", s_te, t_cmm[u // 2 - 2])
                        t_tr = te(lambda eng, o=ps_trb[0:128, sl * 128:(sl + 1) * 128],
                                  i=kt_ap(slab, hh)[:, bb * 256 + halfm * 128: bb * 256 + halfm * 128 + 128]:
                                  eng.transpose(o, i, idb[:]), inc=True)
                        W("vector", s_te, t_tr)
                        tkp[u] = op("vector", lambda eng,
                                    o=ar2[:, KP0 + sl * 128: KP0 + (sl + 1) * 128],
                                    a=ps_trb[0:128, sl * 128:(sl + 1) * 128],
                                    b=kd[:, 2 * h + halfm:2 * h + halfm + 1].broadcast_to([128, 128]):
                                    eng.tensor_tensor(out=o, in0=a, in1=b, op=ALU.mult))
                        t_trfree[sl] = (s_ve, tkp[u])
                        if u % 2 == 1:
                            bb2 = u // 2
                            te_wait_ps(4 + bb2)
                            W("tensor", s_ve, tkp[u])
                            for halfm2 in range(2):
                                t_cmm[bb2] = te(lambda eng, o=PS[4 + bb2][0:128, 0:128],
                                                l=ar2[:, KP0 + ((2 * bb2 + halfm2) % 4) * 128: KP0 + ((2 * bb2 + halfm2) % 4) * 128 + 128],
                                                r=vs_ap(slab)[:, (2 * bb2 + halfm2) * 512 + hh * 128:(2 * bb2 + halfm2) * 512 + hh * 128 + 128],
                                                st=halfm2 == 0, sp=halfm2 == 1:
                                                eng.matmul(o, l, r, start=st, stop=sp), inc=halfm2 == 1)
                    if kvrun_free[0] is not None:
                        W("vector", *kvrun_free[0])
                        if hh == GH - 1:
                            kvrun_free[0] = None
                    W("vector", s_te, t_cmm[0])
                    op("vector", lambda eng, o=kvrun[:, hh * 128:(hh + 1) * 128], i=PS[4][0:128, 0:128]:
                        eng.tensor_copy(out=o, in_=i))
                    op("vector", lambda eng, o=kvloc_ap(slab, hh, 0), i=kvrun[:, hh * 128:(hh + 1) * 128]:
                        eng.tensor_copy(out=o, in_=i))
                    for bb in range(1, NB):
                        W("vector", s_te, t_cmm[bb])
                        t_up = op("vector", lambda eng, o=kvrun[:, hh * 128:(hh + 1) * 128],
                                  sc=bd[h], i1=PS[4 + bb][0:128, 0:128]:
                                  eng.scalar_tensor_tensor(out=o, in0=o, scalar=sc, in1=i1,
                                                           op0=ALU.mult, op1=ALU.add))
                        if bb < NB - 1:
                            op("vector", lambda eng, o=kvloc_ap(slab, hh, bb),
                                i=kvrun[:, hh * 128:(hh + 1) * 128]:
                                eng.tensor_copy(out=o, in_=i))
                    t_ps_free[0] = (s_ve, tkp[7])
                    for bb in range(NB):
                        t_ps_free[4 + bb] = (s_ve, t_up)
                t_states = op("vector", lambda eng: eng.engine_nop())
                W("gpsimd", s_ve, t_states)
                t_gld = dma("gpsimd", s_cci, cci[g].ap(), kvrun[:])
                kvrun_free[0] = (s_cci, t_gld)
                W("gpsimd", s_cci, t_gld)
                on("gpsimd", lambda eng, i=cci[g].ap(), o=cco[g].ap():
                    eng.collective_compute("AllGather", ALU.bypass,
                                           replica_groups=[[0, 1], [2, 3], [4, 5], [6, 7]],
                                           ins=[i], outs=[o]).then_inc(s_cc.h, 1))
                s_cc.inc()

            if g >= 1:
                gp = g - 1
                pslab = gp % 2
                while stageq:
                    stageq.pop(0)()
                W("gpsimd", s_cc, gp + 1)
                if kv0_free[0] is not None:
                    W("gpsimd", *kv0_free[0])
                    kv0_free[0] = None
                t_r = dma("gpsimd", s_kv0r, kv0[:, 0:512], cco[gp].ap()[0])
                W("vector", s_kv0r, t_r)
                t_kv0 = op("vector", lambda eng, o=kv0[:, 0:512],
                           b=msk[:].broadcast_to([128, 512]):
                           eng.tensor_tensor(out=o, in0=o, in1=b, op=ALU.mult))
                pendq = load_pair(wqk[GH * gp, 0])

                def qc_ap(qp):
                    o = QC0 if qp == 0 else QC1
                    return ar2[:, o:o + 1024]

                def qs_ap(qp):
                    o = QS0 if qp == 0 else QS1
                    return ar2[:, o:o + 1024]

                def build_stages(h, hh, hp, qp, t_qev, t_qs, pslab=None):
                    ctx = {}
                    t_sc_l, t_sct_l, t_kvb_l = {}, {}, {}

                    def emit_S(b):
                        t_kvb_l[b] = op("vector", lambda eng,
                                        o=ar2[:, KV2 + (b % 2) * 128: KV2 + (b % 2) * 128 + 128],
                                        i0=kv0[:, hh * 128: hh * 128 + 128],
                                        sc=float(bd_t[h] ** b),
                                        i1=(ar2[:, ZKV:ZKV + 128] if b == 0 else kvloc_ap(pslab, hh, b - 1)):
                                        eng.scalar_tensor_tensor(out=o, in0=i0, scalar=sc, in1=i1,
                                                                 op0=ALU.mult, op1=ALU.add))
                        scb = 4 if b % 2 == 0 else 6
                        te_wait_ps(scb)
                        if b == 0:
                            W("tensor", s_se, t_qev)
                            W("tensor", s_se, kt_ready[(pslab, hh)])
                            W("tensor", s_se, vs_ready[pslab])
                            W("tensor", s_ve, t_qs)
                        for halfm in range(2):
                            t_sc_l[b] = te(lambda eng, o=PS[scb][0:128, halfm * 256:halfm * 256 + 256],
                                           l=kt_ap(pslab, hh)[:, b * 256 + halfm * 128: b * 256 + halfm * 128 + 128],
                                           r=qc_ap(qp)[:, b * 256:(b + 1) * 256]:
                                           eng.matmul(o, l, r, start=True, stop=True), inc=halfm == 1)
                        if b == 0:
                            W("vector", s_dd[hp], ctx["t_dd"])
                        W("vector", s_te, t_sc_l[b])
                        t_sct_l[b] = op("vector", lambda eng,
                                        o=ar2[:, SC0 + (b % 2) * 512: SC0 + (b % 2) * 512 + 512],
                                        a=PS[scb][0:128, 0:512],
                                        b2=ar2[:, DD0 + hp * 512: DD0 + (hp + 1) * 512]:
                                        eng.tensor_tensor(out=o, in0=a, in1=b2, op=ALU.mult))
                        t_ps_free[scb] = (s_ve, t_sct_l[b])

                    def emit_O(b):
                        ob = 5 if b % 2 == 0 else 7
                        bp = b % 2
                        te_wait_ps(ob)
                        W("tensor", s_ve, t_sct_l[b])
                        W("tensor", s_ve, t_kvb_l[b])
                        for halfm in range(2):
                            te(lambda eng, o=PS[ob][0:128, 0:256],
                                l=vs_ap(pslab)[:, (2 * b + halfm) * 512 + hh * 128:(2 * b + halfm) * 512 + hh * 128 + 128],
                                r=ar2[:, SC0 + bp * 512 + halfm * 256: SC0 + bp * 512 + halfm * 256 + 256],
                                st=halfm == 0:
                                eng.matmul(o, l, r, start=st, stop=False))
                        t_o = te(lambda eng, o=PS[ob][0:128, 0:256],
                                 l=ar2[:, KV2 + (b % 2) * 128: KV2 + (b % 2) * 128 + 128],
                                 r=qs_ap(qp)[:, b * 256:(b + 1) * 256]:
                                 eng.matmul(o, l, r, start=False, stop=True), inc=True)
                        if b == NB - 1:
                            qs_guard[qp] = t_o
                        W("vector", s_te, t_o)
                        t_ev = op("vector", lambda eng, o=aost_ap(ctx["ai"])[:, b * 256:(b + 1) * 256],
                                  i=PS[ob][0:128, 0:256]: eng.tensor_copy(out=o, in_=i))
                        t_ps_free[ob] = (s_ve, t_ev)
                        ctx["t_ev"] = t_ev

                    def st_first():
                        ai, cond = aost_ring.acquire()
                        ctx["ai"] = ai
                        if cond is not None:
                            Wl("vector", cond)
                        emit_S(0)

                    def st_rms():
                        ai = ctx["ai"]
                        if t_rw_prev[0] is not None:
                            W("vector", s_te, t_rw_prev[0])
                            t_rw_prev[0] = None
                        t_sq = op("vector", lambda eng, o=sqs[:], a=aost_ap(ai), b=aost_ap(ai):
                                  eng.tensor_tensor(out=o, in0=a, in1=b, op=ALU.mult))
                        te_wait_ps(6)
                        te_wait_ps(7)
                        W("tensor", s_ve, t_sq)
                        for bank, p in ((0, 6), (1, 7)):
                            t_rw = te(lambda eng, o=PS[p][0:1, 0:512],
                                      r=sqs[:, bank * 512:(bank + 1) * 512]:
                                      eng.matmul(o, onc[:], r, start=True, stop=True), inc=bank == 1)
                        t_rw_prev[0] = t_rw
                        W("vector", s_te, t_rw)
                        for bank, p in ((0, 6), (1, 7)):
                            t_acc = op("vector", lambda eng,
                                       o=rows[0:1, _ROWSLOT["RC"] * 1024 + bank * 512: _ROWSLOT["RC"] * 1024 + (bank + 1) * 512],
                                       i=PS[p][0:1, 0:512]:
                                       eng.tensor_tensor(out=o, in0=o, in1=i, op=ALU.add))
                        t_ps_free[6] = (s_ve, t_acc)
                        t_ps_free[7] = (s_ve, t_acc)
                        qd_free[hp] = (s_ve, t_qs)
                        dd_free[hp] = (s_ve, t_acc)
                        kv0_free[0] = (s_ve, t_acc)
                        W("scalar", s_ve, ctx["t_ev"])
                        t_park = dma("scalar", s_ao[ai], ao_park[h], aost_ap(ai))
                        aost_ring.release(ai, s_ao[ai], t_park)

                    return ctx, [st_first,
                                 lambda: emit_S(1),
                                 lambda: (emit_O(0), emit_S(2)),
                                 lambda: (emit_O(1), emit_S(3)),
                                 lambda: emit_O(2),
                                 lambda: emit_O(3),
                                 st_rms]

                for hh in range(GH + 1):
                    if hh < GH:
                        h = GH * gp + hh
                        qslots, qtcs = pendq
                        if hh + 1 < GH:
                            pendq = load_pair(wqk[h + 1, 0])
                        hp = h % 2
                        qp = hh % 2
                        if qd_free[hp] is not None:
                            W("gpsimd", *qd_free[hp])
                            qd_free[hp] = None
                        t_qd = dma("gpsimd", s_qd[hp], ar2[:, QD0 + hp * 256: QD0 + (hp + 1) * 256], qdb[h])
                        if dd_free[hp] is not None:
                            W("gpsimd", *dd_free[hp])
                            dd_free[hp] = None
                        dma("gpsimd", s_dd[hp],
                            ar2[:, DD0 + hp * 512: DD0 + hp * 512 + 256], ddtb[h][0])
                        t_dd = dma("gpsimd", s_dd[hp],
                                   ar2[:, DD0 + hp * 512 + 256: DD0 + hp * 512 + 512], ddtb[h][1])
                        for tc_ in qtcs:
                            W("tensor", *tc_)
                        pp = 0 if hh % 2 == 0 else 2
                        te_wait_ps(pp)
                        te_wait_ps(pp + 1)
                        t_qmm = None
                        for fc in range(NFC):
                            sl = qslots[fc // 16]
                            lt = wb_ap(sl)[:, (fc % 16) * 128:(fc % 16 + 1) * 128]
                            st, sp = fc == 0, fc == NFC - 1
                            for bank in (0, 1):
                                t_qmm = te(lambda eng, o=PS[pp + bank][0:128, 0:512], l=lt,
                                           r=h1_ap(fc)[:, bank * 512:(bank + 1) * 512], st=st, sp=sp:
                                           eng.matmul(o, l, r, start=st, stop=sp),
                                           inc=(sp and bank == 1))
                            if fc % 4 == 3 and stageq:
                                stageq.pop(0)()
                        W("scalar", s_te, t_qmm)
                        for bank in (0, 1):
                            t_qev = op("scalar", lambda eng,
                                       o=qc_ap(qp)[:, bank * 512:(bank + 1) * 512],
                                       i=PS[pp + bank][0:128, 0:512]:
                                       eng.activation(out=o, in_=i, func=AF.Silu))
                        t_ps_free[pp] = (s_se, t_qev)
                        t_ps_free[pp + 1] = (s_se, t_qev)
                        for wi in qslots:
                            wb_ring.release(wi, s_te, t_qmm)
                        W("vector", s_se, t_qev)
                        W("vector", s_qd[hp], t_qd)
                        if qs_guard[qp] is not None:
                            W("vector", s_te, qs_guard[qp])
                            qs_guard[qp] = None
                        for bb in range(NB):
                            t_qs = op("vector", lambda eng,
                                      o=qs_ap(qp)[:, bb * 256:(bb + 1) * 256],
                                      a=qc_ap(qp)[:, bb * 256:(bb + 1) * 256],
                                      b2=ar2[:, QD0 + hp * 256: QD0 + (hp + 1) * 256]:
                                      eng.tensor_tensor(out=o, in0=a, in1=b2, op=ALU.mult))
                        ctx, stages = build_stages(h, hh, hp, qp, t_qev, t_qs, pslab=pslab)
                        ctx["t_dd"] = t_dd
                        stageq.extend(stages)
                        if hh == 2 and g + 1 < NG:
                            nh = GH * (g + 1)
                            xs = wqk[nh, 1].rearrange("p (a c) -> a p c", a=2)
                            cond_x = [extra_free[0]] if extra_free[0] is not None else None
                            extra_free[0] = None
                            tcs_x = []
                            for halfw in range(2):
                                tcs_x.append(load_cast_ap(xs[halfw], XW_APS[halfw], wb_cond=cond_x))
                            pend_extra[0] = (None, tcs_x)
                    else:
                        pass
        while stageq:
            stageq.pop(0)()
        barrier()
        if tap_kt is not None:
            dma("sync", s_out, tap_kt, ar2[:, KT0:KT0 + 8192])
            on("sync", lambda eng, t=s_out.n: eng.wait_ge(s_out.h, t))

    if phase_limit >= 3:
        (t_rs, _) = finalize_ln_rows(None, row("rsms"), rms_only=True, ssq_row_src=row("ssq_attn"))
        t_rsb = split_bcast(sbc, row("rsms"), t_src_se=t_rs)
        t_ps_free3 = {0: (s_ve, t_rsb), 1: (s_ve, t_rsb), 2: None, 3: None}
        t_gated_prev = [None]

        def load_pair3(src2):
            slots_, tcs_ = [], []
            for halfw in range(2):
                wi, cond = wb_ring.acquire()
                if cond is not None:
                    Wl("sync", cond)
                tcs_.append(load_cast(src2.rearrange("p (a c) -> a p c", a=2)[halfw], wi,
                                      wb_cond=cond))
                slots_.append(wi)
            return slots_, tcs_

        pend3 = load_pair3(wg[0])
        for oc in range(NFC):
            slots, tcs = pend3
            if oc + 1 < NFC:
                pend3 = load_pair3(wg[oc + 1])
            ai, cond = aost_ring.acquire()
            if cond is not None:
                Wl("sync", cond)
            t_ao = dma("sync", s_ao[ai], aost_ap(ai), ao_park[oc])
            pp = 0 if oc % 2 == 0 else 2
            for tc_ in tcs:
                W("tensor", *tc_)
            for i in (pp, pp + 1):
                if t_ps_free3[i] is not None:
                    W("tensor", *t_ps_free3[i])
                    t_ps_free3[i] = None
            t_mm = None
            for fc in range(NFC):
                sl = slots[fc // 16]
                lt = wb_ap(sl)[:, (fc % 16) * 128:(fc % 16 + 1) * 128]
                st, sp = fc == 0, fc == NFC - 1
                for bank in (0, 1):
                    t_mm = te(lambda eng, o=PS[pp + bank][0:128, 0:512], l=lt,
                              r=h1_ap(fc)[:, bank * 512:(bank + 1) * 512], st=st, sp=sp:
                              eng.matmul(o, l, r, start=st, stop=sp), inc=(sp and bank == 1))
            W("scalar", s_te, t_mm)
            if t_gated_prev[0] is not None:
                W("scalar", s_ve, t_gated_prev[0])
                t_gated_prev[0] = None
            for bank in (0, 1):
                t_sig = op("scalar", lambda eng, o=sqs[:, bank * 512:(bank + 1) * 512],
                           i=PS[pp + bank][0:128, 0:512]:
                           eng.activation(out=o, in_=i, func=AF.Sigmoid))
            for wi in slots:
                wb_ring.release(wi, s_te, t_mm)
            t_ps_free3[pp] = (s_se, t_sig)
            t_ps_free3[pp + 1] = (s_se, t_sig)
            W("vector", s_ao[ai], t_ao)
            if oc == 0:
                W("vector", s_ve, t_rsb)
            t_nrm = op("vector", lambda eng, o=mbc[:], i0=aost_ap(ai), sc=nw[:, oc:oc + 1], i1=sbc[:]:
                       eng.scalar_tensor_tensor(out=o, in0=i0, scalar=sc, in1=i1,
                                                op0=ALU.mult, op1=ALU.mult))
            aost_ring.release(ai, s_ve, t_nrm)
            W("vector", s_se, t_sig)
            t_gated_prev[0] = op("vector", lambda eng, o=gated_ap(oc), a=mbc[:], b=sqs[:]:
                eng.tensor_tensor(out=o, in0=a, in1=b, op=ALU.mult))
        barrier()
        if tap_gated is not None:
            dma("sync", s_out, tap_gated, ar2[:])
            on("sync", lambda eng, t=s_out.n: eng.wait_ge(s_out.h, t))

    if phase_limit >= 4:
        t_ps_free4 = {i: None for i in range(4)}
        t_stats_mm = [None]
        stats_pend = [None]
        flush4_queue = []

        def load_pair4(src2):
            slots_, tcs_ = [], []
            for halfw in range(2):
                wi, cond = wb_ring.acquire()
                if cond is not None:
                    Wl("sync", cond)
                tcs_.append(load_cast(src2.rearrange("p (a c) -> a p c", a=2)[halfw], wi,
                                      wb_cond=cond))
                slots_.append(wi)
            return slots_, tcs_

        pend4 = load_pair4(wo[0])
        for oc in range(NFC):
            slots, tcs = pend4
            if oc + 1 < NFC:
                pend4 = load_pair4(wo[oc + 1])
            pp = 0 if oc % 2 == 0 else 2
            for tc_ in tcs:
                W("tensor", *tc_)
            for i in (pp, pp + 1):
                if t_ps_free4[i] is not None:
                    W("tensor", *t_ps_free4[i])
                    t_ps_free4[i] = None
            t_mm = None
            for fc in range(NFC):
                sl = slots[fc // 16]
                lt = wb_ap(sl)[:, (fc % 16) * 128:(fc % 16 + 1) * 128]
                st, sp = fc == 0, fc == NFC - 1
                for bank in (0, 1):
                    t_mm = te(lambda eng, o=PS[pp + bank][0:128, 0:512], l=lt,
                              r=gated_ap(fc)[:, bank * 512:(bank + 1) * 512], st=st, sp=sp:
                              eng.matmul(o, l, r, start=st, stop=sp), inc=(sp and bank == 1))
            if flush4_queue:
                flush4_queue.pop(0)()
            for wi in slots:
                wb_ring.release(wi, s_te, t_mm)
            ai, cond = aost_ring.acquire()
            if cond is not None:
                Wl("vector", cond)
            W("vector", s_te, t_mm)
            for bank in (0, 1):
                t_ev = op("vector", lambda eng, o=aost_ap(ai)[:, bank * 512:(bank + 1) * 512],
                          i=PS[pp + bank][0:128, 0:512]: eng.tensor_copy(out=o, in_=i))
            t_ps_free4[pp] = (s_ve, t_ev)
            t_ps_free4[pp + 1] = (s_ve, t_ev)
            if t_stats_mm[0] is not None:
                W("vector", s_te, t_stats_mm[0])
            t_c4 = op("vector", lambda eng, o=sqs[:], i=aost_ap(ai): eng.tensor_copy(out=o, in_=i))
            t_q4 = op("vector", lambda eng, o=sqs2[:], a=aost_ap(ai), b=aost_ap(ai):
                      eng.tensor_tensor(out=o, in0=a, in1=b, op=ALU.mult))
            stats_pend[0] = (oc, t_c4, t_q4)
            W("scalar", s_ve, t_ev)
            t_park = dma("scalar", s_ao[ai], ao2_park[oc], aost_ap(ai))
            aost_ring.release(ai, s_ao[ai], t_park)

            def flush_stats4():
                oc_, t_c_, t_q_ = stats_pend[0]
                stats_pend[0] = None
                W("tensor", s_ve, t_q_)
                st_, sp_ = oc_ == 0, oc_ == NFC - 1
                for bank, p in ((0, 4), (1, 5)):
                    te(lambda eng, o=PS[p][0:1, 0:512], r=sqs[:, bank * 512:(bank + 1) * 512],
                        st=st_, sp=sp_: eng.matmul(o, onc[:], r, start=st_, stop=sp_))
                for bank, p in ((0, 6), (1, 7)):
                    t_stats_mm[0] = te(lambda eng, o=PS[p][0:1, 0:512],
                                       r=sqs2[:, bank * 512:(bank + 1) * 512],
                                       st=st_, sp=sp_: eng.matmul(o, onc[:], r, start=st_, stop=sp_),
                                       inc=bank == 1)
            if oc == NFC - 1:
                flush_stats4()
            else:
                flush4_queue.append(flush_stats4)
        barrier()

    if phase_limit >= 5:
        (t_s4, t_psr5) = finalize_ln_rows(row("m4"), row("s4"))
        split_bcast(mbc, row("m4"))
        t_sb4 = split_bcast(sbc, row("s4"), t_src_se=t_s4)
        wi6, cond = wb_ring.acquire()
        if cond is not None:
            Wl("sync", cond)
        t_wr = load_cast(wr, wi6, wb_cond=cond)
        W("scalar", s_ve, t_sb4)
        t_m1r = dma("scalar", s_wld, row("m2"), m1_park.ap())
        (_, t_eT) = load_cast_ap(ebxTd, ebxT_b)
        W("tensor", s_se, t_eT)
        W("tensor", s_ve, t_sb4)
        for gfc in range(NFC):
            t_G = te(lambda eng, l=ebxT_b[:, gfc * 64:(gfc + 1) * 64],
                     st=gfc == 0, sp=gfc == NFC - 1:
                     eng.matmul(PS[0][0:64, 0:64], l, l, start=st, stop=sp),
                     inc=gfc == NFC - 1)
        x0_ring = Ring(2)
        x1_ring = Ring(2)
        t_m2b = None
        pend5 = {}

        def issue5(fc_):
            ai_, cond_ = aost_ring.acquire()
            if cond_ is not None:
                Wl("sync", cond_)
            t_ao_ = dma("sync", s_ao[ai_], aost_ap(ai_), ao2_park[fc_])
            xi_, cond_ = x0_ring.acquire()
            if cond_ is not None:
                Wl("sync", cond_)
            t_x0_ = dma("sync", s_x0[xi_], x0r_ap[xi_], xT[fc_])
            pend5[fc_] = (ai_, t_ao_, xi_, t_x0_)

        issue5(0)
        for fc in range(NFC):
            ai, t_ao, xi, t_x0 = pend5.pop(fc)
            if fc + 1 < NFC:
                issue5(fc + 1)
            si, cond = x1_ring.acquire()
            SPL = 832
            t_both = {}
            for en, c0, c1 in (("vector", 0, SPL), ("gpsimd", SPL, 1024)):
                if cond is not None:
                    Wl(en, cond)
                W(en, s_ao[ai], t_ao)
                W(en, s_x0[xi], t_x0)
                if fc <= 1:
                    W(en, s_ve, t_sb4)
                op(en, lambda eng, o=x1s_ap[si][:, c0:c1], a=aost_ap(ai)[:, c0:c1], b=mbc[:, c0:c1]:
                    eng.tensor_tensor(out=o, in0=a, in1=b, op=ALU.subtract))
                op(en, lambda eng, o=x1s_ap[si][:, c0:c1], a=x1s_ap[si][:, c0:c1], b=sbc[:, c0:c1]:
                    eng.tensor_tensor(out=o, in0=a, in1=b, op=ALU.mult))
                t_both[en] = op(en, lambda eng, o=x1s_ap[si][:, c0:c1], a=x1s_ap[si][:, c0:c1],
                                b=x0r_ap[xi][:, c0:c1]:
                                eng.tensor_tensor(out=o, in0=a, in1=b, op=ALU.add))
            t_x1 = t_both["vector"]
            t_x1g = t_both["gpsimd"]
            aost_ring.release_multi(ai, [(s_ve, t_x1), (s_ge, t_x1g)])
            x0_ring.release_multi(xi, [(s_ve, t_x1), (s_ge, t_x1g)])
            W("scalar", s_ve, t_x1)
            W("scalar", s_ge, t_x1g)
            if t_m2b is not None:
                W("scalar", s_te, t_m2b)
            if t_m2b is not None:
                W("scalar", s_te, t_m2b)
            t_c = op("scalar", lambda eng, o=sqs[:], i=x1s_ap[si]: eng.copy(out=o, in_=i))
            t_qb = op("scalar", lambda eng, o=sqs2[:], i=x1s_ap[si]:
                      eng.activation(out=o, in_=i, func=AF.Square))
            W("tensor", s_se, t_c)
            if fc == 0:
                W("tensor", s_ve, t_psr5)
                W("tensor", s_ve, t_sb4)
                W("tensor", *t_wr)
            st, sp = fc == 0, fc == NFC - 1
            for bank, p in ((0, 2), (1, 3)):
                te(lambda eng, o=PS[p][0:64, 0:512],
                    l=wb_ap(wi6)[:, fc * 64:(fc + 1) * 64],
                    r=sqs[:, bank * 512:(bank + 1) * 512], st=st, sp=sp:
                    eng.matmul(o, l, r, start=st, stop=sp))
            te(lambda eng, l=wb_ap(wi6)[:, fc * 64:(fc + 1) * 64], st=st, sp=sp:
                eng.matmul(PS[1][0:64, 0:1], l, onc[:], start=st, stop=sp))
            for bank, p in ((0, 4), (1, 5)):
                te(lambda eng, o=PS[p][0:64, 0:512],
                    l=ebxT_b[:, fc * 64:(fc + 1) * 64],
                    r=sqs[:, bank * 512:(bank + 1) * 512], st=st, sp=sp:
                    eng.matmul(o, l, r, start=st, stop=sp))
            W("tensor", s_se, t_qb)
            for bank, p in ((0, 6), (1, 7)):
                t_m2b = te(lambda eng, o=PS[p][0:1, 0:512], r=sqs2[:, bank * 512:(bank + 1) * 512],
                           st=st, sp=sp: eng.matmul(o, onc[:], r, start=st, stop=sp), inc=bank == 1)
            W("scalar", s_ve, t_x1)
            W("scalar", s_ge, t_x1g)
            t_park = dma("scalar", s_x1s[si], x1_park[fc], x1s_ap[si])
            x1_ring.release(si, s_x1s[si], t_park)
        wb_ring.release(wi6, s_te, t_m2b)
        barrier()

    if phase_limit >= 6:
        W("vector", s_wld, t_m1r)
        for _ in range(2):
            li, cond = land_ring.acquire()
            if cond is not None:
                Wl("sync", cond)
        t_eb = dma("sync", s_land[0], land[0:64, 0:4096], ebx)
        W("scalar", s_land[0], t_eb)
        t_ebc = op("scalar", lambda eng, o=ebb, i=land[0:64, 0:4096]: eng.copy(out=o, in_=i))
        (t_s2, t_psr6) = finalize_ln_rows(row("m2"), row("s2"), skip_m=True)
        t_s2sq = op("scalar", lambda eng, o=rowx(4), i=row("s2"): eng.square(out=o, in_=i))
        W("vector", s_te, s_te.n)
        op("vector", lambda eng, o=wsumcol[:], i=PS[1][0:64, 0:1]: eng.tensor_copy(out=o, in_=i))
        t_wsc = op("vector", lambda eng, o=wsumcol[:], i=wsumcol[:]:
                   eng.tensor_scalar(out=o, in0=i, scalar1=-1.0, scalar2=None, op0=ALU.mult))
        for bank, p in ((0, 2), (1, 3)):
            op("vector", lambda eng, o=lgsb[:, bank * 512:(bank + 1) * 512], i=PS[p][0:64, 0:512]:
                eng.tensor_copy(out=o, in_=i))
        W("vector", s_se, t_s2)
        t_ms2 = op("vector", lambda eng, o=row("tm2"), a=row("m2"), b=row("s2"):
                   eng.tensor_tensor(out=o, in0=a, in1=b, op=ALU.mult))
        t_s2b = split_bcast(mbc, row("s2"), t_src_se=t_s2, mp=64, psA=2, psB=3)
        t_msb = split_bcast(sbc, row("tm2"), mp=64, psA=2, psB=3)
        W("scalar", s_te, t_G)
        t_gb = op("scalar", lambda eng, o=gb, i=PS[0][0:64, 0:64]: eng.copy(out=o, in_=i))
        op("vector", lambda eng, o=lgsb, a=lgsb, b=mbc[0:64, 0:1024]:
            eng.tensor_tensor(out=o, in0=a, in1=b, op=ALU.mult))
        op("vector", lambda eng, o=lgsb, i0=sbc[0:64, 0:1024], sc=wsumcol[:], i1=lgsb:
            eng.scalar_tensor_tensor(out=o, in0=i0, scalar=sc, in1=i1, op0=ALU.mult, op1=ALU.add))
        t_lg = op("vector", lambda eng, o=lgsb, i=lgsb, sc=rb64[:]:
                  eng.tensor_scalar(out=o, in0=i, scalar1=sc, scalar2=None, op0=ALU.add))
        barrier()
        if tap_h2 is not None:
            dma("sync", s_out, tap_h2, ar1[:])
            on("sync", lambda eng, t=s_out.n: eng.wait_ge(s_out.h, t))

    if phase_limit >= 7:
        evac_tok = {0: None, 1: None}
        for tt in range(8):
            pb = tt % 2
            if evac_tok[pb] is not None:
                W("tensor", s_ve, evac_tok[pb])
            t_tr = te(lambda eng, o=PS[pb][0:128, 0:64], i=lgsb[:, tt * 128:(tt + 1) * 128]:
                      eng.transpose(o, i, id32[0:64, 0:64]), inc=True)
            W("vector", s_te, t_tr)
            t_e = op("vector", lambda eng, o=lgT[:, tt * 64:(tt + 1) * 64], i=PS[pb][0:128, 0:64]:
                     eng.tensor_copy(out=o, in_=i))
            evac_tok[pb] = t_e
        W("tensor", s_ve, t_e)
        lgT3 = lgT.rearrange("p (a b) -> p a b", a=8)
        esc3 = esc.rearrange("p (a b) -> p a b", a=8)
        wrk3 = wrk.rearrange("p (a b) -> p a b", a=8)
        nmax = smx[:, 0:8]
        Zr = smx[:, 8:16]
        acc = smx[:, 16:24]
        mx = smx[:, 24:32]
        op("vector", lambda eng, o=nmax, i=lgT3: eng.reduce_max(out=o, in_=i, axis=AX.X, negate=True))
        t_sub = op("vector", lambda eng, o=esc3, a=lgT3,
                   b=nmax.unsqueeze(2).broadcast_to([128, 8, 64]):
                   eng.tensor_tensor(out=o, in0=a, in1=b, op=ALU.add))
        W("scalar", s_ve, t_sub)
        t_exp = op("scalar", lambda eng, o=esc, i=esc: eng.activation(out=o, in_=i, func=AF.Exp))
        W("vector", s_se, t_exp)
        op("vector", lambda eng, o=Zr, i=esc3: eng.reduce_sum(out=o, in_=i, axis=AX.X))
        op("vector", lambda eng, o=wrk, i=esc: eng.tensor_copy(out=o, in_=i))
        op("vector", lambda eng, o=acc: eng.memset(o, 0.0))
        sbc3 = sbc[:, 0:512].rearrange("p (a b) -> p a b", a=8)
        wdv3 = wdv.rearrange("p (a b) -> p a b", a=8)
        for rnd in range(8):
            op("vector", lambda eng, o=mx, i=wrk3: eng.reduce_max(out=o, in_=i, axis=AX.X))
            op("vector", lambda eng, o=mx, i=mx: eng.tensor_scalar(out=o, in0=i, scalar1=0.0,
                                                                   scalar2=None, op0=ALU.max))
            op("vector", lambda eng, o=acc, a=acc, b=mx: eng.tensor_tensor(out=o, in0=a, in1=b, op=ALU.add))
            if rnd < 7:
                op("vector", lambda eng, o=sbc3, a=wrk3,
                    b=mx.unsqueeze(2).broadcast_to([128, 8, 64]):
                    eng.tensor_tensor(out=o, in0=a, in1=b, op=ALU.is_ge))
                op("vector", lambda eng, o=wrk, i0=sbc[:, 0:512], sc=NEG_BIG, i1=wrk:
                    eng.scalar_tensor_tensor(out=o, in0=i0, scalar=sc, in1=i1, op0=ALU.mult, op1=ALU.add))
        op("vector", lambda eng, o=wdv3, a=esc3,
            b=mx.unsqueeze(2).broadcast_to([128, 8, 64]):
            eng.tensor_tensor(out=o, in0=a, in1=b, op=ALU.is_ge))
        op("vector", lambda eng, o=wdv, a=wdv, b=esc: eng.tensor_tensor(out=o, in0=a, in1=b, op=ALU.mult))
        op("vector", lambda eng, o=Zr, i=Zr: eng.reciprocal(out=o, in_=i))
        op("vector", lambda eng, o=wdv3, a=wdv3,
            b=Zr.unsqueeze(2).broadcast_to([128, 8, 64]):
            eng.tensor_tensor(out=o, in0=a, in1=b, op=ALU.mult))
        t_w = op("vector", lambda eng, o=acc, a=acc, b=Zr:
                 eng.tensor_tensor(out=o, in0=a, in1=b, op=ALU.mult))
        W("tensor", s_ve, t_psr6)
        for gfc in range(NFC):
            t_ebmm = te(lambda eng, l=ebxT_b[:, gfc * 64:(gfc + 1) * 64],
                        st=gfc == 0, sp=gfc == NFC - 1:
                        eng.matmul(PS[6][0:64, 0:1], l, onc[:], start=st, stop=sp),
                        inc=gfc == NFC - 1)
        W("tensor", s_ve, t_w)
        t_tr = te(lambda eng, o=PS[1][0:8, 0:128], i=acc: eng.transpose(o, i, id32[:]), inc=True)
        W("vector", s_te, t_tr)
        t_s8 = op("vector", lambda eng, o=sb8, i=PS[1][0:8, 0:128]: eng.tensor_copy(out=o, in_=i))
        W("sync", s_ve, t_s8)
        t_wr_ = dma("sync", s_wld, row("wsum"), sb8)
        evac_tok = {0: None, 1: t_s8}
        for tt in range(8):
            pb = tt % 2
            if evac_tok[pb] is not None:
                W("tensor", s_ve, evac_tok[pb])
            t_tr = te(lambda eng, o=PS[pb][0:64, 0:128], i=wdv[:, tt * 64:(tt + 1) * 64]:
                      eng.transpose(o, i, id32[:]), inc=True)
            W("vector", s_te, t_tr)
            t_e = op("vector", lambda eng, o=wdT[:, tt * 128:(tt + 1) * 128], i=PS[pb][0:64, 0:128]:
                     eng.tensor_copy(out=o, in_=i))
            evac_tok[pb] = t_e
        W("tensor", s_ve, t_e)
        W("vector", s_wld, t_wr_)
        t_Arow = op("vector", lambda eng, o=row("A"), a=row("s2"), b=row("wsum"):
                    eng.tensor_tensor(out=o, in0=a, in1=b, op=ALU.mult))
        W("scalar", s_ve, t_Arow)
        op("scalar", lambda eng, o=rowx(5), i=row("A"): eng.square(out=o, in_=i))
        op("vector", lambda eng, o=rowx(0), a=row("m2"), b=row("A"):
            eng.tensor_tensor(out=o, in0=a, in1=b, op=ALU.mult))
        t_A = op("vector", lambda eng, o=rowx(0), i=rowx(0):
                 eng.tensor_scalar(out=o, in0=i, scalar1=-1.0, scalar2=None, op0=ALU.mult))
        if "tap_misc" in taps:
            W("sync", s_ve, t_A)
            dma("sync", s_wld, tap_misc[:, 0:512], lgT)
            dma("sync", s_wld, tap_misc[:, 512:1024], esc)
            dma("sync", s_wld, tap_misc[:, 1024:1536], wdv)
            dma("sync", s_wld, tap_misc[:, 1536:1568], smx)
            dma("sync", s_wld, tap_misc[:, 2048:3072], mbc[:])
        barrier()

    if phase_limit >= 8:
        W("vector", s_te, t_ebmm)
        t_ebm = op("vector", lambda eng, o=sqs[0:64, 0:1], i=PS[6][0:64, 0:1]:
                   eng.tensor_copy(out=o, in_=i))
        W("tensor", s_ve, t_ebm)
        for bank, p in ((0, 6), (1, 7)):
            t_emx = te(lambda eng, o=PS[p][0:1, 0:512], l=sqs[0:64, 0:1],
                       r=wdT[:, bank * 512:(bank + 1) * 512]:
                       eng.matmul(o, l, r, start=True, stop=True), inc=bank == 1)
        W("vector", s_te, t_emx)
        for bank, p in ((0, 6), (1, 7)):
            t_m3r = op("vector", lambda eng, o=row("m3")[0:1, bank * 512:(bank + 1) * 512],
                       i=PS[p][0:1, 0:512], cc=1.0 / D:
                       eng.tensor_scalar(out=o, in0=i, scalar1=cc, scalar2=None, op0=ALU.mult))
        W("tensor", s_se, t_gb)
        for bank, p in ((0, 2), (1, 3)):
            t_Q = te(lambda eng, o=PS[p][0:64, 0:512], l=gb,
                     r=wdT[:, bank * 512:(bank + 1) * 512]:
                     eng.matmul(o, l, r, start=True, stop=True), inc=bank == 1)
        op("vector", lambda eng, o=rowx(2), a=row("A"), b=row("A"):
            eng.tensor_tensor(out=o, in0=a, in1=b, op=ALU.add))
        split_bcast(sbc, rowx(2), mp=64)
        W("vector", s_te, t_Q)
        for bank, p4, pq in ((0, 4, 2), (1, 5, 3)):
            op("vector", lambda eng, o=lgsb[:, bank * 512:(bank + 1) * 512],
                a=PS[p4][0:64, 0:512], b=sbc[0:64, bank * 512:(bank + 1) * 512]:
                eng.tensor_tensor(out=o, in0=a, in1=b, op=ALU.mult))
            op("vector", lambda eng, o=lgsb[:, bank * 512:(bank + 1) * 512],
                a=lgsb[:, bank * 512:(bank + 1) * 512], b=PS[pq][0:64, 0:512]:
                eng.tensor_tensor(out=o, in0=a, in1=b, op=ALU.add))
        t_z = op("vector", lambda eng, o=lgsb, a=lgsb, b=wdT:
                 eng.tensor_tensor(out=o, in0=a, in1=b, op=ALU.mult))
        W("scalar", s_ve, t_z)
        t_zb = op("scalar", lambda eng, o=zb, i=lgsb: eng.copy(out=o, in_=i))
        W("tensor", s_se, t_zb)
        for bank, p in ((0, 6), (1, 7)):
            t_ss = te(lambda eng, o=PS[p][0:1, 0:512], r=zb[:, bank * 512:(bank + 1) * 512]:
                      eng.matmul(o, onc[0:64, 0:1], r, start=True, stop=True), inc=bank == 1)
        W("scalar", s_ve, t_m3r)
        t_m3sq = op("scalar", lambda eng, o=rowx(6), i=row("m3"): eng.square(out=o, in_=i))
        op("vector", lambda eng, o=rowx(3), i=rowx(4): eng.reciprocal(out=o, in_=i))
        op("vector", lambda eng, o=rowx(3), i0=rowx(3), sc=-EPS, i1=rowx(5):
            eng.scalar_tensor_tensor(out=o, in0=i0, scalar=sc, in1=i1, op0=ALU.add, op1=ALU.mult))
        W("vector", s_te, t_ss)
        for bank, p in ((0, 6), (1, 7)):
            op("vector", lambda eng, o=rowx(3)[0:1, bank * 512:(bank + 1) * 512],
                i0=PS[p][0:1, 0:512], cc=1.0 / D,
                i1=rowx(3)[0:1, bank * 512:(bank + 1) * 512]:
                eng.scalar_tensor_tensor(out=o, in0=i0, scalar=cc, in1=i1,
                                         op0=ALU.mult, op1=ALU.add))
        op("vector", lambda eng, o=rowx(2), a=rowx(0), b=row("m3"):
            eng.tensor_tensor(out=o, in0=a, in1=b, op=ALU.mult))
        op("vector", lambda eng, o=rowx(3), i0=rowx(2), sc=2.0, i1=rowx(3):
            eng.scalar_tensor_tensor(out=o, in0=i0, scalar=sc, in1=i1, op0=ALU.mult, op1=ALU.add))
        W("vector", s_se, t_m3sq)
        op("vector", lambda eng, o=rowx(3), a=rowx(3), b=rowx(6):
            eng.tensor_tensor(out=o, in0=a, in1=b, op=ALU.subtract))
        op("vector", lambda eng, o=rowx(3), i=rowx(3), e=EPS:
            eng.tensor_scalar(out=o, in0=i, scalar1=e, scalar2=None, op0=ALU.add))
        t_s3 = row_rsqrt(row("s3"), rowx(3))
        op("vector", lambda eng, o=rowx(2), a=rowx(0), b=row("m3"):
            eng.tensor_tensor(out=o, in0=a, in1=b, op=ALU.subtract))
        W("vector", s_se, t_s3)
        op("vector", lambda eng, o=rowx(2), a=rowx(2), b=row("s3"):
            eng.tensor_tensor(out=o, in0=a, in1=b, op=ALU.mult))
        for bank in (0, 1):
            t_rb2 = op("vector", lambda eng, o=rowb2[0:1, bank * 512:(bank + 1) * 512],
                       i=rowx(2)[0:1, bank * 512:(bank + 1) * 512]:
                       eng.tensor_copy(out=o, in_=i))
        split_bcast(sbc, row("s3"), mp=64)
        t_wdm = op("vector", lambda eng, o=wdT, a=wdT, b=sbc[0:64, 0:1024]:
                   eng.tensor_tensor(out=o, in0=a, in1=b, op=ALU.mult))
        op("vector", lambda eng, o=rowx(2), a=row("A"), b=row("s3"):
            eng.tensor_tensor(out=o, in0=a, in1=b, op=ALU.mult))
        op("vector", lambda eng, o=rowx(2), i=rowx(2), e=1.0:
            eng.tensor_scalar(out=o, in0=i, scalar1=e, scalar2=None, op0=ALU.add))
        t_A2 = split_bcast(mbc, rowx(2))
        x1_ring8 = Ring(2)
        t_ps_free8 = {i: None for i in range(4)}
        pend8 = {}

        def issue8(oc_):
            ai_, cond_ = x1_ring8.acquire()
            if cond_ is not None:
                Wl("scalar", cond_)
            pend8[oc_] = (ai_, dma("scalar", s_ao[ai_], aost_ap(ai_), x1_park[oc_]))

        issue8(0)
        for oc in range(NFC):
            ai, t_x1 = pend8.pop(oc)
            if oc + 1 < NFC:
                issue8(oc + 1)
            pp = 0 if oc % 2 == 0 else 2
            if oc == 0:
                W("tensor", s_se, t_ebc)
                W("tensor", s_ve, t_A2)
            for i in (pp, pp + 1):
                if t_ps_free8[i] is not None:
                    W("tensor", *t_ps_free8[i])
                    t_ps_free8[i] = None
            for bank in (0, 1):
                te(lambda eng, o=PS[pp + bank][0:128, 0:512],
                    l=ebb[:, oc * 128:(oc + 1) * 128],
                    r=wdT[:, bank * 512:(bank + 1) * 512]:
                    eng.matmul(o, l, r, start=True, stop=False))
            for bank in (0, 1):
                t_mm = te(lambda eng, o=PS[pp + bank][0:128, 0:512],
                          r=rowb2[0:1, bank * 512:(bank + 1) * 512]:
                          eng.matmul(o, onr[:], r, start=False, stop=True), inc=bank == 1)
            W("vector", s_ao[ai], t_x1)
            W("gpsimd", s_ao[ai], t_x1)
            if oc == 0:
                W("gpsimd", s_ve, t_A2)
            W("vector", s_te, t_mm)
            op("vector", lambda eng, o=ffn_ap(oc)[:, 0:704], a=aost_ap(ai)[:, 0:704],
                b=mbc[:, 0:704]:
                eng.tensor_tensor(out=o, in0=a, in1=b, op=ALU.mult))
            t_mg = op("gpsimd", lambda eng, o=ffn_ap(oc)[:, 704:1024], a=aost_ap(ai)[:, 704:1024],
                      b=mbc[:, 704:1024]:
                      eng.tensor_tensor(out=o, in0=a, in1=b, op=ALU.mult))
            op("vector", lambda eng, o=ffn_ap(oc)[:, 0:512],
                a=ffn_ap(oc)[:, 0:512], b=PS[pp][0:128, 0:512]:
                eng.tensor_tensor(out=o, in0=a, in1=b, op=ALU.add))
            W("vector", s_ge, t_mg)
            t_f = op("vector", lambda eng, o=ffn_ap(oc)[:, 512:1024],
                     a=ffn_ap(oc)[:, 512:1024], b=PS[pp + 1][0:128, 0:512]:
                     eng.tensor_tensor(out=o, in0=a, in1=b, op=ALU.add))
            x1_ring8.release_multi(ai, [(s_ve, t_f), (s_ge, t_mg)])
            t_ps_free8[pp] = (s_ve, t_f)
            t_ps_free8[pp + 1] = (s_ve, t_f)
            W("sync", s_ve, t_f)
            dma("sync", s_out, yT[oc], ffn_ap(oc))
        on("sync", lambda eng: eng.wait_ge(s_out.h, s_out.n))
    else:
        on("sync", lambda eng: eng.wait_ge(s_bar.h, s_bar.n))
        dma("sync", s_out, yT[0], mbc[:])
        on("sync", lambda eng: eng.wait_ge(s_out.h, s_out.n))

    for e, sems in DMA_SEMS.items():
        for s in sems:
            if s.n:
                W(e, s, s.n)

    with stack:
        with nc.Block() as block:
            @block.sync
            def _(eng):
                for f in prog["sync"]:
                    f(eng)

            @block.tensor
            def _(eng):
                for f in prog["tensor"]:
                    f(eng)

            @block.vector
            def _(eng):
                for f in prog["vector"]:
                    f(eng)

            @block.scalar
            def _(eng):
                for f in prog["scalar"]:
                    f(eng)

            @block.gpsimd
            def _(eng):
                for f in prog["gpsimd"]:
                    f(eng)

    return nc



def _prep_shared(qkv_w, out_w, gate_w, norm_w, rw, rb, eb):
    f32 = np.float32
    A1 = np.ascontiguousarray(qkv_w.reshape(32, 3, 128, 32, 128))
    wqk_ = np.ascontiguousarray(np.transpose(A1[:, 0:2], (0, 1, 4, 3, 2))).reshape(32, 2, 128, 4096)
    B = A1[:, 2]
    C2 = B.reshape(8, 4, 128, 8, 4, 128)
    wv_ = np.ascontiguousarray(np.transpose(C2, (0, 3, 5, 4, 1, 2))).reshape(8, 8, 128, 2048)
    G = gate_w.reshape(32, 128, 32, 128)
    wg_ = np.ascontiguousarray(np.transpose(G, (0, 3, 2, 1))).reshape(32, 128, 4096)
    O = out_w.reshape(32, 128, 32, 128)
    wo_ = np.ascontiguousarray(np.transpose(O, (0, 3, 2, 1))).reshape(32, 128, 4096)
    Dm = rw.reshape(64, 32, 128)
    wr_ = np.ascontiguousarray(np.transpose(Dm, (2, 1, 0))).reshape(128, 2048)
    nwc_ = np.ascontiguousarray(norm_w.reshape(32, 128).T)
    rbc_ = np.ascontiguousarray(rb.reshape(64, 1))
    slope, qd, kdv, ddT, bdv = _decay_tables()
    qdb_ = np.ascontiguousarray(np.broadcast_to(qd[:, None, :], (32, 128, 256))).astype(BF16NP)
    ddtb_ = np.ascontiguousarray(ddT.reshape(32, 2, 128, 256)).astype(BF16NP)
    kdc_ = np.ascontiguousarray(kdv.reshape(32, 2, 128).transpose(2, 0, 1)).reshape(128, 64)
    kdc_ = np.ascontiguousarray(kdc_).astype(f32)
    ebxT_ = np.ascontiguousarray(eb.T.reshape(32, 128, 64).transpose(1, 0, 2)).reshape(128, 2048)
    return dict(wqk=wqk_.astype(f32), wv=wv_.astype(f32), wg=wg_.astype(f32), wo=wo_.astype(f32),
                wr=wr_.astype(f32), ebx=np.ascontiguousarray(eb).astype(f32),
                ebxT=ebxT_.astype(f32),
                nwc=nwc_.astype(f32), rbc=rbc_.astype(f32), qdb=qdb_, ddtb=ddtb_, kdc=kdc_)


def _prep_core_inputs(x_full, shared):
    maps = []
    for c in range(8):
        j, p = c // 2, c % 2
        xs = x_full[j, p * 1024:(p + 1) * 1024, :]
        xTt = np.ascontiguousarray(xs.T).reshape(32, 128, 1024).astype(np.float32)
        m = dict(shared)
        m["xT"] = xTt
        m["rmask"] = np.full((128, 1), 1.0 if p == 1 else 0.0, np.float32)
        maps.append(m)
    return maps


def kernel(L0_attn_hidden_states, L0_attn_qkv_w, L0_attn_out_w, L0_attn_gate_w,
           L0_attn_norm_w, L0_ffn_router_weight, L0_ffn_router_bias, L0_ffn_expert_bias):
    x = np.asarray(L0_attn_hidden_states, np.float32)
    shared = _prep_shared(np.asarray(L0_attn_qkv_w, np.float32),
                          np.asarray(L0_attn_out_w, np.float32),
                          np.asarray(L0_attn_gate_w, np.float32),
                          np.asarray(L0_attn_norm_w, np.float32),
                          np.asarray(L0_ffn_router_weight, np.float32),
                          np.asarray(L0_ffn_router_bias, np.float32),
                          np.asarray(L0_ffn_expert_bias, np.float32))
    in_maps = _prep_core_inputs(x, shared)
    nc = build_nc(PHASE_LIMIT, TAPS)
    res = run_bass_kernel_spmd(nc, in_maps, core_ids=list(range(8)))
    out = np.empty((4, 2048, 4096), np.float32)
    for c in range(8):
        j, p = c // 2, c % 2
        yt = res.results[c]["yT"].reshape(4096, 1024)
        out[j, p * 1024:(p + 1) * 1024, :] = yt.T
    return out
